# version 04
# baseline (speedup 1.0000x reference)
import math

import numpy as np
import ml_dtypes

import concourse.bass as bass
import concourse.mybir as mybir
from concourse.bass_utils import run_bass_kernel_spmd

F32 = mybir.dt.float32
BF16 = mybir.dt.bfloat16
AF = mybir.ActivationFunctionType
ALU = mybir.AluOpType
AX = mybir.AxisListType

D = 2048
T = 2048
H = 16
HD = 128
NSEQ = 2
NC = D // 128
NT = T // 512
KT = 10
FH = 5632
NHH = FH // 128
TBLK = 4
MAGIC = 12582912.0
INV_SQRT_HD = 1.0 / math.sqrt(HD)
PCOLS = 1024

DETECT_RACES = False


def _kstart(j):
    return j * 128 if j < 2 else 1024 + (j - 2) * 128


def _host_consts():
    ident = np.eye(128, dtype=ml_dtypes.bfloat16)
    ones_col_bf = np.ones((128, 1), dtype=ml_dtypes.bfloat16)
    ones_row_bf = np.ones((1, 128), dtype=ml_dtypes.bfloat16)
    ones_row_f = np.ones((1, 128), dtype=np.float32)
    ones_col_f = np.ones((128, 1), dtype=np.float32)
    R = np.zeros((128, 128), dtype=np.float32)
    R[np.arange(64), np.arange(64) + 64] = -1.0
    R[np.arange(64) + 64, np.arange(64)] = 1.0
    rot_lhsT = np.ascontiguousarray(R.T).astype(ml_dtypes.bfloat16)
    a = np.arange(0, HD, 2, dtype=np.float32)
    inv_freq = (np.float32(1.0)
                / np.power(np.float32(10000.0), a / np.float32(HD))).astype(np.float32)
    pos = np.arange(T, dtype=np.float32)
    freqs = pos[None, :] * inv_freq[:, None]
    emb = np.concatenate([freqs, freqs], axis=0)
    return dict(ident=ident, ocb=ones_col_bf, orb=ones_row_bf, orf=ones_row_f,
                ocf=ones_col_f, rot=rot_lhsT,
                magic=np.full((128, 1), MAGIC, dtype=np.float32),
                cos=np.cos(emb).astype(np.float32),
                sin=np.sin(emb).astype(np.float32))


class Prog:
    def __init__(self, nc):
        self.nc = nc
        self.ops = []
        self.cnt = {}
        self.sems = {}
        self.banks = {}

    def sem(self, name):
        if name not in self.sems:
            self.sems[name] = self.nc.alloc_semaphore(name)
            self.cnt[name] = 0
        return self.sems[name]

    def c(self, name):
        self.sem(name)
        return self.cnt[name]

    def _add(self, eng, fn, waits, inc, mode):
        ws = []
        for s, v in waits:
            self.sem(s)
            if v > 0:
                ws.append((s, v))
        if inc is not None:
            self.sem(inc[0])
            self.cnt[inc[0]] += inc[1]
        self.ops.append((eng, ws, fn, inc, mode))

    def pe(self, fn, waits=(), sig=1):
        self._add("pe", fn, waits, ("smpe", sig) if sig else None, "instr")
        return self.cnt["smpe"]

    def act(self, fn, waits=(), sig=1):
        self._add("act", fn, waits, ("smact", sig) if sig else None, "instr")
        return self.cnt["smact"]

    def dve(self, fn, waits=(), sig=1, mode="drain"):
        self._add("dve", fn, waits, ("smdve", sig) if sig else None, mode)
        return self.cnt["smdve"]

    def pool(self, fn, waits=(), sig=1):
        self._add("pool", fn, waits, ("smpool", sig) if sig else None, "drain")
        return self.cnt["smpool"]

    def barrier(self, eng, waits):
        self._add(eng, lambda e: e.nop(), waits, None, "none")

    def dma(self, queue, sem_name, out_ap, in_ap, waits=(), transpose=False):
        def fn(eng, out_ap=out_ap, in_ap=in_ap, transpose=transpose):
            return eng.dma_start(out=out_ap, in_=in_ap, transpose=transpose)
        self._add(queue, fn, waits, (sem_name, 16), "instr")
        return self.cnt[sem_name]

    def bank_w(self, *names):
        ws = []
        for n in names:
            ws.extend(self.banks.get(n, []))
        return ws

    def bank_r(self, names, entries):
        for n in names:
            self.banks[n] = list(entries)

    def emit(self):
        waited = {}

        def body(eng_name):
            def run(eng):
                for op_eng, ws, fn, inc, mode in self.ops:
                    if op_eng != eng_name:
                        continue
                    for s, v in ws:
                        key = (eng_name, s)
                        if waited.get(key, 0) >= v:
                            continue
                        waited[key] = v
                        eng.wait_ge(self.sems[s], v)
                    ins = fn(eng)
                    if inc is not None:
                        s, n = inc
                        if mode == "instr":
                            assert ins is not None
                            ins.then_inc(self.sems[s], n)
                        else:
                            eng.drain()
                            eng.nop().then_inc(self.sems[s], n)
            return run
        return body


def build_program(dump=None):
    nc = bass.Bass("TRN2", target_bir_lowering=False, debug=False,
                   detect_race_conditions=DETECT_RACES)
    consts = _host_consts()

    x_in = nc.dram_tensor("x", [NSEQ, T, D], F32, kind="ExternalInput").ap()
    win = {}
    for nm, shp in (("wq", [D, D]), ("wk", [D, D]), ("wv", [D, D]), ("wo", [D, D]),
                    ("w1", [FH, D]), ("w2", [FH, D]), ("w3", [D, FH])):
        win[nm] = nc.dram_tensor(nm, shp, F32, kind="ExternalInput").ap()
    bu = nc.dram_tensor("bu", [1, H], F32, kind="ExternalInput").ap()
    y_out = nc.dram_tensor("y", [NSEQ, T, D], F32, kind="ExternalOutput").ap()
    dbg = None
    if dump is not None:
        dbg = nc.dram_tensor("dbg", [4096, 4096], F32, kind="ExternalOutput").ap()

    x_bf = nc.dram_tensor("x_bf", [NSEQ, T, D], BF16, kind="Internal").ap()
    aw_bf = {n: nc.dram_tensor(f"{n}_bf", [D, D], BF16, kind="Internal").ap()
             for n in ("wq", "wk", "wv", "wo")}
    q_bf = {0: nc.dram_tensor("q1_bf", [FH, D], BF16, kind="Internal").ap(),
            1: nc.dram_tensor("q2_bf", [FH, D], BF16, kind="Internal").ap(),
            2: nc.dram_tensor("q3_bf", [D, FH], BF16, kind="Internal").ap()}
    at_sp = nc.dram_tensor("at_sp", [NC, 128, T], BF16, kind="Internal").ap()
    awT_img = {w: nc.dram_tensor(f"awT_{w}", [H, 128, D], BF16, kind="Internal").ap()
               for w in ("wq", "wk", "wv")}
    woT_img = nc.dram_tensor("woT_img", [8, 128, 4096], BF16, kind="Internal").ap()
    q12T_img = [nc.dram_tensor(f"q{i}T_img", [NHH, 128, D], BF16,
                               kind="Internal").ap() for i in (1, 2)]
    q3T_img = nc.dram_tensor("q3T_img", [NC, 128, NHH * 128], BF16,
                             kind="Internal").ap()
    x2_sp = nc.dram_tensor("x2_sp", [NSEQ, NC, 128, T], BF16, kind="Internal").ap()

    cmap = {nm: nc.inline_tensor(arr, name=f"c_{nm}").ap()
            for nm, arr in consts.items()}

    SBUF_LIMIT = nc.sbuf_top
    cur = [nc.sbuf_base]

    def at(name, shape, dtype, base=None):
        bpe = 2 if dtype == BF16 else 4
        ptr = cur if base is None else base
        o = (ptr[0] + 31) & ~31
        t = nc.alloc_sbuf_tensor_at(name, list(shape), dtype, offset=o)
        ptr[0] = o + int(np.prod(shape[1:])) * bpe
        return t

    def multi(name, n, shape, dtype, base=None):
        return [at(f"{name}{i}", shape, dtype, base=base) for i in range(n)]

    ident_s = at("ident_s", [128, 128], BF16)
    ocb_s = at("ocb_s", [128, 1], BF16)
    orb_s = at("orb_s", [1, 128], BF16)
    orf_s = at("orf_s", [1, 128], F32)
    ocf_s = at("ocf_s", [128, 1], F32)
    rot_s = at("rot_s", [128, 128], BF16)
    bu_s = at("bu_s", [1, H], F32)
    rsc_s = multi("rsc", 3, [128, 1], F32)
    gacc_s = at("gacc_s", [1, 1024], F32)
    ve_s = at("ve_s", [128, 4], F32)
    vebf_s = at("vebf_s", [128, 1], BF16)
    pdbg_s = at("pdbg_s", [1, 48], F32)
    magic_s = at("magic_s", [128, 1], F32)

    attn_base = cur[0]
    cos_s = at("cos_s", [128, T], F32)
    sin_s = at("sin_s", [128, T], F32)
    X_s = multi("X", NC, [128, T], BF16)
    Qh_s = at("Qh_s", [128, T], BF16)
    Kh_s = at("Kh_s", [128, T], BF16)
    Vh_s = at("Vh_s", [128, T], BF16)
    qpre_s = multi("qpre", 2, [128, 512], BF16)
    t1_s = at("t1_s", [128, 512], F32)
    t2_s = at("t2_s", [128, 512], F32)
    eT_s = at("eT_s", [128, KT * 512], BF16)
    vtm_s = at("vtm", [128, KT * 128], BF16)
    shb_s = at("shb_s", [128, 512], BF16)
    wT_s = [multi(f"wT{p}_", 3, [128, D], BF16) for p in range(2)]
    atv_s = multi("atv", 2, [128, 512], BF16)
    atb_s = multi("atb", NC, [128, 512], BF16)
    woT_s = [at(f"woT{p}", [128, 4096], BF16) for p in range(2)]
    sq_s = multi("sq", 2, [128, 512], BF16)
    srow_s = at("srow_s", [1, 512], F32)
    qrow_s = at("qrow_s", [1, 512], F32)
    mb_s = at("mb_s", [128, 512], F32)
    rb_s = at("rb_s", [128, 512], F32)
    attn_end = cur[0]

    pw_s = multi("pw", 2, [128, PCOLS], F32)
    pt_s = at("pt_s", [128, PCOLS], F32)
    pq_s = at("pq_s", [128, PCOLS], BF16)
    pqt_s = multi("pqt", 2, [128, 1024], BF16)
    pabs_s = at("pabs_s", [128, PCOLS], F32)
    assert cur[0] <= SBUF_LIMIT, f"attn+prep over: {cur[0]}"

    fb = [attn_base]
    x2b_s = [multi(f"x2b{s}_", NC, [128, 512], BF16, base=fb) for s in range(NSEQ)]
    h_s = [multi(f"h{s}_", NHH, [128, 512], BF16, base=fb) for s in range(NSEQ)]
    qwT_s = [[at(f"qwT{p}{w}", [128, D], BF16, base=fb) for w in range(2)]
             for p in range(2)]
    q3T_s = [at(f"q3T{p}", [128, NHH * 128], BF16, base=fb) for p in range(2)]
    r2f_s = multi("r2f", 2, [128, 512], BF16, base=fb)
    r2t_s = multi("r2t", NT, [128, D], BF16, base=fb)
    sg_s = multi("sg", 2, [128, 512], BF16, base=fb)
    ost_s = multi("ost", 2, [128, D], F32, base=fb)
    l2v_s = at("l2v", [128, 8], F32, base=fb)
    assert fb[0] <= SBUF_LIMIT, f"ffn over: {fb[0]}"

    psA = nc.alloc_psum_tensor("psA", [128, 1024], F32)
    psB = nc.alloc_psum_tensor("psB", [128, 1024], F32)
    psC = nc.alloc_psum_tensor("psC", [128, 512], F32)
    psE = nc.alloc_psum_tensor("psE", [128, 512], F32)
    psF = nc.alloc_psum_tensor("psF", [128, 512], F32)
    psG = nc.alloc_psum_tensor("psG", [128, 1024], BF16)

    P = Prog(nc)
    c = P.c
    gslot = [[] for _ in range(8)]

    for nm, dst in (("ident", ident_s), ("ocb", ocb_s), ("rot", rot_s),
                    ("cos", cos_s), ("sin", sin_s)):
        P.dma("sp", "smc", dst[:, :], cmap[nm])
    P.dma("sp", "smc", orb_s[0:1, :], cmap["orb"])
    P.dma("sp", "smc", orf_s[0:1, :], cmap["orf"])
    P.dma("sp", "smc", ocf_s[:, :], cmap["ocf"])
    P.dma("sp", "smc", magic_s[:, :], cmap["magic"])
    N_CONST = c("smc")
    P.dma("pool", "smc2", bu_s[0:1, :], bu)

    for s in range(NSEQ):
        P.dma("pool", f"smxc{s}", x_bf[s], x_in[s])
    for n in ("wq", "wk", "wv", "wo"):
        P.dma("pool", "smwc", aw_bf[n], win[n])

    P.dve(lambda eng: eng.memset(eT_s[64:128, 512:1024], 0.0))

    prep_steps = []
    ps = {"pw_free": [[("smc2", 16)], [("smc2", 16)]], "pt_free": 0, "pq_free": 0,
          "pabs_free": 0}

    def gamma_chunk(widx, w_ap, row0, col0, ncols, par, first):
        def step():
            nd = P.dma("pool", f"smqf{par}", pw_s[par][:, 0:ncols],
                       w_ap[row0:row0 + 128, col0:col0 + ncols],
                       waits=ps["pw_free"][par])

            def ab(eng, par=par, ncols=ncols):
                return eng.activation(pabs_s[:, 0:ncols], pw_s[par][:, 0:ncols],
                                      AF.Abs)
            av = P.act(ab, waits=[(f"smqf{par}", nd), ("smpool", ps["pabs_free"])])
            ps["pw_free"][par] = [("smact", av)]

            def fn(eng, ncols=ncols, first=first):
                eng.tensor_reduce(pt_s[0:1, 0:ncols], pabs_s[:, 0:ncols],
                                  axis=AX.C, op=ALU.add)
                eng.drain()
                if first:
                    return eng.tensor_copy(gacc_s[0:1, 0:ncols], pt_s[0:1, 0:ncols])
                return eng.tensor_add(gacc_s[0:1, 0:ncols], gacc_s[0:1, 0:ncols],
                                      pt_s[0:1, 0:ncols])
            dv = P.pool(fn, waits=[("smact", av)] +
                        ([("smdve", c("smdve"))] if first else []))
            ps["pabs_free"] = dv
        return step

    def gamma_fin(widx, count):
        def step():
            dv0 = c("smpool")

            def fin(eng, count=count):
                eng.reduce_sum(ve_s[0:1, 0:1], gacc_s[0:1, :], axis=AX.X)
                eng.drain()
                eng.tensor_scalar(ve_s[0:1, 0:1], ve_s[0:1, 0:1],
                                  1.0 / count, 1e-8, op0=ALU.mult, op1=ALU.add)
                eng.drain()
                return eng.reciprocal(ve_s[0:1, 1:2], ve_s[0:1, 0:1])
            dv = P.dve(fin, waits=[("smpool", dv0)])

            def bc(eng):
                return eng.matmul(psC[:, 0:1], orf_s[0:1, :], ve_s[0:1, 1:2],
                                  start=True, stop=True)
            pv2 = P.pe(bc, waits=[("smdve", dv)] + P.bank_w("C"))

            def ev(eng, widx=widx):
                return eng.tensor_copy(rsc_s[widx][:, 0:1], psC[:, 0:1])
            dv2 = P.dve(ev, waits=[("smpe", pv2)])
            P.bank_r(["C"], [("smdve", dv2)])
        return step

    def quant_chunk(widx, w_ap, q_ap, row0, col0, ncols, par):
        def step():
            nd = P.dma("pool", f"smqf{par}", pw_s[par][:, 0:ncols],
                       w_ap[row0:row0 + 128, col0:col0 + ncols],
                       waits=ps["pw_free"][par])

            def qf(eng, par=par, ncols=ncols, widx=widx):
                eng.tensor_scalar(pt_s[:, 0:ncols], pw_s[par][:, 0:ncols],
                                  rsc_s[widx][:, 0:1], MAGIC,
                                  op0=ALU.mult, op1=ALU.add)
                eng.tensor_scalar(pt_s[:, 0:ncols], pt_s[:, 0:ncols],
                                  MAGIC, 1.0, op0=ALU.subtract, op1=ALU.min)
                return eng.tensor_scalar_max(pq_s[:, 0:ncols], pt_s[:, 0:ncols], -1.0)
            dv = P.pool(qf, waits=[(f"smqf{par}", nd), ("smdve", c("smdve")),
                                   ("smqs", ps["pq_free"])])
            ps["pw_free"][par] = [("smpool", dv)]
            ps["pq_free"] = P.dma("pool", "smqs",
                                  q_ap[row0:row0 + 128, col0:col0 + ncols],
                                  pq_s[:, 0:ncols], waits=[("smpool", dv)])
        return step

    def chunks_of(total, width):
        out = []
        o = 0
        while o < total:
            out.append((o, min(width, total - o)))
            o += width
        return out

    wshapes = {0: (win["w1"], FH, D), 1: (win["w2"], FH, D), 2: (win["w3"], D, FH)}
    for widx in (0, 1, 2):
        w_ap, rows, cols = wshapes[widx]
        first = True
        for i in range(rows // 128):
            for (c0, cn) in chunks_of(cols, PCOLS):
                prep_steps.append(gamma_chunk(widx, w_ap, i * 128, c0, cn,
                                              len(prep_steps) % 2, first))
                first = False
        prep_steps.append(gamma_fin(widx, rows * cols))
    for widx in (0, 1, 2):
        w_ap, rows, cols = wshapes[widx]
        for i in range(rows // 128):
            for (c0, cn) in chunks_of(cols, PCOLS):
                prep_steps.append(quant_chunk(widx, w_ap, q_bf[widx], i * 128,
                                              c0, cn, len(prep_steps) % 2))
        def mk_thop(widx=widx, rows=rows, cols=cols):
            def step():
                gate = [("smqs", c("smqs"))]
                if widx < 2:
                    for cc in range(NC):
                        for r0, nr in chunks_of(rows, 1024):
                            dst = q12T_img[widx][r0 // 128:(r0 + nr) // 128, 0:128,
                                                 cc * 128:(cc + 1) * 128].rearrange(
                                                     "h p f -> p h f")
                            thop_pair(q_bf[widx][r0:r0 + nr,
                                                 cc * 128:(cc + 1) * 128],
                                      dst, gate)
                else:
                    for hhc in range(NHH):
                        for r0, nr in chunks_of(D, 1024):
                            dst = q3T_img[r0 // 128:(r0 + nr) // 128, 0:128,
                                          hhc * 128:(hhc + 1) * 128].rearrange(
                                              "d p f -> p d f")
                            thop_pair(q_bf[2][r0:r0 + nr,
                                              hhc * 128:(hhc + 1) * 128],
                                      dst, gate)
            return step
        prep_steps.append(mk_thop())
    prep_iter = iter(prep_steps)
    PREP_PER_SLOT = max(1, (len(prep_steps) + NSEQ * H - 1) // (NSEQ * H))

    th_state = {"par": 0}

    def thop_pair(src_ap, img_dst, gate_waits):
        par = th_state["par"]
        th_state["par"] ^= 1
        nr = src_ap.shape[0]
        P.dma("sp", f"smth{par}", pqt_s[par][:, 0:nr], src_ap,
              waits=gate_waits + [(f"smtw{par}", c(f"smtw{par}"))], transpose=True)
        P.dma("pool", f"smtw{par}", img_dst, pqt_s[par][:, 0:nr],
              waits=[(f"smth{par}", c(f"smth{par}"))])

    def sched_attn_thop():
        gates = []
        for h0 in (0, 8):
            for w in ("wq", "wk", "wv"):
                for cc in range(NC):
                    dst = awT_img[w][h0:h0 + 8, 0:128,
                                     cc * 128:(cc + 1) * 128].rearrange(
                                         "h p f -> p h f")
                    thop_pair(aw_bf[w][h0 * 128:(h0 + 8) * 128,
                                       cc * 128:(cc + 1) * 128],
                              dst, [("smwc", 64)])
            gates.append([("smtw0", c("smtw0")), ("smtw1", c("smtw1"))])
        for g0 in (0, 4):
            for cc in range(NC):
                dst = woT_img[g0:g0 + 4, 0:128,
                              cc * 256:(cc + 1) * 256].rearrange("g p f -> p g f")
                thop_pair(aw_bf["wo"][g0 * 256:g0 * 256 + 1024,
                                      cc * 128:(cc + 1) * 128],
                          dst, [("smwc", 64)])
        gates.append([("smtw0", c("smtw0")), ("smtw1", c("smtw1"))])
        return gates

    st = {"wt_free": [[], []], "wt_have": [0, 0], "x_have": 0,
          "qpre_free": [0, 0], "t1_free": 0, "t2_free": 0,
          "Qh_free": 0, "Kh_free": 0, "Vh_free": 0, "eT_free": 0,
          "qk_ready": 0, "v_ready": 0,
          "shb_free": 0, "vtm_free": 0, "atv_free": [0, 0],
          "sq_free": [0, 0], "woT_free": [[], []], "atb_free": []}

    def load_X(s, extra=()):
        v = 0
        for cc in range(NC):
            v = P.dma("sp", "smxt", X_s[cc][:, :],
                      x_bf[s][0:T, cc * 128:(cc + 1) * 128],
                      waits=[(f"smxc{s}", 16), ("smc", N_CONST)] + list(extra),
                      transpose=True)
        return v

    def load_wT(h, par):
        gate = st["awT_gate"][0 if h < 8 else 1]
        for wi, name in enumerate(("wq", "wk", "wv")):
            P.dma("sp", f"smwt{par}", wT_s[par][wi][:, :], awT_img[name][h],
                  waits=gate + st["wt_free"][par])
        return c(f"smwt{par}")

    def proj_head(s, h, par, only=None):
        nw = st["wt_have"][par]
        nx = st["x_have"]
        for wi, name in enumerate(("q", "k", "v")[:only] if only else ("q", "k", "v")):
            for tc in range(NT):
                bank = ("E", "F")[tc % 2]
                pj = (psE, psF)[tc % 2]
                sl = slice(tc * 512, (tc + 1) * 512)

                def mk_mm(eng, wi=wi, pj=pj, sl=sl, par=par):
                    ins = None
                    for cc in range(NC):
                        ins = eng.matmul(pj[:, 0:512],
                                         wT_s[par][wi][:, cc * 128:(cc + 1) * 128],
                                         X_s[cc][:, sl],
                                         start=(cc == 0), stop=(cc == NC - 1))
                    return ins
                mmv = P.pe(mk_mm, waits=[(f"smwt{par}", nw), ("smxt", nx)]
                           + P.bank_w(bank))
                if name == "v":
                    def ev(eng, pj=pj, sl=sl):
                        return eng.copy(Vh_s[:, sl], pj[:, 0:512])
                    av = P.act(ev, waits=[("smpe", mmv), ("smpe", st["Vh_free"])])
                    P.bank_r([bank], [("smact", av)])
                else:
                    dst, dfree = ((Qh_s, "Qh_free") if name == "q"
                                  else (Kh_s, "Kh_free"))
                    qp = tc % 2

                    def ev(eng, pj=pj, qp=qp):
                        return eng.copy(qpre_s[qp][:, :], pj[:, 0:512])
                    av = P.act(ev, waits=[("smpe", mmv),
                                          ("smpe", st["qpre_free"][qp])])
                    P.bank_r([bank], [("smact", av)])

                    def mt1(eng, qp=qp, sl=sl):
                        return eng.tensor_mul(t1_s[:, :], qpre_s[qp][:, :],
                                              cos_s[:, sl])
                    dv1 = P.dve(mt1, waits=[("smact", av), ("smdve", st["t1_free"])])

                    def mrot(eng, qp=qp):
                        return eng.matmul(psC[:, 0:512], rot_s[:, :],
                                          qpre_s[qp][:, :], start=True, stop=True)
                    mmr = P.pe(mrot, waits=[("smact", av)] + P.bank_w("C"))
                    st["qpre_free"][qp] = mmr

                    def mt2(eng, sl=sl):
                        return eng.tensor_mul(t2_s[:, :], psC[:, 0:512], sin_s[:, sl])
                    dv2 = P.dve(mt2, waits=[("smpe", mmr), ("smdve", st["t2_free"])])
                    P.bank_r(["C"], [("smdve", dv2)])

                    def madd(eng, dst=dst, sl=sl):
                        return eng.tensor_add(dst[:, sl], t1_s[:, :], t2_s[:, :])
                    dv3 = P.dve(madd, waits=[("smpe", st[dfree])])
                    st["t1_free"] = dv3
                    st["t2_free"] = dv3
        st["wt_free"][par] = [("smpe", c("smpe"))]

    def sT_exp(qg):
        qsl = slice(qg * 512, (qg + 1) * 512)
        for p in range(KT // 2):
            pp = (psA, psB)[p % 2]
            bks = ("A0", "A1") if p % 2 == 0 else ("B0", "B1")

            def mm(eng, p=p, pp=pp, qsl=qsl):
                ins = None
                for jj in range(2):
                    j = 2 * p + jj
                    ins = eng.matmul(pp[:, jj * 512:(jj + 1) * 512],
                                     Kh_s[:, _kstart(j):_kstart(j) + 128],
                                     Qh_s[:, qsl], start=True, stop=True)
                return ins
            mmv = P.pe(mm, waits=[("smdve", st["qk_ready"])] + P.bank_w(*bks))
            if p == 0:
                def ex(eng):
                    eng.activation(eT_s[:, 0:512], psA[:, 0:512], AF.Exp,
                                   bias=0.0, scale=INV_SQRT_HD)
                    return eng.activation(eT_s[0:76, 512:1024], psA[0:76, 512:1024],
                                          AF.Exp, bias=0.0, scale=INV_SQRT_HD)
            else:
                def ex(eng, p=p, pp=pp):
                    return eng.activation(eT_s[:, p * 1024:(p + 1) * 1024], pp[:, :],
                                          AF.Exp, bias=0.0, scale=INV_SQRT_HD)
            av = P.act(ex, waits=[("smpe", mmv), ("smpe", st["eT_free"])])
            P.bank_r(bks, [("smact", av)])
        return c("smact")

    def S_sum(exp_done):
        def mm(eng):
            ins = None
            for j in range(KT):
                ins = eng.matmul(psC[0:1, 0:512], ocb_s[:, 0:1],
                                 eT_s[:, j * 512:(j + 1) * 512],
                                 start=(j == 0), stop=(j == KT - 1))
            return ins
        return P.pe(mm, waits=[("smact", exp_done)] + P.bank_w("C"))

    def shb_make(sv):
        def rc(eng):
            with nc.allow_low_precision(reason="1/S in bf16; probs are bf16 anyway"):
                return eng.reciprocal(shb_s[0:1, 0:512], psC[0:1, 0:512])
        dv = P.dve(rc, waits=[("smpe", sv), ("smdve", st["shb_free"])])
        P.bank_r(["C"], [("smdve", dv)])

        def bc(eng):
            return eng.matmul(psC[:, 0:512], orb_s[0:1, :], shb_s[0:1, 0:512],
                              start=True, stop=True)
        pv = P.pe(bc, waits=[("smdve", dv)] + P.bank_w("C"))

        def ev(eng):
            return eng.tensor_copy(shb_s[:, :], psC[:, 0:512])
        dv2 = P.dve(ev, waits=[("smpe", pv)])
        P.bank_r(["C"], [("smdve", dv2)])
        return dv2

    def pv_evac(s, h, qg, exp_done, shb_done, vtm_done):
        bank = ("E", "F")[qg % 2]
        pv_ = (psE, psF)[qg % 2]

        def mm(eng, pv_=pv_):
            ins = None
            for j in range(KT):
                ins = eng.matmul(pv_[:, 0:512], vtm_s[:, j * 128:(j + 1) * 128],
                                 eT_s[:, j * 512:(j + 1) * 512],
                                 start=(j == 0), stop=(j == KT - 1))
            return ins
        mv = P.pe(mm, waits=[("smact", exp_done), ("smdve", shb_done),
                             ("smact", vtm_done)] + P.bank_w(bank))
        ap = qg % 2

        def ev(eng, pv_=pv_, ap=ap):
            return eng.tensor_mul(atv_s[ap][:, :], pv_[:, 0:512], shb_s[:, :])
        dv = P.dve(ev, waits=[("smpe", mv), (f"smat{ap}", st["atv_free"][ap])])
        P.bank_r([bank], [("smdve", dv)])
        st["eT_free"] = mv
        st["shb_free"] = dv
        st["atv_free"][ap] = P.dma("act", f"smat{ap}",
                                   at_sp[h][:, qg * 512:(qg + 1) * 512],
                                   atv_s[ap][:, :], waits=[("smdve", dv)])
        return dv

    def prepass_vupdate(h, exp_done, sv):
        def asum(eng):
            ins = None
            for j in range(2, KT):
                ins = eng.matmul(psE[0:1, 0:1],
                                 eT_s[:, j * 512 + 511:j * 512 + 512],
                                 ocb_s[:, 0:1], start=(j == 2), stop=(j == KT - 1))
            return ins
        pa = P.pe(asum, waits=[("smact", exp_done)] + P.bank_w("E"))

        def pmath(eng, h=h):
            eng.tensor_copy(ve_s[0:1, 0:1], psE[0:1, 0:1])
            eng.tensor_copy(ve_s[0:1, 1:2], eT_s[0:1, 2 * 512 + 511:2 * 512 + 512])
            eng.tensor_scalar_mul(ve_s[0:1, 2:3], psC[0:1, 511:512], 1e-6)
            eng.drain()
            eng.tensor_sub(ve_s[0:1, 0:1], ve_s[0:1, 0:1], ve_s[0:1, 1:2])
            eng.drain()
            eng.tensor_scalar_mul(ve_s[0:1, 0:1], ve_s[0:1, 0:1], 1.0 / 1023.0)
            eng.drain()
            eng.tensor_add(ve_s[0:1, 0:1], ve_s[0:1, 0:1], ve_s[0:1, 2:3])
            eng.drain()
            eng.reciprocal(ve_s[0:1, 0:1], ve_s[0:1, 0:1])
            eng.drain()
            eng.tensor_mul(ve_s[0:1, 0:1], ve_s[0:1, 1:2], ve_s[0:1, 0:1])
            eng.drain()
            eng.tensor_copy(pdbg_s[0:1, h:h + 1], ve_s[0:1, 0:1])
            eng.tensor_tensor(ve_s[0:1, 1:2], bu_s[0:1, h:h + 1], ve_s[0:1, 0:1],
                              op=ALU.is_lt)
            eng.drain()
            eng.tensor_copy(pdbg_s[0:1, 16 + h:17 + h], ve_s[0:1, 1:2])
            return eng.tensor_scalar_mul(ve_s[0:1, 1:2], ve_s[0:1, 1:2], 1.0 / 1024.0)
        dv = P.dve(pmath, waits=[("smpe", pa), ("smpe", sv), ("smc2", 16)])
        P.bank_r(["E"], [("smdve", dv)])
        P.bank_r(["C"], [("smdve", dv)])

        def bcf(eng):
            return eng.matmul(psE[:, 0:1], orf_s[0:1, :], ve_s[0:1, 1:2],
                              start=True, stop=True)
        pb = P.pe(bcf, waits=[("smdve", dv)] + P.bank_w("E"))

        def vupd(eng):
            eng.tensor_copy(ve_s[:, 0:1], psE[:, 0:1])
            eng.tensor_copy(ve_s[:, 1:2], Vh_s[:, 1024:1025])
            eng.drain()
            eng.tensor_mul(ve_s[:, 0:1], ve_s[:, 0:1], ve_s[:, 1:2])
            eng.drain()
            return eng.tensor_scalar_add(Vh_s[:, 1025:2048], Vh_s[:, 1025:2048],
                                         ve_s[:, 0:1])
        dv2 = P.dve(vupd, waits=[("smpe", pb), ("smact", st["v_ready"])])
        P.bank_r(["E"], [("smdve", dv2)])
        return dv2

    def v_transposes(vupd_done):
        for g0, g1 in ((0, 8), (8, KT)):
            tv = 0
            for j in range(g0, g1):
                slot = j - g0

                def tr(eng, slot=slot, j=j):
                    return eng.transpose(psG[:, slot * 128:(slot + 1) * 128],
                                         Vh_s[:, _kstart(j):_kstart(j) + 128],
                                         ident_s[:, :])
                w = [("smdve", vupd_done)] + (P.bank_w("G") if j == g0 else [])
                tv = P.pe(tr, waits=w)

            def ev(eng, g0=g0, g1=g1):
                return eng.copy(vtm_s[:, g0 * 128:g1 * 128],
                                psG[:, 0:(g1 - g0) * 128])
            av = P.act(ev, waits=[("smpe", tv), ("smpe", st["vtm_free"])])
            P.bank_r(["G"], [("smact", av)])
        return c("smact")

    def attn_head(s, h):
        par = h % 2
        proj_head(s, h, par)
        st["qk_ready"] = c("smdve")
        st["v_ready"] = c("smact")
        nh, npar = h + 1, (h + 1) % 2
        if nh < H:
            st["wt_have"][npar] = load_wT(nh, npar)
        elif s + 1 < NSEQ:
            st["wt_have"][0] = load_wT(0, 0)
        ed3 = sT_exp(3)
        sv3 = S_sum(ed3)
        vu = prepass_vupdate(h, ed3, sv3)
        vt = v_transposes(vu)
        st["Vh_free"] = c("smpe")
        sh3 = shb_make(sv3)
        pv_evac(s, h, 3, ed3, sh3, vt)
        for qg in (0, 1, 2):
            ed = sT_exp(qg)
            sv = S_sum(ed)
            sh = shb_make(sv)
            pv_evac(s, h, qg, ed, sh, vt)
        st["Qh_free"] = c("smpe")
        st["Kh_free"] = c("smpe")
        st["vtm_free"] = c("smpe")

    def oproj_ln1(s):
        nat0, nat1 = c("smat0"), c("smat1")
        for tc in range(NT):
            sl = slice(tc * 512, (tc + 1) * 512)
            for cc in range(NC):
                P.dma("sp", "smatb", atb_s[cc][:, :], at_sp[cc][:, sl],
                      waits=[("smat0", nat0), ("smat1", nat1)] + st["atb_free"])
            natb = c("smatb")
            for g in range(8):
                gpar = g % 2
                P.dma("sp", f"smwo{gpar}", woT_s[gpar][:, :], woT_img[g],
                      waits=st["awT_gate"][2] + st["woT_free"][gpar])
                nwo = c(f"smwo{gpar}")
                for mi in range(2):
                    m = g * 2 + mi
                    bank = ("E", "F")[m % 2]
                    pj = (psE, psF)[m % 2]

                    def mm(eng, mi=mi, pj=pj, gpar=gpar):
                        ins = None
                        for cc in range(NC):
                            ins = eng.matmul(
                                pj[:, 0:512],
                                woT_s[gpar][:, cc * 256 + mi * 128:
                                            cc * 256 + (mi + 1) * 128],
                                atb_s[cc][:, :],
                                start=(cc == 0), stop=(cc == NC - 1))
                        return ins
                    mv = P.pe(mm, waits=[(f"smwo{gpar}", nwo), ("smatb", natb)]
                              + P.bank_w(bank))

                    def ev(eng, m=m, pj=pj, sl=sl):
                        return eng.tensor_add(X_s[m][:, sl], pj[:, 0:512],
                                              X_s[m][:, sl])
                    dv = P.dve(ev, waits=[("smpe", mv)], mode="instr")
                    P.bank_r([bank], [("smdve", dv)])
                st["woT_free"][gpar] = [("smpe", c("smpe"))]
            rX = P.dve(lambda e: e.engine_nop())
            def mms(eng, sl=sl):
                ins = None
                for m in range(NC):
                    ins = eng.matmul(psC[0:1, 0:512], ocb_s[:, 0:1], X_s[m][:, sl],
                                     start=(m == 0), stop=(m == NC - 1))
                return ins
            sv = P.pe(mms, waits=[("smdve", rX)] + P.bank_w("C"))

            def evs(eng):
                return eng.tensor_copy(srow_s[0:1, :], psC[0:1, 0:512])
            dvs = P.dve(evs, waits=[("smpe", sv)])
            P.bank_r(["C"], [("smdve", dvs)])
            for m in range(NC):
                def sq(eng, m=m, sl=sl):
                    return eng.activation(sq_s[m % 2][:, :], X_s[m][:, sl], AF.Square)
                av = P.act(sq, waits=[("smdve", rX), ("smpe", st["sq_free"][m % 2])])

                def mmq(eng, m=m):
                    return eng.matmul(psE[0:1, 0:512], ocb_s[:, 0:1],
                                      sq_s[m % 2][:, :], start=(m == 0),
                                      stop=(m == NC - 1))
                st["sq_free"][m % 2] = P.pe(
                    mmq, waits=[("smact", av)] + (P.bank_w("E") if m == 0 else []))
            qv = c("smpe")

            def evq(eng):
                return eng.tensor_copy(qrow_s[0:1, :], psE[0:1, 0:512])
            dvq = P.dve(evq, waits=[("smpe", qv)])
            P.bank_r(["E"], [("smdve", dvq)])

            def rows(eng):
                eng.tensor_scalar_mul(srow_s[0:1, :], srow_s[0:1, :], 1.0 / D)
                eng.tensor_scalar(qrow_s[0:1, :], qrow_s[0:1, :], 1.0 / D, 1e-5,
                                  op0=ALU.mult, op1=ALU.add)
                eng.drain()
                eng.tensor_mul(t1_s[0:1, 0:512], srow_s[0:1, :], srow_s[0:1, :])
                eng.drain()
                return eng.tensor_sub(qrow_s[0:1, :], qrow_s[0:1, :],
                                      t1_s[0:1, 0:512])
            rv = P.dve(rows, waits=[("smdve", st["t1_free"])])
            st["t1_free"] = rv

            def rsq(eng):
                return eng.activation(qrow_s[0:1, :], qrow_s[0:1, :], AF.Sqrt)
            av2 = P.act(rsq, waits=[("smdve", rv)])

            def rrec(eng):
                return eng.reciprocal(qrow_s[0:1, :], qrow_s[0:1, :])
            rrv = P.dve(rrec, waits=[("smact", av2)])

            def bcm(eng):
                return eng.matmul(psC[:, 0:512], orf_s[0:1, :], srow_s[0:1, :],
                                  start=True, stop=True)
            p1 = P.pe(bcm, waits=[("smdve", rrv)] + P.bank_w("C"))

            def bcr(eng):
                return eng.matmul(psE[:, 0:512], orf_s[0:1, :], qrow_s[0:1, :],
                                  start=True, stop=True)
            p2 = P.pe(bcr, waits=P.bank_w("E"))

            def evb(eng):
                eng.tensor_copy(mb_s[:, :], psC[:, 0:512])
                return eng.tensor_copy(rb_s[:, :], psE[:, 0:512])
            dvb = P.dve(evb, waits=[("smpe", p2)])
            P.bank_r(["C"], [("smdve", dvb)])
            P.bank_r(["E"], [("smdve", dvb)])
            def ap_(eng, sl=sl):
                ins = None
                for m in range(NC):
                    eng.tensor_sub(X_s[m][:, sl], X_s[m][:, sl], mb_s[:, :])
                    ins = eng.tensor_mul(X_s[m][:, sl], X_s[m][:, sl], rb_s[:, :])
                return ins
            P.dve(ap_, waits=[("smdve", dvb)])
            st["atb_free"] = [("smpe", c("smpe")), ("smdve", c("smdve"))]
        xv = c("smdve")
        for m in range(NC):
            P.dma("act", "smsp", x2_sp[s][m], X_s[m][:, :], waits=[("smdve", xv)])
        return c("smsp")

    fs = {"qw_free": [[], []], "qw_have": [0, 0], "q3_free": [[], []],
          "q3_have": [0, 0], "ost_free": [0, 0], "x2_free": [],
          "h_free": 0, "r2f_free": [0, 0], "r2t_free": 0, "sg_free": [0, 0]}

    def ffn_tb(tb, nq, nsp, ntw):
        for s in range(NSEQ):
            for m in range(NC):
                P.dma("pool", "smx2", x2b_s[s][m][:, :],
                      x2_sp[s][m][:, tb * 512:(tb + 1) * 512],
                      waits=[("smsp", nsp)] + fs["x2_free"])
        nx2 = c("smx2")

        def load_qw12(hhc, par):
            for wi in range(2):
                P.dma("sp", f"smqt{par}", qwT_s[par][wi][:, :], q12T_img[wi][hhc],
                      waits=ntw + fs["qw_free"][par])
            fs["qw_have"][par] = c(f"smqt{par}")
        load_qw12(0, 0)
        for hhc in range(NHH):
            par = hhc % 2
            have = fs["qw_have"][par]
            if hhc + 1 < NHH:
                load_qw12(hhc + 1, (hhc + 1) % 2)
            for s in range(NSEQ):
                pA2 = (psA, psB)[s]
                bks = ("A0", "A1") if s == 0 else ("B0", "B1")

                def mm1(eng, s=s, par=par, pA2=pA2):
                    ins = None
                    for cc in range(NC):
                        ins = eng.matmul(pA2[:, 0:512],
                                         qwT_s[par][0][:, cc * 128:(cc + 1) * 128],
                                         x2b_s[s][cc][:, :],
                                         start=(cc == 0), stop=(cc == NC - 1))
                    return ins
                P.pe(mm1, waits=[(f"smqt{par}", have), ("smx2", nx2)]
                     + P.bank_w(bks[0]))

                def mm2(eng, s=s, par=par, pA2=pA2):
                    ins = None
                    for cc in range(NC):
                        ins = eng.matmul(pA2[:, 512:1024],
                                         qwT_s[par][1][:, cc * 128:(cc + 1) * 128],
                                         x2b_s[s][cc][:, :],
                                         start=(cc == 0), stop=(cc == NC - 1))
                    return ins
                mv = P.pe(mm2, waits=P.bank_w(bks[1]))

                def sgm(eng, s=s, pA2=pA2):
                    return eng.activation(sg_s[s][:, :], pA2[:, 0:512], AF.Sigmoid)
                av = P.act(sgm, waits=[("smpe", mv), ("smdve", fs["sg_free"][s])])

                def hm(eng, s=s, hhc=hhc, pA2=pA2):
                    return eng.tensor_mul(h_s[s][hhc][:, :], sg_s[s][:, :],
                                          pA2[:, 512:1024])
                dv = P.dve(hm, waits=[("smact", av), ("smpe", fs["h_free"])])
                fs["sg_free"][s] = dv
                P.bank_r([bks[0]], [("smact", av)])
                P.bank_r([bks[1]], [("smdve", dv)])
            fs["qw_free"][par] = [("smpe", c("smpe"))]
        hv = c("smdve")

        def load_qw3(dc, par):
            P.dma("sp", f"smq3{par}", q3T_s[par][:, :], q3T_img[dc],
                  waits=ntw + fs["q3_free"][par])
            fs["q3_have"][par] = c(f"smq3{par}")
        for s in range(NSEQ):
            load_qw3(0, 0)
            for dc in range(NC):
                par = dc % 2
                have = fs["q3_have"][par]
                if dc + 1 < NC:
                    load_qw3(dc + 1, (dc + 1) % 2)
                bank = ("E", "F")[dc % 2]
                pv_ = (psE, psF)[dc % 2]

                def mm(eng, s=s, par=par, pv_=pv_):
                    ins = None
                    for hhc in range(NHH):
                        ins = eng.matmul(pv_[:, 0:512],
                                         q3T_s[par][:, hhc * 128:(hhc + 1) * 128],
                                         h_s[s][hhc][:, :],
                                         start=(hhc == 0), stop=(hhc == NHH - 1))
                    return ins
                mv = P.pe(mm, waits=[(f"smq3{par}", have), ("smdve", hv)]
                          + P.bank_w(bank))

                def ev(eng, s=s, dc=dc, pv_=pv_):
                    return eng.tensor_add(r2f_s[dc % 2][:, :], pv_[:, 0:512],
                                          x2b_s[s][dc][:, :])
                dv = P.dve(ev, waits=[("smpe", mv), ("smpe", fs["r2f_free"][dc % 2])])
                P.bank_r([bank], [("smdve", dv)])
                tv = 0
                for tc in range(4):
                    def tr(eng, dc=dc, tc=tc):
                        return eng.transpose(psG[:, tc * 128:(tc + 1) * 128],
                                             r2f_s[dc % 2][:, tc * 128:(tc + 1) * 128],
                                             ident_s[:, :])
                    w = [("smdve", dv)] + (P.bank_w("G") if tc == 0 else [])
                    tv = P.pe(tr, waits=w)
                av = 0
                for tc in range(4):
                    def tev(eng, dc=dc, tc=tc):
                        return eng.copy(r2t_s[tc][:, dc * 128:(dc + 1) * 128],
                                        psG[:, tc * 128:(tc + 1) * 128])
                    av = P.act(tev, waits=[("smpe", tv), ("smdve", fs["r2t_free"])])
                P.bank_r(["G"], [("smact", av)])
                fs["r2f_free"][dc % 2] = c("smpe")
                fs["q3_free"][par] = [("smpe", c("smpe"))]
            rT = c("smact")
            for tc in range(4):
                def st1(eng, tc=tc):
                    return eng.reduce_sum(l2v_s[:, 0:1], r2t_s[tc][:, :], axis=AX.X)
                P.dve(st1, waits=[("smact", rT)])

                def st2(eng, tc=tc):
                    ins = None
                    for piece in range(4):
                        sl2 = slice(piece * 512, (piece + 1) * 512)
                        ins = eng.activation(sg_s[0][:, :], r2t_s[tc][:, sl2],
                                             AF.Square,
                                             accum_out=l2v_s[:, 2 + piece:3 + piece])
                    return ins
                a2 = P.act(st2, waits=[("smact", rT), ("smdve", fs["sg_free"][0])])

                def vecs(eng):
                    eng.tensor_add(l2v_s[:, 1:2], l2v_s[:, 2:3], l2v_s[:, 3:4])
                    eng.tensor_add(l2v_s[:, 6:7], l2v_s[:, 4:5], l2v_s[:, 5:6])
                    eng.tensor_scalar_mul(l2v_s[:, 0:1], l2v_s[:, 0:1], 1.0 / D)
                    eng.drain()
                    eng.tensor_add(l2v_s[:, 1:2], l2v_s[:, 1:2], l2v_s[:, 6:7])
                    eng.tensor_mul(l2v_s[:, 2:3], l2v_s[:, 0:1], l2v_s[:, 0:1])
                    eng.drain()
                    eng.tensor_scalar_mul(l2v_s[:, 1:2], l2v_s[:, 1:2], 1.0 / D)
                    eng.drain()
                    eng.tensor_sub(l2v_s[:, 1:2], l2v_s[:, 1:2], l2v_s[:, 2:3])
                    eng.drain()
                    return eng.tensor_scalar_add(l2v_s[:, 1:2], l2v_s[:, 1:2], 1e-5)
                vv = P.dve(vecs, waits=[("smact", a2)])

                def sqr(eng):
                    return eng.activation(l2v_s[:, 1:2], l2v_s[:, 1:2], AF.Sqrt)
                a3 = P.act(sqr, waits=[("smdve", vv)])

                def rstd_bias(eng):
                    eng.reciprocal(l2v_s[:, 1:2], l2v_s[:, 1:2])
                    eng.drain()
                    eng.tensor_mul(l2v_s[:, 2:3], l2v_s[:, 0:1], l2v_s[:, 1:2])
                    eng.drain()
                    return eng.tensor_scalar_mul(l2v_s[:, 2:3], l2v_s[:, 2:3], -1.0)
                rb2 = P.dve(rstd_bias, waits=[("smact", a3)])

                def appl(eng, tc=tc):
                    return eng.activation(ost_s[tc % 2][:, :], r2t_s[tc][:, :],
                                          AF.Identity, bias=l2v_s[:, 2:3],
                                          scale=l2v_s[:, 1:2])
                ov = P.act(appl, waits=[("smdve", rb2),
                                        (f"smout{tc % 2}", fs["ost_free"][tc % 2])])
                fs["ost_free"][tc % 2] = P.dma(
                    "act", f"smout{tc % 2}",
                    y_out[s][tb * 512 + tc * 128: tb * 512 + (tc + 1) * 128, :],
                    ost_s[tc % 2][:, :], waits=[("smact", ov)])
            fs["r2t_free"] = c("smdve")
        fs["h_free"] = c("smpe")
        fs["x2_free"] = [("smpe", c("smpe")), ("smdve", c("smdve"))]

    def dump_rows(entries, waits):
        for ap_, r0, wdt in entries:
            P.dma("pool", "smdbg", dbg[r0:r0 + 128, 0:wdt], ap_, waits=waits)
        P.barrier("sp", [("smdbg", c("smdbg"))])

    st["awT_gate"] = sched_attn_thop()
    st["x_have"] = load_X(0)
    st["wt_have"][0] = load_wT(0, 0)

    stopped = False
    if dump in ("pq0", "pq1", "pq2"):
        level = {"pq0": 0, "pq1": 1, "pq2": 2}[dump]
        ntc = 2 if dump == "pq2" else NT
        nw = st["wt_have"][0]
        nx = st["x_have"]
        for tc in range(ntc):
            bank = ("E", "F")[tc % 2]
            pj = (psE, psF)[tc % 2]
            sl = slice(tc * 512, (tc + 1) * 512)
            def mk_mm(eng, pj=pj, sl=sl):
                ins = None
                for cc in range(NC):
                    ins = eng.matmul(pj[:, 0:512], wT_s[0][0][:, cc * 128:(cc + 1) * 128],
                                     X_s[cc][:, sl], start=(cc == 0), stop=(cc == NC - 1))
                return ins
            mmv = P.pe(mk_mm, waits=[("smwt0", nw), ("smxt", nx)] + P.bank_w(bank))
            qp = tc % 2
            def ev(eng, pj=pj, qp=qp):
                return eng.copy(qpre_s[qp][:, :], pj[:, 0:512])
            av = P.act(ev, waits=[("smpe", mmv), ("smpe", st["qpre_free"][qp])])
            P.bank_r([bank], [("smact", av)])
            if level >= 1:
                def mt1(eng, qp=qp, sl=sl):
                    return eng.tensor_mul(t1_s[:, :], qpre_s[qp][:, :], cos_s[:, sl])
                dv1 = P.dve(mt1, waits=[("smact", av), ("smdve", st["t1_free"])])
            if level >= 2:
                def mrot(eng, qp=qp):
                    return eng.matmul(psC[:, 0:512], rot_s[:, :], qpre_s[qp][:, :],
                                      start=True, stop=True)
                mmr = P.pe(mrot, waits=[("smact", av)] + P.bank_w("C"))
                st["qpre_free"][qp] = mmr
                def mt2(eng, sl=sl):
                    return eng.tensor_mul(t2_s[:, :], psC[:, 0:512], sin_s[:, sl])
                dv2 = P.dve(mt2, waits=[("smpe", mmr), ("smdve", st["t2_free"])])
                P.bank_r(["C"], [("smdve", dv2)])
                def madd(eng, sl=sl):
                    return eng.tensor_add(Qh_s[:, sl], t1_s[:, :], t2_s[:, :])
                dv3 = P.dve(madd, waits=[("smpe", st["Qh_free"])])
                st["t1_free"] = dv3
                st["t2_free"] = dv3
        dump_rows([(Qh_s[:, :], 0, T)],
                  [("smdve", c("smdve")), ("smact", c("smact")), ("smpe", c("smpe"))])
        stopped = True
    if dump in ("projq", "projqk"):
        proj_head(0, 0, 0, only=1 if dump == "projq" else 2)
        dump_rows([(Qh_s[:, :], 0, T), (Kh_s[:, :], 128, T)],
                  [("smdve", c("smdve")), ("smact", c("smact")), ("smpe", c("smpe"))])
        stopped = True
    if dump in ("p0", "p1", "p2", "p3"):
        level = int(dump[1])
        nw = st["wt_have"][0]
        nx = st["x_have"]
        def mk_mm(eng):
            ins = None
            for cc in range(NC):
                ins = eng.matmul(psE[:, 0:512], wT_s[0][0][:, cc * 128:(cc + 1) * 128],
                                 X_s[cc][:, 0:512], start=(cc == 0), stop=(cc == NC - 1))
            return ins
        mmv = P.pe(mk_mm, waits=[("smwt0", nw), ("smxt", nx)])
        def ev(eng):
            return eng.copy(qpre_s[0][:, :], psE[:, 0:512])
        P.act(ev, waits=[("smpe", mmv)])
        if level >= 1:
            def mt1(eng):
                return eng.tensor_mul(t1_s[:, :], psE[:, 0:512], cos_s[:, 0:512])
            P.dve(mt1, waits=[("smpe", mmv)])
        if level >= 2:
            def mrot(eng):
                return eng.matmul(psC[:, 0:512], rot_s[:, :], qpre_s[0][:, :],
                                  start=True, stop=True)
            mmr = P.pe(mrot, waits=[("smact", c("smact"))])
            if level >= 3:
                def mt2(eng):
                    return eng.tensor_mul(t2_s[:, :], psC[:, 0:512], sin_s[:, 0:512])
                P.dve(mt2, waits=[("smpe", mmr)])
                def madd(eng):
                    return eng.tensor_add(Qh_s[:, 0:512], t1_s[:, :], t2_s[:, :])
                P.dve(madd)
        dump_rows([(qpre_s[0][:, :], 0, 512), (Qh_s[:, 0:512], 128, 512)],
                  [("smdve", c("smdve")), ("smact", c("smact")), ("smpe", c("smpe"))])
        stopped = True
    if dump == "proj":
        proj_head(0, 0, 0)
        dump_rows([(Qh_s[:, :], 0, T), (Kh_s[:, :], 128, T),
                   (Vh_s[:, :], 256, T)],
                  [("smdve", c("smdve")), ("smact", c("smact")),
                   ("smpe", c("smpe"))])
        stopped = True
    if dump == "X":
        dump_rows([(X_s[m][:, :], m * 128, T) for m in range(NC)],
                  [("smxt", st["x_have"])])
        stopped = True

    done_sp = 0
    if not stopped:
        for s in range(NSEQ):
            for h in range(H):
                attn_head(s, h)
                for _ in range(PREP_PER_SLOT):
                    stp = next(prep_iter, None)
                    if stp is not None:
                        stp()
                if dump == "qkv" and s == 0 and h == 0:
                    dump_rows([(Qh_s[:, :], 0, T), (Kh_s[:, :], 128, T),
                               (Vh_s[:, :], 256, T)],
                              [("smdve", c("smdve")), ("smact", c("smact")),
                               ("smpe", c("smpe"))])
                    stopped = True
                    break
            if stopped:
                break
            if dump == "attn" and s == 0:
                dump_rows([], [])
                for m in range(NC):
                    P.dma("pool", "smdbg", dbg[m * 128:(m + 1) * 128, 0:T],
                          at_sp[m], waits=[("smat0", c("smat0")),
                                           ("smat1", c("smat1"))])
                P.dma("pool", "smdbg", dbg[2048:2049, 0:48], pdbg_s[0:1, :],
                      waits=[("smdve", c("smdve"))])
                P.barrier("sp", [("smdbg", c("smdbg"))])
                stopped = True
                break
            done_sp = oproj_ln1(s)
            if dump == "x2" and s == 0:
                dump_rows([(X_s[m][:, :], m * 128, T) for m in range(NC)],
                          [("smdve", c("smdve")), ("smsp", done_sp)])
                stopped = True
                break
            if s + 1 < NSEQ:
                st["x_have"] = load_X(s + 1, extra=[("smsp", done_sp)])

    if not stopped:
        for stp in prep_iter:
            stp()
        nq = c("smqs")
        ntw = [("smtw0", c("smtw0")), ("smtw1", c("smtw1"))]
        for tb in range(TBLK):
            ffn_tb(tb, nq, done_sp, ntw)
            if dump == "h" and tb == 0:
                dump_rows([(h_s[0][m][:, :], m * 128, 512) for m in range(32)],
                          [("smdve", c("smdve"))])
                stopped = True
                break
        if not stopped:
            P.barrier("sp", [("smout0", c("smout0")), ("smout1", c("smout1"))])

    body = P.emit()
    with nc.Block() as blk:
        blk.tensor(body("pe"))
        blk.scalar(body("act"))
        blk.vector(body("dve"))
        blk.gpsimd(body("pool"))
        blk.sync(body("sp"))
    return nc


def kernel(L0_attn_hidden_states, L0_attn_wq, L0_attn_wk, L0_attn_wv,
           L0_attn_wo, L0_attn_bern_u, L0_ffn_w1, L0_ffn_w2, L0_ffn_w3):
    x = np.ascontiguousarray(L0_attn_hidden_states, dtype=np.float32)
    nc = build_program()
    common = {
        "wq": np.ascontiguousarray(L0_attn_wq, dtype=np.float32),
        "wk": np.ascontiguousarray(L0_attn_wk, dtype=np.float32),
        "wv": np.ascontiguousarray(L0_attn_wv, dtype=np.float32),
        "wo": np.ascontiguousarray(L0_attn_wo, dtype=np.float32),
        "w1": np.ascontiguousarray(L0_ffn_w1, dtype=np.float32),
        "w2": np.ascontiguousarray(L0_ffn_w2, dtype=np.float32),
        "w3": np.ascontiguousarray(L0_ffn_w3, dtype=np.float32),
        "bu": np.ascontiguousarray(L0_attn_bern_u, dtype=np.float32),
    }
    in_maps = [dict(common, x=x[i * 2:(i + 1) * 2]) for i in range(8)]
    res = run_bass_kernel_spmd(nc, in_maps, core_ids=list(range(8)))
    return np.concatenate([res.results[i]["y"] for i in range(8)], axis=0)


# version 05
# speedup vs baseline: 2.7654x; 2.7654x over previous
import math

import numpy as np
import ml_dtypes

import concourse.bass as bass
import concourse.mybir as mybir
from concourse.bass_utils import run_bass_kernel_spmd

F32 = mybir.dt.float32
BF16 = mybir.dt.bfloat16
AF = mybir.ActivationFunctionType
ALU = mybir.AluOpType
AX = mybir.AxisListType

D = 2048
T = 2048
H = 16
HD = 128
NSEQ = 2
NC = D // 128
NT = T // 512
KT = 10
FH = 5632
NHH = FH // 128
TBLK = 4
MAGIC = 12582912.0
INV_SQRT_HD = 1.0 / math.sqrt(HD)
PCOLS = 1024

DETECT_RACES = False


def _kstart(j):
    return j * 128 if j < 2 else 1024 + (j - 2) * 128


def _host_consts():
    ident = np.eye(128, dtype=ml_dtypes.bfloat16)
    ones_col_bf = np.ones((128, 1), dtype=ml_dtypes.bfloat16)
    ones_row_bf = np.ones((1, 128), dtype=ml_dtypes.bfloat16)
    ones_row_f = np.ones((1, 128), dtype=np.float32)
    ones_col_f = np.ones((128, 1), dtype=np.float32)
    R = np.zeros((128, 128), dtype=np.float32)
    R[np.arange(64), np.arange(64) + 64] = -1.0
    R[np.arange(64) + 64, np.arange(64)] = 1.0
    rot_lhsT = np.ascontiguousarray(R.T).astype(ml_dtypes.bfloat16)
    a = np.arange(0, HD, 2, dtype=np.float32)
    inv_freq = (np.float32(1.0)
                / np.power(np.float32(10000.0), a / np.float32(HD))).astype(np.float32)
    pos = np.arange(T, dtype=np.float32)
    freqs = pos[None, :] * inv_freq[:, None]
    emb = np.concatenate([freqs, freqs], axis=0)
    return dict(ident=ident, ocb=ones_col_bf, orb=ones_row_bf, orf=ones_row_f,
                ocf=ones_col_f, rot=rot_lhsT,
                magic=np.full((128, 1), MAGIC, dtype=np.float32),
                cos=np.cos(emb).astype(np.float32),
                sin=np.sin(emb).astype(np.float32))


class Prog:
    def __init__(self, nc):
        self.nc = nc
        self.ops = []
        self.cnt = {}
        self.sems = {}
        self.banks = {}

    def sem(self, name):
        if name not in self.sems:
            self.sems[name] = self.nc.alloc_semaphore(name)
            self.cnt[name] = 0
        return self.sems[name]

    def c(self, name):
        self.sem(name)
        return self.cnt[name]

    def _add(self, eng, fn, waits, inc, mode):
        ws = []
        for s, v in waits:
            self.sem(s)
            if v > 0:
                ws.append((s, v))
        if inc is not None:
            self.sem(inc[0])
            self.cnt[inc[0]] += inc[1]
        self.ops.append((eng, ws, fn, inc, mode))

    def pe(self, fn, waits=(), sig=1):
        self._add("pe", fn, waits, ("smpe", sig) if sig else None, "instr")
        return self.cnt["smpe"]

    def act(self, fn, waits=(), sig=1):
        self._add("act", fn, waits, ("smact", sig) if sig else None, "instr")
        return self.cnt["smact"]

    def dve(self, fn, waits=(), sig=1, mode="drain"):
        self._add("dve", fn, waits, ("smdve", sig) if sig else None, mode)
        return self.cnt["smdve"]

    def pool(self, fn, waits=(), sig=1):
        self._add("pool", fn, waits, ("smpool", sig) if sig else None, "drain")
        return self.cnt["smpool"]

    def barrier(self, eng, waits):
        self._add(eng, lambda e: e.nop(), waits, None, "none")

    def dma(self, queue, sem_name, out_ap, in_ap, waits=(), transpose=False):
        def fn(eng, out_ap=out_ap, in_ap=in_ap, transpose=transpose):
            return eng.dma_start(out=out_ap, in_=in_ap, transpose=transpose)
        self._add(queue, fn, waits, (sem_name, 16), "instr")
        return self.cnt[sem_name]

    def bank_w(self, *names):
        ws = []
        for n in names:
            ws.extend(self.banks.get(n, []))
        return ws

    def bank_r(self, names, entries):
        for n in names:
            self.banks[n] = list(entries)

    def emit(self):
        waited = {}

        def body(eng_name):
            def run(eng):
                for op_eng, ws, fn, inc, mode in self.ops:
                    if op_eng != eng_name:
                        continue
                    for s, v in ws:
                        key = (eng_name, s)
                        if waited.get(key, 0) >= v:
                            continue
                        waited[key] = v
                        eng.wait_ge(self.sems[s], v)
                    ins = fn(eng)
                    if inc is not None:
                        s, n = inc
                        if mode == "instr":
                            assert ins is not None
                            ins.then_inc(self.sems[s], n)
                        else:
                            eng.drain()
                            eng.nop().then_inc(self.sems[s], n)
            return run
        return body


def build_program(dump=None):
    nc = bass.Bass("TRN2", target_bir_lowering=False, debug=False,
                   detect_race_conditions=DETECT_RACES)
    consts = _host_consts()

    x_in = nc.dram_tensor("x", [NSEQ, T, D], F32, kind="ExternalInput").ap()
    win = {}
    for nm, shp in (("wq", [D, D]), ("wk", [D, D]), ("wv", [D, D]), ("wo", [D, D]),
                    ("w1", [FH, D]), ("w2", [FH, D]), ("w3", [D, FH])):
        win[nm] = nc.dram_tensor(nm, shp, F32, kind="ExternalInput").ap()
    bu = nc.dram_tensor("bu", [1, H], F32, kind="ExternalInput").ap()
    y_out = nc.dram_tensor("y", [NSEQ, T, D], F32, kind="ExternalOutput").ap()
    dbg = None
    if dump is not None:
        dbg = nc.dram_tensor("dbg", [4096, 4096], F32, kind="ExternalOutput").ap()

    x_bf = nc.dram_tensor("x_bf", [NSEQ, T, D], BF16, kind="Internal").ap()
    aw_bf = {n: nc.dram_tensor(f"{n}_bf", [D, D], BF16, kind="Internal").ap()
             for n in ("wq", "wk", "wv", "wo")}
    q_bf = {0: nc.dram_tensor("q1_bf", [FH, D], BF16, kind="Internal").ap(),
            1: nc.dram_tensor("q2_bf", [FH, D], BF16, kind="Internal").ap(),
            2: nc.dram_tensor("q3_bf", [D, FH], BF16, kind="Internal").ap()}
    at_sp = nc.dram_tensor("at_sp", [NC, 128, T], BF16, kind="Internal").ap()
    awT_img = {w: nc.dram_tensor(f"awT_{w}", [H, 128, D], BF16, kind="Internal").ap()
               for w in ("wq", "wk", "wv")}
    woT_img = nc.dram_tensor("woT_img", [8, 128, 4096], BF16, kind="Internal").ap()
    q12T_img = [nc.dram_tensor(f"q{i}T_img", [NHH, 128, D], BF16,
                               kind="Internal").ap() for i in (1, 2)]
    q3T_img = nc.dram_tensor("q3T_img", [NC, 128, NHH * 128], BF16,
                             kind="Internal").ap()
    x2_sp = nc.dram_tensor("x2_sp", [NSEQ, NC, 128, T], BF16, kind="Internal").ap()

    cmap = {nm: nc.inline_tensor(arr, name=f"c_{nm}").ap()
            for nm, arr in consts.items()}

    SBUF_LIMIT = nc.sbuf_top
    cur = [nc.sbuf_base]

    def at(name, shape, dtype, base=None):
        bpe = 2 if dtype == BF16 else 4
        ptr = cur if base is None else base
        o = (ptr[0] + 31) & ~31
        t = nc.alloc_sbuf_tensor_at(name, list(shape), dtype, offset=o)
        ptr[0] = o + int(np.prod(shape[1:])) * bpe
        return t

    def multi(name, n, shape, dtype, base=None):
        return [at(f"{name}{i}", shape, dtype, base=base) for i in range(n)]

    ident_s = at("ident_s", [128, 128], BF16)
    ocb_s = at("ocb_s", [128, 1], BF16)
    orb_s = at("orb_s", [1, 128], BF16)
    orf_s = at("orf_s", [1, 128], F32)
    ocf_s = at("ocf_s", [128, 1], F32)
    rot_s = at("rot_s", [128, 128], BF16)
    bu_s = at("bu_s", [1, H], F32)
    rsc_s = multi("rsc", 3, [128, 1], F32)
    gacc_s = at("gacc_s", [1, 1024], F32)
    ve_s = at("ve_s", [128, 4], F32)
    vebf_s = at("vebf_s", [128, 1], BF16)
    pdbg_s = at("pdbg_s", [1, 48], F32)
    magic_s = at("magic_s", [128, 1], F32)

    attn_base = cur[0]
    cos_s = at("cos_s", [128, T], F32)
    sin_s = at("sin_s", [128, T], F32)
    X_s = multi("X", NC, [128, T], BF16)
    Qh_s = at("Qh_s", [128, T], BF16)
    Kh_s = at("Kh_s", [128, T], BF16)
    Vh_s = at("Vh_s", [128, T], BF16)
    qpre_s = multi("qpre", 2, [128, 512], BF16)
    t1_s = at("t1_s", [128, 512], F32)
    t2_s = at("t2_s", [128, 512], F32)
    eT_s = at("eT_s", [128, KT * 512], BF16)
    vtm_s = at("vtm", [128, KT * 128], BF16)
    shb_s = at("shb_s", [128, 512], BF16)
    wT_s = [multi(f"wT{p}_", 3, [128, D], BF16) for p in range(2)]
    atv_s = multi("atv", 2, [128, 512], BF16)
    atb_s = multi("atb", NC, [128, 512], BF16)
    woT_s = [at(f"woT{p}", [128, 4096], BF16) for p in range(2)]
    sq_s = multi("sq", 2, [128, 512], BF16)
    srow_s = at("srow_s", [1, 512], F32)
    qrow_s = at("qrow_s", [1, 512], F32)
    mb_s = at("mb_s", [128, 512], F32)
    rb_s = at("rb_s", [128, 512], F32)
    attn_end = cur[0]

    pw_s = multi("pw", 2, [128, PCOLS], F32)
    pt_s = at("pt_s", [128, PCOLS], F32)
    pq_s = at("pq_s", [128, PCOLS], BF16)
    pqt_s = multi("pqt", 2, [128, 1024], BF16)
    pabs_s = at("pabs_s", [128, PCOLS], F32)
    assert cur[0] <= SBUF_LIMIT, f"attn+prep over: {cur[0]}"

    fb = [attn_base]
    x2b_s = [multi(f"x2b{s}_", NC, [128, 512], BF16, base=fb) for s in range(NSEQ)]
    h_s = [multi(f"h{s}_", NHH, [128, 512], BF16, base=fb) for s in range(NSEQ)]
    qwT_s = [[at(f"qwT{p}{w}", [128, D], BF16, base=fb) for w in range(2)]
             for p in range(2)]
    q3T_s = [at(f"q3T{p}", [128, NHH * 128], BF16, base=fb) for p in range(2)]
    r2f_s = multi("r2f", 2, [128, 512], BF16, base=fb)
    r2t_s = multi("r2t", NT, [128, D], BF16, base=fb)
    sg_s = multi("sg", 2, [128, 512], BF16, base=fb)
    ost_s = multi("ost", 2, [128, D], F32, base=fb)
    l2v_s = at("l2v", [128, 8], F32, base=fb)
    assert fb[0] <= SBUF_LIMIT, f"ffn over: {fb[0]}"

    psA = nc.alloc_psum_tensor("psA", [128, 1024], F32)
    psB = nc.alloc_psum_tensor("psB", [128, 1024], F32)
    psC = nc.alloc_psum_tensor("psC", [128, 512], F32)
    psE = nc.alloc_psum_tensor("psE", [128, 512], F32)
    psF = nc.alloc_psum_tensor("psF", [128, 512], F32)
    psG = nc.alloc_psum_tensor("psG", [128, 1024], BF16)

    P = Prog(nc)
    c = P.c
    gslot = [[] for _ in range(8)]

    for nm, dst in (("ident", ident_s), ("ocb", ocb_s), ("rot", rot_s),
                    ("cos", cos_s), ("sin", sin_s)):
        P.dma("sp", "smc", dst[:, :], cmap[nm])
    P.dma("sp", "smc", orb_s[0:1, :], cmap["orb"])
    P.dma("sp", "smc", orf_s[0:1, :], cmap["orf"])
    P.dma("sp", "smc", ocf_s[:, :], cmap["ocf"])
    P.dma("sp", "smc", magic_s[:, :], cmap["magic"])
    N_CONST = c("smc")
    P.dma("pool", "smc2", bu_s[0:1, :], bu)

    for s in range(NSEQ):
        P.dma("pool", f"smxc{s}", x_bf[s], x_in[s])
    for n in ("wq", "wk", "wv", "wo"):
        P.dma("pool", "smwc", aw_bf[n], win[n])

    P.dve(lambda eng: eng.memset(eT_s[64:128, 512:1024], 0.0))

    prep_steps = []
    ps = {"pw_free": [[("smc2", 16)], [("smc2", 16)]], "pt_free": 0, "pq_free": 0,
          "pabs_free": 0}

    def gamma_chunk(widx, w_ap, row0, col0, ncols, par, first):
        def step():
            nd = P.dma("pool", f"smqf{par}", pw_s[par][:, 0:ncols],
                       w_ap[row0:row0 + 128, col0:col0 + ncols],
                       waits=ps["pw_free"][par])

            def fn2(eng, par=par, ncols=ncols, first=first):
                eng.tensor_reduce(ve_s[:, 3:4], pw_s[par][:, 0:ncols],
                                  axis=AX.X, op=ALU.add, apply_absolute_value=True)
                eng.drain()
                if first:
                    return eng.tensor_copy(pabs_s[:, 0:1], ve_s[:, 3:4])
                return eng.tensor_add(pabs_s[:, 0:1], pabs_s[:, 0:1], ve_s[:, 3:4])
            dv = P.dve(fn2, waits=[(f"smqf{par}", nd)] +
                       ([("smpe", c("smpe"))] if first else []))
            ps["pw_free"][par] = [("smdve", dv)]
        return step

    def gamma_fin(widx, count):
        def step():
            dv0 = c("smdve")

            def mm0(eng):
                return eng.matmul(psC[0:1, 0:1], pabs_s[:, 0:1], ocf_s[:, 0:1],
                                  start=True, stop=True)
            pv0 = P.pe(mm0, waits=[("smdve", dv0)] + P.bank_w("C"))

            def fin(eng, count=count):
                eng.tensor_scalar(ve_s[0:1, 0:1], psC[0:1, 0:1],
                                  1.0 / count, 1e-8, op0=ALU.mult, op1=ALU.add)
                eng.drain()
                return eng.reciprocal(ve_s[0:1, 1:2], ve_s[0:1, 0:1])
            dv = P.dve(fin, waits=[("smpe", pv0)])
            P.bank_r(["C"], [("smdve", dv)])

            def bc(eng):
                return eng.matmul(psC[:, 0:1], orf_s[0:1, :], ve_s[0:1, 1:2],
                                  start=True, stop=True)
            pv2 = P.pe(bc, waits=[("smdve", dv)] + P.bank_w("C"))

            def ev(eng, widx=widx):
                return eng.tensor_copy(rsc_s[widx][:, 0:1], psC[:, 0:1])
            dv2 = P.dve(ev, waits=[("smpe", pv2)])
            P.bank_r(["C"], [("smdve", dv2)])
        return step

    def quant_chunk(widx, w_ap, q_ap, row0, col0, ncols, par):
        def step():
            nd = P.dma("pool", f"smqf{par}", pw_s[par][:, 0:ncols],
                       w_ap[row0:row0 + 128, col0:col0 + ncols],
                       waits=ps["pw_free"][par])

            def qf(eng, par=par, ncols=ncols, widx=widx):
                eng.tensor_scalar(pt_s[:, 0:ncols], pw_s[par][:, 0:ncols],
                                  rsc_s[widx][:, 0:1], MAGIC,
                                  op0=ALU.mult, op1=ALU.add)
                eng.tensor_scalar(pt_s[:, 0:ncols], pt_s[:, 0:ncols],
                                  MAGIC, 1.0, op0=ALU.subtract, op1=ALU.min)
                return eng.tensor_scalar_max(pq_s[:, 0:ncols], pt_s[:, 0:ncols], -1.0)
            dv = P.pool(qf, waits=[(f"smqf{par}", nd), ("smdve", c("smdve")),
                                   ("smqs", ps["pq_free"])])
            ps["pw_free"][par] = [("smpool", dv)]
            ps["pq_free"] = P.dma("pool", "smqs",
                                  q_ap[row0:row0 + 128, col0:col0 + ncols],
                                  pq_s[:, 0:ncols], waits=[("smpool", dv)])
        return step

    def chunks_of(total, width):
        out = []
        o = 0
        while o < total:
            out.append((o, min(width, total - o)))
            o += width
        return out

    wshapes = {0: (win["w1"], FH, D), 1: (win["w2"], FH, D), 2: (win["w3"], D, FH)}
    for widx in (0, 1, 2):
        w_ap, rows, cols = wshapes[widx]
        first = True
        for i in range(rows // 128):
            for (c0, cn) in chunks_of(cols, PCOLS):
                prep_steps.append(gamma_chunk(widx, w_ap, i * 128, c0, cn,
                                              len(prep_steps) % 2, first))
                first = False
        prep_steps.append(gamma_fin(widx, rows * cols))
    for widx in (0, 1, 2):
        w_ap, rows, cols = wshapes[widx]
        for i in range(rows // 128):
            for (c0, cn) in chunks_of(cols, PCOLS):
                prep_steps.append(quant_chunk(widx, w_ap, q_bf[widx], i * 128,
                                              c0, cn, len(prep_steps) % 2))
        def mk_thop(widx=widx, rows=rows, cols=cols):
            def step():
                gate = [("smqs", c("smqs"))]
                if widx < 2:
                    for cc in range(NC):
                        for r0, nr in chunks_of(rows, 1024):
                            dst = q12T_img[widx][r0 // 128:(r0 + nr) // 128, 0:128,
                                                 cc * 128:(cc + 1) * 128].rearrange(
                                                     "h p f -> p h f")
                            thop_pair(q_bf[widx][r0:r0 + nr,
                                                 cc * 128:(cc + 1) * 128],
                                      dst, gate)
                else:
                    for hhc in range(NHH):
                        for r0, nr in chunks_of(D, 1024):
                            dst = q3T_img[r0 // 128:(r0 + nr) // 128, 0:128,
                                          hhc * 128:(hhc + 1) * 128].rearrange(
                                              "d p f -> p d f")
                            thop_pair(q_bf[2][r0:r0 + nr,
                                              hhc * 128:(hhc + 1) * 128],
                                      dst, gate)
            return step
        prep_steps.append(mk_thop())
    prep_iter = iter(prep_steps)
    PREP_PER_SLOT = max(1, (len(prep_steps) + NSEQ * H - 1) // (NSEQ * H))

    th_state = {"par": 0}

    def thop_pair(src_ap, img_dst, gate_waits):
        par = th_state["par"]
        th_state["par"] ^= 1
        nr = src_ap.shape[0]
        P.dma("sp", f"smth{par}", pqt_s[par][:, 0:nr], src_ap,
              waits=gate_waits + [(f"smtw{par}", c(f"smtw{par}"))], transpose=True)
        P.dma("pool", f"smtw{par}", img_dst, pqt_s[par][:, 0:nr],
              waits=[(f"smth{par}", c(f"smth{par}"))])

    def sched_attn_thop():
        gates = []
        for h0 in (0, 8):
            for w in ("wq", "wk", "wv"):
                for cc in range(NC):
                    dst = awT_img[w][h0:h0 + 8, 0:128,
                                     cc * 128:(cc + 1) * 128].rearrange(
                                         "h p f -> p h f")
                    thop_pair(aw_bf[w][h0 * 128:(h0 + 8) * 128,
                                       cc * 128:(cc + 1) * 128],
                              dst, [("smwc", 64)])
            gates.append([("smtw0", c("smtw0")), ("smtw1", c("smtw1"))])
        for g0 in (0, 4):
            for cc in range(NC):
                dst = woT_img[g0:g0 + 4, 0:128,
                              cc * 256:(cc + 1) * 256].rearrange("g p f -> p g f")
                thop_pair(aw_bf["wo"][g0 * 256:g0 * 256 + 1024,
                                      cc * 128:(cc + 1) * 128],
                          dst, [("smwc", 64)])
        gates.append([("smtw0", c("smtw0")), ("smtw1", c("smtw1"))])
        return gates

    st = {"wt_free": [[], []], "wt_have": [0, 0], "x_have": 0,
          "qpre_free": [0, 0], "t1_free": 0, "t2_free": 0,
          "Qh_free": 0, "Kh_free": 0, "Vh_free": 0, "eT_free": 0,
          "qk_ready": 0, "v_ready": 0,
          "shb_free": 0, "vtm_free": 0, "atv_free": [0, 0],
          "sq_free": [0, 0], "woT_free": [[], []], "atb_free": []}

    def load_X(s, extra=()):
        v = 0
        for cc in range(NC):
            v = P.dma("sp", "smxt", X_s[cc][:, :],
                      x_bf[s][0:T, cc * 128:(cc + 1) * 128],
                      waits=[(f"smxc{s}", 16), ("smc", N_CONST)] + list(extra),
                      transpose=True)
        return v

    def load_wT(h, par):
        gate = st["awT_gate"][0 if h < 8 else 1]
        for wi, name in enumerate(("wq", "wk", "wv")):
            P.dma("sp", f"smwt{par}", wT_s[par][wi][:, :], awT_img[name][h],
                  waits=gate + st["wt_free"][par])
        return c(f"smwt{par}")

    def proj_head(s, h, par, only=None):
        nw = st["wt_have"][par]
        nx = st["x_have"]
        for wi, name in enumerate(("q", "k", "v")[:only] if only else ("q", "k", "v")):
            for tc in range(NT):
                bank = ("E", "F")[tc % 2]
                pj = (psE, psF)[tc % 2]
                sl = slice(tc * 512, (tc + 1) * 512)

                def mk_mm(eng, wi=wi, pj=pj, sl=sl, par=par):
                    ins = None
                    for cc in range(NC):
                        ins = eng.matmul(pj[:, 0:512],
                                         wT_s[par][wi][:, cc * 128:(cc + 1) * 128],
                                         X_s[cc][:, sl],
                                         start=(cc == 0), stop=(cc == NC - 1))
                    return ins
                mmv = P.pe(mk_mm, waits=[(f"smwt{par}", nw), ("smxt", nx)]
                           + P.bank_w(bank))
                if name == "v":
                    def ev(eng, pj=pj, sl=sl):
                        return eng.copy(Vh_s[:, sl], pj[:, 0:512])
                    av = P.act(ev, waits=[("smpe", mmv), ("smpe", st["Vh_free"])])
                    P.bank_r([bank], [("smact", av)])
                else:
                    dst, dfree = ((Qh_s, "Qh_free") if name == "q"
                                  else (Kh_s, "Kh_free"))
                    qp = tc % 2

                    def ev(eng, pj=pj, qp=qp):
                        return eng.copy(qpre_s[qp][:, :], pj[:, 0:512])
                    av = P.act(ev, waits=[("smpe", mmv),
                                          ("smpe", st["qpre_free"][qp])])
                    P.bank_r([bank], [("smact", av)])

                    def mt1(eng, qp=qp, sl=sl):
                        return eng.tensor_mul(t1_s[:, :], qpre_s[qp][:, :],
                                              cos_s[:, sl])
                    dv1 = P.dve(mt1, waits=[("smact", av), ("smdve", st["t1_free"])])

                    def mrot(eng, qp=qp):
                        return eng.matmul(psC[:, 0:512], rot_s[:, :],
                                          qpre_s[qp][:, :], start=True, stop=True)
                    mmr = P.pe(mrot, waits=[("smact", av)] + P.bank_w("C"))
                    st["qpre_free"][qp] = mmr

                    def mt2(eng, sl=sl):
                        return eng.tensor_mul(t2_s[:, :], psC[:, 0:512], sin_s[:, sl])
                    dv2 = P.dve(mt2, waits=[("smpe", mmr), ("smdve", st["t2_free"])])
                    P.bank_r(["C"], [("smdve", dv2)])

                    def madd(eng, dst=dst, sl=sl):
                        return eng.tensor_add(dst[:, sl], t1_s[:, :], t2_s[:, :])
                    dv3 = P.dve(madd, waits=[("smpe", st[dfree])])
                    st["t1_free"] = dv3
                    st["t2_free"] = dv3
        st["wt_free"][par] = [("smpe", c("smpe"))]

    def sT_exp(qg):
        qsl = slice(qg * 512, (qg + 1) * 512)
        for p in range(KT // 2):
            pp = (psA, psB)[p % 2]
            bks = ("A0", "A1") if p % 2 == 0 else ("B0", "B1")

            def mm(eng, p=p, pp=pp, qsl=qsl):
                ins = None
                for jj in range(2):
                    j = 2 * p + jj
                    ins = eng.matmul(pp[:, jj * 512:(jj + 1) * 512],
                                     Kh_s[:, _kstart(j):_kstart(j) + 128],
                                     Qh_s[:, qsl], start=True, stop=True)
                return ins
            mmv = P.pe(mm, waits=[("smdve", st["qk_ready"])] + P.bank_w(*bks))
            if p == 0:
                def ex(eng):
                    eng.activation(eT_s[:, 0:512], psA[:, 0:512], AF.Exp,
                                   bias=0.0, scale=INV_SQRT_HD)
                    return eng.activation(eT_s[0:76, 512:1024], psA[0:76, 512:1024],
                                          AF.Exp, bias=0.0, scale=INV_SQRT_HD)
            else:
                def ex(eng, p=p, pp=pp):
                    return eng.activation(eT_s[:, p * 1024:(p + 1) * 1024], pp[:, :],
                                          AF.Exp, bias=0.0, scale=INV_SQRT_HD)
            av = P.act(ex, waits=[("smpe", mmv), ("smpe", st["eT_free"])])
            P.bank_r(bks, [("smact", av)])
        return c("smact")

    def S_sum(exp_done):
        def mm(eng):
            ins = None
            for j in range(KT):
                ins = eng.matmul(psC[0:1, 0:512], ocb_s[:, 0:1],
                                 eT_s[:, j * 512:(j + 1) * 512],
                                 start=(j == 0), stop=(j == KT - 1))
            return ins
        return P.pe(mm, waits=[("smact", exp_done)] + P.bank_w("C"))

    def shb_make(sv):
        def rc(eng):
            with nc.allow_low_precision(reason="1/S in bf16; probs are bf16 anyway"):
                return eng.reciprocal(shb_s[0:1, 0:512], psC[0:1, 0:512])
        dv = P.dve(rc, waits=[("smpe", sv), ("smdve", st["shb_free"])])
        P.bank_r(["C"], [("smdve", dv)])

        def bc(eng):
            return eng.matmul(psC[:, 0:512], orb_s[0:1, :], shb_s[0:1, 0:512],
                              start=True, stop=True)
        pv = P.pe(bc, waits=[("smdve", dv)] + P.bank_w("C"))

        def ev(eng):
            return eng.tensor_copy(shb_s[:, :], psC[:, 0:512])
        dv2 = P.dve(ev, waits=[("smpe", pv)])
        P.bank_r(["C"], [("smdve", dv2)])
        return dv2

    def pv_evac(s, h, qg, exp_done, shb_done, vtm_done):
        bank = ("E", "F")[qg % 2]
        pv_ = (psE, psF)[qg % 2]

        def mm(eng, pv_=pv_):
            ins = None
            for j in range(KT):
                ins = eng.matmul(pv_[:, 0:512], vtm_s[:, j * 128:(j + 1) * 128],
                                 eT_s[:, j * 512:(j + 1) * 512],
                                 start=(j == 0), stop=(j == KT - 1))
            return ins
        mv = P.pe(mm, waits=[("smact", exp_done), ("smdve", shb_done),
                             ("smact", vtm_done)] + P.bank_w(bank))
        ap = qg % 2

        def ev(eng, pv_=pv_, ap=ap):
            return eng.tensor_mul(atv_s[ap][:, :], pv_[:, 0:512], shb_s[:, :])
        dv = P.dve(ev, waits=[("smpe", mv), (f"smat{ap}", st["atv_free"][ap])])
        P.bank_r([bank], [("smdve", dv)])
        st["eT_free"] = mv
        st["shb_free"] = dv
        st["atv_free"][ap] = P.dma("act", f"smat{ap}",
                                   at_sp[h][:, qg * 512:(qg + 1) * 512],
                                   atv_s[ap][:, :], waits=[("smdve", dv)])
        return dv

    def prepass_vupdate(h, exp_done, sv):
        def asum(eng):
            ins = None
            for j in range(2, KT):
                ins = eng.matmul(psE[0:1, 0:1],
                                 eT_s[:, j * 512 + 511:j * 512 + 512],
                                 ocb_s[:, 0:1], start=(j == 2), stop=(j == KT - 1))
            return ins
        pa = P.pe(asum, waits=[("smact", exp_done)] + P.bank_w("E"))

        def pmath(eng, h=h):
            eng.tensor_copy(ve_s[0:1, 0:1], psE[0:1, 0:1])
            eng.tensor_copy(ve_s[0:1, 1:2], eT_s[0:1, 2 * 512 + 511:2 * 512 + 512])
            eng.tensor_scalar_mul(ve_s[0:1, 2:3], psC[0:1, 511:512], 1e-6)
            eng.drain()
            eng.tensor_sub(ve_s[0:1, 0:1], ve_s[0:1, 0:1], ve_s[0:1, 1:2])
            eng.drain()
            eng.tensor_scalar_mul(ve_s[0:1, 0:1], ve_s[0:1, 0:1], 1.0 / 1023.0)
            eng.drain()
            eng.tensor_add(ve_s[0:1, 0:1], ve_s[0:1, 0:1], ve_s[0:1, 2:3])
            eng.drain()
            eng.reciprocal(ve_s[0:1, 0:1], ve_s[0:1, 0:1])
            eng.drain()
            eng.tensor_mul(ve_s[0:1, 0:1], ve_s[0:1, 1:2], ve_s[0:1, 0:1])
            eng.drain()
            eng.tensor_copy(pdbg_s[0:1, h:h + 1], ve_s[0:1, 0:1])
            eng.tensor_tensor(ve_s[0:1, 1:2], bu_s[0:1, h:h + 1], ve_s[0:1, 0:1],
                              op=ALU.is_lt)
            eng.drain()
            eng.tensor_copy(pdbg_s[0:1, 16 + h:17 + h], ve_s[0:1, 1:2])
            return eng.tensor_scalar_mul(ve_s[0:1, 1:2], ve_s[0:1, 1:2], 1.0 / 1024.0)
        dv = P.dve(pmath, waits=[("smpe", pa), ("smpe", sv), ("smc2", 16)])
        P.bank_r(["E"], [("smdve", dv)])
        P.bank_r(["C"], [("smdve", dv)])

        def bcf(eng):
            return eng.matmul(psE[:, 0:1], orf_s[0:1, :], ve_s[0:1, 1:2],
                              start=True, stop=True)
        pb = P.pe(bcf, waits=[("smdve", dv)] + P.bank_w("E"))

        def vupd(eng):
            eng.tensor_copy(ve_s[:, 0:1], psE[:, 0:1])
            eng.tensor_copy(ve_s[:, 1:2], Vh_s[:, 1024:1025])
            eng.drain()
            eng.tensor_mul(ve_s[:, 0:1], ve_s[:, 0:1], ve_s[:, 1:2])
            eng.drain()
            return eng.tensor_scalar_add(Vh_s[:, 1025:2048], Vh_s[:, 1025:2048],
                                         ve_s[:, 0:1])
        dv2 = P.dve(vupd, waits=[("smpe", pb), ("smact", st["v_ready"])])
        P.bank_r(["E"], [("smdve", dv2)])
        return dv2

    def v_transposes(vupd_done):
        for g0, g1 in ((0, 8), (8, KT)):
            tv = 0
            for j in range(g0, g1):
                slot = j - g0

                def tr(eng, slot=slot, j=j):
                    return eng.transpose(psG[:, slot * 128:(slot + 1) * 128],
                                         Vh_s[:, _kstart(j):_kstart(j) + 128],
                                         ident_s[:, :])
                w = [("smdve", vupd_done)] + (P.bank_w("G") if j == g0 else [])
                tv = P.pe(tr, waits=w)

            def ev(eng, g0=g0, g1=g1):
                return eng.copy(vtm_s[:, g0 * 128:g1 * 128],
                                psG[:, 0:(g1 - g0) * 128])
            av = P.act(ev, waits=[("smpe", tv), ("smpe", st["vtm_free"])])
            P.bank_r(["G"], [("smact", av)])
        return c("smact")

    def attn_head(s, h):
        par = h % 2
        proj_head(s, h, par)
        st["qk_ready"] = c("smdve")
        st["v_ready"] = c("smact")
        nh, npar = h + 1, (h + 1) % 2
        if nh < H:
            st["wt_have"][npar] = load_wT(nh, npar)
        elif s + 1 < NSEQ:
            st["wt_have"][0] = load_wT(0, 0)
        ed3 = sT_exp(3)
        sv3 = S_sum(ed3)
        vu = prepass_vupdate(h, ed3, sv3)
        vt = v_transposes(vu)
        st["Vh_free"] = c("smpe")
        sh3 = shb_make(sv3)
        pv_evac(s, h, 3, ed3, sh3, vt)
        for qg in (0, 1, 2):
            ed = sT_exp(qg)
            sv = S_sum(ed)
            sh = shb_make(sv)
            pv_evac(s, h, qg, ed, sh, vt)
        st["Qh_free"] = c("smpe")
        st["Kh_free"] = c("smpe")
        st["vtm_free"] = c("smpe")

    def oproj_ln1(s):
        nat0, nat1 = c("smat0"), c("smat1")
        for tc in range(NT):
            sl = slice(tc * 512, (tc + 1) * 512)
            for cc in range(NC):
                P.dma("sp", "smatb", atb_s[cc][:, :], at_sp[cc][:, sl],
                      waits=[("smat0", nat0), ("smat1", nat1)] + st["atb_free"])
            natb = c("smatb")
            for g in range(8):
                gpar = g % 2
                P.dma("sp", f"smwo{gpar}", woT_s[gpar][:, :], woT_img[g],
                      waits=st["awT_gate"][2] + st["woT_free"][gpar])
                nwo = c(f"smwo{gpar}")
                for mi in range(2):
                    m = g * 2 + mi
                    bank = ("E", "F")[m % 2]
                    pj = (psE, psF)[m % 2]

                    def mm(eng, mi=mi, pj=pj, gpar=gpar):
                        ins = None
                        for cc in range(NC):
                            ins = eng.matmul(
                                pj[:, 0:512],
                                woT_s[gpar][:, cc * 256 + mi * 128:
                                            cc * 256 + (mi + 1) * 128],
                                atb_s[cc][:, :],
                                start=(cc == 0), stop=(cc == NC - 1))
                        return ins
                    mv = P.pe(mm, waits=[(f"smwo{gpar}", nwo), ("smatb", natb)]
                              + P.bank_w(bank))

                    def ev(eng, m=m, pj=pj, sl=sl):
                        return eng.tensor_add(X_s[m][:, sl], pj[:, 0:512],
                                              X_s[m][:, sl])
                    dv = P.dve(ev, waits=[("smpe", mv)], mode="instr")
                    P.bank_r([bank], [("smdve", dv)])
                st["woT_free"][gpar] = [("smpe", c("smpe"))]
            rX = P.dve(lambda e: e.engine_nop())
            def mms(eng, sl=sl):
                ins = None
                for m in range(NC):
                    ins = eng.matmul(psC[0:1, 0:512], ocb_s[:, 0:1], X_s[m][:, sl],
                                     start=(m == 0), stop=(m == NC - 1))
                return ins
            sv = P.pe(mms, waits=[("smdve", rX)] + P.bank_w("C"))

            def evs(eng):
                return eng.tensor_copy(srow_s[0:1, :], psC[0:1, 0:512])
            dvs = P.dve(evs, waits=[("smpe", sv)])
            P.bank_r(["C"], [("smdve", dvs)])
            for m in range(NC):
                def sq(eng, m=m, sl=sl):
                    return eng.activation(sq_s[m % 2][:, :], X_s[m][:, sl], AF.Square)
                av = P.act(sq, waits=[("smdve", rX), ("smpe", st["sq_free"][m % 2])])

                def mmq(eng, m=m):
                    return eng.matmul(psE[0:1, 0:512], ocb_s[:, 0:1],
                                      sq_s[m % 2][:, :], start=(m == 0),
                                      stop=(m == NC - 1))
                st["sq_free"][m % 2] = P.pe(
                    mmq, waits=[("smact", av)] + (P.bank_w("E") if m == 0 else []))
            qv = c("smpe")

            def evq(eng):
                return eng.tensor_copy(qrow_s[0:1, :], psE[0:1, 0:512])
            dvq = P.dve(evq, waits=[("smpe", qv)])
            P.bank_r(["E"], [("smdve", dvq)])

            def rows(eng):
                eng.tensor_scalar_mul(srow_s[0:1, :], srow_s[0:1, :], 1.0 / D)
                eng.tensor_scalar(qrow_s[0:1, :], qrow_s[0:1, :], 1.0 / D, 1e-5,
                                  op0=ALU.mult, op1=ALU.add)
                eng.drain()
                eng.tensor_mul(t1_s[0:1, 0:512], srow_s[0:1, :], srow_s[0:1, :])
                eng.drain()
                return eng.tensor_sub(qrow_s[0:1, :], qrow_s[0:1, :],
                                      t1_s[0:1, 0:512])
            rv = P.dve(rows, waits=[("smdve", st["t1_free"])])
            st["t1_free"] = rv

            def rsq(eng):
                return eng.activation(qrow_s[0:1, :], qrow_s[0:1, :], AF.Sqrt)
            av2 = P.act(rsq, waits=[("smdve", rv)])

            def rrec(eng):
                return eng.reciprocal(qrow_s[0:1, :], qrow_s[0:1, :])
            rrv = P.dve(rrec, waits=[("smact", av2)])

            def bcm(eng):
                return eng.matmul(psC[:, 0:512], orf_s[0:1, :], srow_s[0:1, :],
                                  start=True, stop=True)
            p1 = P.pe(bcm, waits=[("smdve", rrv)] + P.bank_w("C"))

            def bcr(eng):
                return eng.matmul(psE[:, 0:512], orf_s[0:1, :], qrow_s[0:1, :],
                                  start=True, stop=True)
            p2 = P.pe(bcr, waits=P.bank_w("E"))

            def evb(eng):
                eng.tensor_copy(mb_s[:, :], psC[:, 0:512])
                return eng.tensor_copy(rb_s[:, :], psE[:, 0:512])
            dvb = P.dve(evb, waits=[("smpe", p2)])
            P.bank_r(["C"], [("smdve", dvb)])
            P.bank_r(["E"], [("smdve", dvb)])
            def ap_(eng, sl=sl):
                ins = None
                for m in range(NC):
                    eng.tensor_sub(X_s[m][:, sl], X_s[m][:, sl], mb_s[:, :])
                    ins = eng.tensor_mul(X_s[m][:, sl], X_s[m][:, sl], rb_s[:, :])
                return ins
            P.dve(ap_, waits=[("smdve", dvb)])
            st["atb_free"] = [("smpe", c("smpe")), ("smdve", c("smdve"))]
        xv = c("smdve")
        for m in range(NC):
            P.dma("act", "smsp", x2_sp[s][m], X_s[m][:, :], waits=[("smdve", xv)])
        return c("smsp")

    fs = {"qw_free": [[], []], "qw_have": [0, 0], "q3_free": [[], []],
          "q3_have": [0, 0], "ost_free": [0, 0], "x2_free": [],
          "h_free": 0, "r2f_free": [0, 0], "r2t_free": 0, "sg_free": [0, 0]}

    def ffn_tb(tb, nq, nsp, ntw):
        for s in range(NSEQ):
            for m in range(NC):
                P.dma("pool", "smx2", x2b_s[s][m][:, :],
                      x2_sp[s][m][:, tb * 512:(tb + 1) * 512],
                      waits=[("smsp", nsp)] + fs["x2_free"])
        nx2 = c("smx2")

        def load_qw12(hhc, par):
            for wi in range(2):
                P.dma("sp", f"smqt{par}", qwT_s[par][wi][:, :], q12T_img[wi][hhc],
                      waits=ntw + fs["qw_free"][par])
            fs["qw_have"][par] = c(f"smqt{par}")
        load_qw12(0, 0)
        for hhc in range(NHH):
            par = hhc % 2
            have = fs["qw_have"][par]
            if hhc + 1 < NHH:
                load_qw12(hhc + 1, (hhc + 1) % 2)
            for s in range(NSEQ):
                pA2 = (psA, psB)[s]
                bks = ("A0", "A1") if s == 0 else ("B0", "B1")

                def mm1(eng, s=s, par=par, pA2=pA2):
                    ins = None
                    for cc in range(NC):
                        ins = eng.matmul(pA2[:, 0:512],
                                         qwT_s[par][0][:, cc * 128:(cc + 1) * 128],
                                         x2b_s[s][cc][:, :],
                                         start=(cc == 0), stop=(cc == NC - 1))
                    return ins
                P.pe(mm1, waits=[(f"smqt{par}", have), ("smx2", nx2)]
                     + P.bank_w(bks[0]))

                def mm2(eng, s=s, par=par, pA2=pA2):
                    ins = None
                    for cc in range(NC):
                        ins = eng.matmul(pA2[:, 512:1024],
                                         qwT_s[par][1][:, cc * 128:(cc + 1) * 128],
                                         x2b_s[s][cc][:, :],
                                         start=(cc == 0), stop=(cc == NC - 1))
                    return ins
                mv = P.pe(mm2, waits=P.bank_w(bks[1]))

                def sgm(eng, s=s, pA2=pA2):
                    return eng.activation(sg_s[s][:, :], pA2[:, 0:512], AF.Sigmoid)
                av = P.act(sgm, waits=[("smpe", mv), ("smdve", fs["sg_free"][s])])

                def hm(eng, s=s, hhc=hhc, pA2=pA2):
                    return eng.tensor_mul(h_s[s][hhc][:, :], sg_s[s][:, :],
                                          pA2[:, 512:1024])
                dv = P.dve(hm, waits=[("smact", av), ("smpe", fs["h_free"])])
                fs["sg_free"][s] = dv
                P.bank_r([bks[0]], [("smact", av)])
                P.bank_r([bks[1]], [("smdve", dv)])
            fs["qw_free"][par] = [("smpe", c("smpe"))]
        hv = c("smdve")

        def load_qw3(dc, par):
            P.dma("sp", f"smq3{par}", q3T_s[par][:, :], q3T_img[dc],
                  waits=ntw + fs["q3_free"][par])
            fs["q3_have"][par] = c(f"smq3{par}")
        for s in range(NSEQ):
            load_qw3(0, 0)
            for dc in range(NC):
                par = dc % 2
                have = fs["q3_have"][par]
                if dc + 1 < NC:
                    load_qw3(dc + 1, (dc + 1) % 2)
                bank = ("E", "F")[dc % 2]
                pv_ = (psE, psF)[dc % 2]

                def mm(eng, s=s, par=par, pv_=pv_):
                    ins = None
                    for hhc in range(NHH):
                        ins = eng.matmul(pv_[:, 0:512],
                                         q3T_s[par][:, hhc * 128:(hhc + 1) * 128],
                                         h_s[s][hhc][:, :],
                                         start=(hhc == 0), stop=(hhc == NHH - 1))
                    return ins
                mv = P.pe(mm, waits=[(f"smq3{par}", have), ("smdve", hv)]
                          + P.bank_w(bank))

                def ev(eng, s=s, dc=dc, pv_=pv_):
                    return eng.tensor_add(r2f_s[dc % 2][:, :], pv_[:, 0:512],
                                          x2b_s[s][dc][:, :])
                dv = P.dve(ev, waits=[("smpe", mv), ("smpe", fs["r2f_free"][dc % 2])])
                P.bank_r([bank], [("smdve", dv)])
                tv = 0
                for tc in range(4):
                    def tr(eng, dc=dc, tc=tc):
                        return eng.transpose(psG[:, tc * 128:(tc + 1) * 128],
                                             r2f_s[dc % 2][:, tc * 128:(tc + 1) * 128],
                                             ident_s[:, :])
                    w = [("smdve", dv)] + (P.bank_w("G") if tc == 0 else [])
                    tv = P.pe(tr, waits=w)
                av = 0
                for tc in range(4):
                    def tev(eng, dc=dc, tc=tc):
                        return eng.copy(r2t_s[tc][:, dc * 128:(dc + 1) * 128],
                                        psG[:, tc * 128:(tc + 1) * 128])
                    av = P.act(tev, waits=[("smpe", tv), ("smdve", fs["r2t_free"])])
                P.bank_r(["G"], [("smact", av)])
                fs["r2f_free"][dc % 2] = c("smpe")
                fs["q3_free"][par] = [("smpe", c("smpe"))]
            rT = c("smact")
            for tc in range(4):
                def st1(eng, tc=tc):
                    return eng.reduce_sum(l2v_s[:, 0:1], r2t_s[tc][:, :], axis=AX.X)
                P.dve(st1, waits=[("smact", rT)])

                def st2(eng, tc=tc):
                    ins = None
                    for piece in range(4):
                        sl2 = slice(piece * 512, (piece + 1) * 512)
                        ins = eng.activation(sg_s[0][:, :], r2t_s[tc][:, sl2],
                                             AF.Square,
                                             accum_out=l2v_s[:, 2 + piece:3 + piece])
                    return ins
                a2 = P.act(st2, waits=[("smact", rT), ("smdve", fs["sg_free"][0])])

                def vecs(eng):
                    eng.tensor_add(l2v_s[:, 1:2], l2v_s[:, 2:3], l2v_s[:, 3:4])
                    eng.tensor_add(l2v_s[:, 6:7], l2v_s[:, 4:5], l2v_s[:, 5:6])
                    eng.tensor_scalar_mul(l2v_s[:, 0:1], l2v_s[:, 0:1], 1.0 / D)
                    eng.drain()
                    eng.tensor_add(l2v_s[:, 1:2], l2v_s[:, 1:2], l2v_s[:, 6:7])
                    eng.tensor_mul(l2v_s[:, 2:3], l2v_s[:, 0:1], l2v_s[:, 0:1])
                    eng.drain()
                    eng.tensor_scalar_mul(l2v_s[:, 1:2], l2v_s[:, 1:2], 1.0 / D)
                    eng.drain()
                    eng.tensor_sub(l2v_s[:, 1:2], l2v_s[:, 1:2], l2v_s[:, 2:3])
                    eng.drain()
                    return eng.tensor_scalar_add(l2v_s[:, 1:2], l2v_s[:, 1:2], 1e-5)
                vv = P.dve(vecs, waits=[("smact", a2)])

                def sqr(eng):
                    return eng.activation(l2v_s[:, 1:2], l2v_s[:, 1:2], AF.Sqrt)
                a3 = P.act(sqr, waits=[("smdve", vv)])

                def rstd_bias(eng):
                    eng.reciprocal(l2v_s[:, 1:2], l2v_s[:, 1:2])
                    eng.drain()
                    eng.tensor_mul(l2v_s[:, 2:3], l2v_s[:, 0:1], l2v_s[:, 1:2])
                    eng.drain()
                    return eng.tensor_scalar_mul(l2v_s[:, 2:3], l2v_s[:, 2:3], -1.0)
                rb2 = P.dve(rstd_bias, waits=[("smact", a3)])

                def appl(eng, tc=tc):
                    return eng.activation(ost_s[tc % 2][:, :], r2t_s[tc][:, :],
                                          AF.Identity, bias=l2v_s[:, 2:3],
                                          scale=l2v_s[:, 1:2])
                ov = P.act(appl, waits=[("smdve", rb2),
                                        (f"smout{tc % 2}", fs["ost_free"][tc % 2])])
                fs["ost_free"][tc % 2] = P.dma(
                    "act", f"smout{tc % 2}",
                    y_out[s][tb * 512 + tc * 128: tb * 512 + (tc + 1) * 128, :],
                    ost_s[tc % 2][:, :], waits=[("smact", ov)])
            fs["r2t_free"] = c("smdve")
        fs["h_free"] = c("smpe")
        fs["x2_free"] = [("smpe", c("smpe")), ("smdve", c("smdve"))]

    def dump_rows(entries, waits):
        for ap_, r0, wdt in entries:
            P.dma("pool", "smdbg", dbg[r0:r0 + 128, 0:wdt], ap_, waits=waits)
        P.barrier("sp", [("smdbg", c("smdbg"))])

    st["awT_gate"] = sched_attn_thop()
    st["x_have"] = load_X(0)
    st["wt_have"][0] = load_wT(0, 0)

    stopped = False
    if dump in ("pq0", "pq1", "pq2"):
        level = {"pq0": 0, "pq1": 1, "pq2": 2}[dump]
        ntc = 2 if dump == "pq2" else NT
        nw = st["wt_have"][0]
        nx = st["x_have"]
        for tc in range(ntc):
            bank = ("E", "F")[tc % 2]
            pj = (psE, psF)[tc % 2]
            sl = slice(tc * 512, (tc + 1) * 512)
            def mk_mm(eng, pj=pj, sl=sl):
                ins = None
                for cc in range(NC):
                    ins = eng.matmul(pj[:, 0:512], wT_s[0][0][:, cc * 128:(cc + 1) * 128],
                                     X_s[cc][:, sl], start=(cc == 0), stop=(cc == NC - 1))
                return ins
            mmv = P.pe(mk_mm, waits=[("smwt0", nw), ("smxt", nx)] + P.bank_w(bank))
            qp = tc % 2
            def ev(eng, pj=pj, qp=qp):
                return eng.copy(qpre_s[qp][:, :], pj[:, 0:512])
            av = P.act(ev, waits=[("smpe", mmv), ("smpe", st["qpre_free"][qp])])
            P.bank_r([bank], [("smact", av)])
            if level >= 1:
                def mt1(eng, qp=qp, sl=sl):
                    return eng.tensor_mul(t1_s[:, :], qpre_s[qp][:, :], cos_s[:, sl])
                dv1 = P.dve(mt1, waits=[("smact", av), ("smdve", st["t1_free"])])
            if level >= 2:
                def mrot(eng, qp=qp):
                    return eng.matmul(psC[:, 0:512], rot_s[:, :], qpre_s[qp][:, :],
                                      start=True, stop=True)
                mmr = P.pe(mrot, waits=[("smact", av)] + P.bank_w("C"))
                st["qpre_free"][qp] = mmr
                def mt2(eng, sl=sl):
                    return eng.tensor_mul(t2_s[:, :], psC[:, 0:512], sin_s[:, sl])
                dv2 = P.dve(mt2, waits=[("smpe", mmr), ("smdve", st["t2_free"])])
                P.bank_r(["C"], [("smdve", dv2)])
                def madd(eng, sl=sl):
                    return eng.tensor_add(Qh_s[:, sl], t1_s[:, :], t2_s[:, :])
                dv3 = P.dve(madd, waits=[("smpe", st["Qh_free"])])
                st["t1_free"] = dv3
                st["t2_free"] = dv3
        dump_rows([(Qh_s[:, :], 0, T)],
                  [("smdve", c("smdve")), ("smact", c("smact")), ("smpe", c("smpe"))])
        stopped = True
    if dump in ("projq", "projqk"):
        proj_head(0, 0, 0, only=1 if dump == "projq" else 2)
        dump_rows([(Qh_s[:, :], 0, T), (Kh_s[:, :], 128, T)],
                  [("smdve", c("smdve")), ("smact", c("smact")), ("smpe", c("smpe"))])
        stopped = True
    if dump in ("p0", "p1", "p2", "p3"):
        level = int(dump[1])
        nw = st["wt_have"][0]
        nx = st["x_have"]
        def mk_mm(eng):
            ins = None
            for cc in range(NC):
                ins = eng.matmul(psE[:, 0:512], wT_s[0][0][:, cc * 128:(cc + 1) * 128],
                                 X_s[cc][:, 0:512], start=(cc == 0), stop=(cc == NC - 1))
            return ins
        mmv = P.pe(mk_mm, waits=[("smwt0", nw), ("smxt", nx)])
        def ev(eng):
            return eng.copy(qpre_s[0][:, :], psE[:, 0:512])
        P.act(ev, waits=[("smpe", mmv)])
        if level >= 1:
            def mt1(eng):
                return eng.tensor_mul(t1_s[:, :], psE[:, 0:512], cos_s[:, 0:512])
            P.dve(mt1, waits=[("smpe", mmv)])
        if level >= 2:
            def mrot(eng):
                return eng.matmul(psC[:, 0:512], rot_s[:, :], qpre_s[0][:, :],
                                  start=True, stop=True)
            mmr = P.pe(mrot, waits=[("smact", c("smact"))])
            if level >= 3:
                def mt2(eng):
                    return eng.tensor_mul(t2_s[:, :], psC[:, 0:512], sin_s[:, 0:512])
                P.dve(mt2, waits=[("smpe", mmr)])
                def madd(eng):
                    return eng.tensor_add(Qh_s[:, 0:512], t1_s[:, :], t2_s[:, :])
                P.dve(madd)
        dump_rows([(qpre_s[0][:, :], 0, 512), (Qh_s[:, 0:512], 128, 512)],
                  [("smdve", c("smdve")), ("smact", c("smact")), ("smpe", c("smpe"))])
        stopped = True
    if dump == "proj":
        proj_head(0, 0, 0)
        dump_rows([(Qh_s[:, :], 0, T), (Kh_s[:, :], 128, T),
                   (Vh_s[:, :], 256, T)],
                  [("smdve", c("smdve")), ("smact", c("smact")),
                   ("smpe", c("smpe"))])
        stopped = True
    if dump == "X":
        dump_rows([(X_s[m][:, :], m * 128, T) for m in range(NC)],
                  [("smxt", st["x_have"])])
        stopped = True

    done_sp = 0
    if not stopped:
        for s in range(NSEQ):
            for h in range(H):
                attn_head(s, h)
                for _ in range(PREP_PER_SLOT):
                    stp = next(prep_iter, None)
                    if stp is not None:
                        stp()
                if dump == "qkv" and s == 0 and h == 0:
                    dump_rows([(Qh_s[:, :], 0, T), (Kh_s[:, :], 128, T),
                               (Vh_s[:, :], 256, T)],
                              [("smdve", c("smdve")), ("smact", c("smact")),
                               ("smpe", c("smpe"))])
                    stopped = True
                    break
            if stopped:
                break
            if dump == "attn" and s == 0:
                dump_rows([], [])
                for m in range(NC):
                    P.dma("pool", "smdbg", dbg[m * 128:(m + 1) * 128, 0:T],
                          at_sp[m], waits=[("smat0", c("smat0")),
                                           ("smat1", c("smat1"))])
                P.dma("pool", "smdbg", dbg[2048:2049, 0:48], pdbg_s[0:1, :],
                      waits=[("smdve", c("smdve"))])
                P.barrier("sp", [("smdbg", c("smdbg"))])
                stopped = True
                break
            done_sp = oproj_ln1(s)
            if dump == "x2" and s == 0:
                dump_rows([(X_s[m][:, :], m * 128, T) for m in range(NC)],
                          [("smdve", c("smdve")), ("smsp", done_sp)])
                stopped = True
                break
            if s + 1 < NSEQ:
                st["x_have"] = load_X(s + 1, extra=[("smsp", done_sp)])

    if not stopped:
        for stp in prep_iter:
            stp()
        nq = c("smqs")
        ntw = [("smtw0", c("smtw0")), ("smtw1", c("smtw1"))]
        for tb in range(TBLK):
            ffn_tb(tb, nq, done_sp, ntw)
            if dump == "h" and tb == 0:
                dump_rows([(h_s[0][m][:, :], m * 128, 512) for m in range(32)],
                          [("smdve", c("smdve"))])
                stopped = True
                break
        if not stopped:
            P.barrier("sp", [("smout0", c("smout0")), ("smout1", c("smout1"))])

    body = P.emit()
    with nc.Block() as blk:
        blk.tensor(body("pe"))
        blk.scalar(body("act"))
        blk.vector(body("dve"))
        blk.gpsimd(body("pool"))
        blk.sync(body("sp"))
    return nc


def kernel(L0_attn_hidden_states, L0_attn_wq, L0_attn_wk, L0_attn_wv,
           L0_attn_wo, L0_attn_bern_u, L0_ffn_w1, L0_ffn_w2, L0_ffn_w3):
    x = np.ascontiguousarray(L0_attn_hidden_states, dtype=np.float32)
    nc = build_program()
    common = {
        "wq": np.ascontiguousarray(L0_attn_wq, dtype=np.float32),
        "wk": np.ascontiguousarray(L0_attn_wk, dtype=np.float32),
        "wv": np.ascontiguousarray(L0_attn_wv, dtype=np.float32),
        "wo": np.ascontiguousarray(L0_attn_wo, dtype=np.float32),
        "w1": np.ascontiguousarray(L0_ffn_w1, dtype=np.float32),
        "w2": np.ascontiguousarray(L0_ffn_w2, dtype=np.float32),
        "w3": np.ascontiguousarray(L0_ffn_w3, dtype=np.float32),
        "bu": np.ascontiguousarray(L0_attn_bern_u, dtype=np.float32),
    }
    in_maps = [dict(common, x=x[i * 2:(i + 1) * 2]) for i in range(8)]
    res = run_bass_kernel_spmd(nc, in_maps, core_ids=list(range(8)))
    return np.concatenate([res.results[i]["y"] for i in range(8)], axis=0)


# version 06
# speedup vs baseline: 3.2412x; 1.1721x over previous
import math

import numpy as np
import ml_dtypes

import concourse.bass as bass
import concourse.mybir as mybir
from concourse.bass_utils import run_bass_kernel_spmd

F32 = mybir.dt.float32
BF16 = mybir.dt.bfloat16
AF = mybir.ActivationFunctionType
ALU = mybir.AluOpType
AX = mybir.AxisListType

D = 2048
T = 2048
H = 16
HD = 128
NSEQ = 2
NC = D // 128
NT = T // 512
KT = 10
FH = 5632
NHH = FH // 128
TBLK = 4
MAGIC = 12582912.0
INV_SQRT_HD = 1.0 / math.sqrt(HD)
PCOLS = 1024

DETECT_RACES = False


def _kstart(j):
    return j * 128 if j < 2 else 1024 + (j - 2) * 128


def _host_consts():
    ident = np.eye(128, dtype=ml_dtypes.bfloat16)
    ones_col_bf = np.ones((128, 1), dtype=ml_dtypes.bfloat16)
    ones_row_bf = np.ones((1, 128), dtype=ml_dtypes.bfloat16)
    ones_row_f = np.ones((1, 128), dtype=np.float32)
    ones_col_f = np.ones((128, 1), dtype=np.float32)
    R = np.zeros((128, 128), dtype=np.float32)
    R[np.arange(64), np.arange(64) + 64] = -1.0
    R[np.arange(64) + 64, np.arange(64)] = 1.0
    rot_lhsT = np.ascontiguousarray(R.T).astype(ml_dtypes.bfloat16)
    a = np.arange(0, HD, 2, dtype=np.float32)
    inv_freq = (np.float32(1.0)
                / np.power(np.float32(10000.0), a / np.float32(HD))).astype(np.float32)
    pos = np.arange(T, dtype=np.float32)
    freqs = pos[None, :] * inv_freq[:, None]
    emb = np.concatenate([freqs, freqs], axis=0)
    return dict(ident=ident, ocb=ones_col_bf, orb=ones_row_bf, orf=ones_row_f,
                ocf=ones_col_f, rot=rot_lhsT,
                magic=np.full((128, 1), MAGIC, dtype=np.float32),
                nmagic=np.full((128, 1), -MAGIC, dtype=np.float32),
                cos=np.cos(emb).astype(np.float32),
                sin=np.sin(emb).astype(np.float32))


class Prog:
    def __init__(self, nc):
        self.nc = nc
        self.ops = []
        self.cnt = {}
        self.sems = {}
        self.banks = {}

    def sem(self, name):
        if name not in self.sems:
            self.sems[name] = self.nc.alloc_semaphore(name)
            self.cnt[name] = 0
        return self.sems[name]

    def c(self, name):
        self.sem(name)
        return self.cnt[name]

    def _add(self, eng, fn, waits, inc, mode):
        ws = []
        for s, v in waits:
            self.sem(s)
            if v > 0:
                ws.append((s, v))
        if inc is not None:
            self.sem(inc[0])
            self.cnt[inc[0]] += inc[1]
        self.ops.append((eng, ws, fn, inc, mode))

    def pe(self, fn, waits=(), sig=1):
        self._add("pe", fn, waits, ("smpe", sig) if sig else None, "instr")
        return self.cnt["smpe"]

    def act(self, fn, waits=(), sig=1):
        self._add("act", fn, waits, ("smact", sig) if sig else None, "instr")
        return self.cnt["smact"]

    def dve(self, fn, waits=(), sig=1, mode="drain"):
        self._add("dve", fn, waits, ("smdve", sig) if sig else None, mode)
        return self.cnt["smdve"]

    def pool(self, fn, waits=(), sig=1):
        self._add("pool", fn, waits, ("smpool", sig) if sig else None, "drain")
        return self.cnt["smpool"]

    def barrier(self, eng, waits):
        self._add(eng, lambda e: e.nop(), waits, None, "none")

    def dma(self, queue, sem_name, out_ap, in_ap, waits=(), transpose=False):
        def fn(eng, out_ap=out_ap, in_ap=in_ap, transpose=transpose):
            return eng.dma_start(out=out_ap, in_=in_ap, transpose=transpose)
        self._add(queue, fn, waits, (sem_name, 16), "instr")
        return self.cnt[sem_name]

    def bank_w(self, *names):
        ws = []
        for n in names:
            ws.extend(self.banks.get(n, []))
        return ws

    def bank_r(self, names, entries):
        for n in names:
            self.banks[n] = list(entries)

    def emit(self):
        waited = {}

        def body(eng_name):
            def run(eng):
                for op_eng, ws, fn, inc, mode in self.ops:
                    if op_eng != eng_name:
                        continue
                    for s, v in ws:
                        key = (eng_name, s)
                        if waited.get(key, 0) >= v:
                            continue
                        waited[key] = v
                        eng.wait_ge(self.sems[s], v)
                    ins = fn(eng)
                    if inc is not None:
                        s, n = inc
                        if mode == "instr":
                            assert ins is not None
                            ins.then_inc(self.sems[s], n)
                        else:
                            eng.drain()
                            eng.nop().then_inc(self.sems[s], n)
            return run
        return body


def build_program(dump=None):
    nc = bass.Bass("TRN2", target_bir_lowering=False, debug=False,
                   detect_race_conditions=DETECT_RACES)
    consts = _host_consts()

    x_in = nc.dram_tensor("x", [NSEQ, T, D], F32, kind="ExternalInput").ap()
    win = {}
    for nm, shp in (("wq", [D, D]), ("wk", [D, D]), ("wv", [D, D]), ("wo", [D, D]),
                    ("w1", [FH, D]), ("w2", [FH, D]), ("w3", [D, FH])):
        win[nm] = nc.dram_tensor(nm, shp, F32, kind="ExternalInput").ap()
    bu = nc.dram_tensor("bu", [1, H], F32, kind="ExternalInput").ap()
    y_out = nc.dram_tensor("y", [NSEQ, T, D], F32, kind="ExternalOutput").ap()
    dbg = None
    if dump is not None:
        dbg = nc.dram_tensor("dbg", [4096, 4096], F32, kind="ExternalOutput").ap()

    x_bf = nc.dram_tensor("x_bf", [NSEQ, T, D], BF16, kind="Internal").ap()
    aw_bf = {n: nc.dram_tensor(f"{n}_bf", [D, D], BF16, kind="Internal").ap()
             for n in ("wq", "wk", "wv", "wo")}
    q_bf = {0: nc.dram_tensor("q1_bf", [FH, D], BF16, kind="Internal").ap(),
            1: nc.dram_tensor("q2_bf", [FH, D], BF16, kind="Internal").ap(),
            2: nc.dram_tensor("q3_bf", [D, FH], BF16, kind="Internal").ap()}
    at_sp = nc.dram_tensor("at_sp", [NC, 128, T], BF16, kind="Internal").ap()
    awT_img = {w: nc.dram_tensor(f"awT_{w}", [H, 128, D], BF16, kind="Internal").ap()
               for w in ("wq", "wk", "wv")}
    woT_img = nc.dram_tensor("woT_img", [8, 128, 4096], BF16, kind="Internal").ap()
    q12T_img = [nc.dram_tensor(f"q{i}T_img", [NHH, 128, D], BF16,
                               kind="Internal").ap() for i in (1, 2)]
    q3T_img = nc.dram_tensor("q3T_img", [NC, 128, NHH * 128], BF16,
                             kind="Internal").ap()
    x2_sp = nc.dram_tensor("x2_sp", [NSEQ, NC, 128, T], BF16, kind="Internal").ap()

    cmap = {nm: nc.inline_tensor(arr, name=f"c_{nm}").ap()
            for nm, arr in consts.items()}

    SBUF_LIMIT = nc.sbuf_top
    cur = [nc.sbuf_base]

    def at(name, shape, dtype, base=None):
        bpe = 2 if dtype == BF16 else 4
        ptr = cur if base is None else base
        o = (ptr[0] + 31) & ~31
        t = nc.alloc_sbuf_tensor_at(name, list(shape), dtype, offset=o)
        ptr[0] = o + int(np.prod(shape[1:])) * bpe
        return t

    def multi(name, n, shape, dtype, base=None):
        return [at(f"{name}{i}", shape, dtype, base=base) for i in range(n)]

    ident_s = at("ident_s", [128, 128], BF16)
    ocb_s = at("ocb_s", [128, 1], BF16)
    orb_s = at("orb_s", [1, 128], BF16)
    orf_s = at("orf_s", [1, 128], F32)
    ocf_s = at("ocf_s", [128, 1], F32)
    rot_s = at("rot_s", [128, 128], BF16)
    bu_s = at("bu_s", [1, H], F32)
    rsc_s = multi("rsc", 3, [128, 1], F32)
    gacc_s = at("gacc_s", [1, 1024], F32)
    ve_s = at("ve_s", [128, 4], F32)
    vebf_s = at("vebf_s", [128, 1], BF16)
    pdbg_s = at("pdbg_s", [1, 48], F32)
    magic_s = at("magic_s", [128, 1], F32)
    nmagic_s = at("nmagic_s", [128, 1], F32)

    attn_base = cur[0]
    cos_s = at("cos_s", [128, T], F32)
    sin_s = at("sin_s", [128, T], F32)
    X_s = multi("X", NC, [128, T], BF16)
    Qh_s = at("Qh_s", [128, T], BF16)
    Kh_s = at("Kh_s", [128, T], BF16)
    Vh_s = at("Vh_s", [128, T], BF16)
    qpre_s = multi("qpre", 2, [128, 512], BF16)
    t1_s = at("t1_s", [128, 512], F32)
    t2_s = at("t2_s", [128, 512], F32)
    eT_s = at("eT_s", [128, KT * 512], BF16)
    vtm_s = at("vtm", [128, KT * 128], BF16)
    shb_s = at("shb_s", [128, 512], BF16)
    wT_s = [multi(f"wT{p}_", 3, [128, D], BF16) for p in range(2)]
    atv_s = multi("atv", 2, [128, 512], BF16)
    atb_s = multi("atb", NC, [128, 512], BF16)
    woT_s = [at(f"woT{p}", [128, 4096], BF16) for p in range(2)]
    sq_s = multi("sq", 2, [128, 512], BF16)
    srow_s = at("srow_s", [1, 512], F32)
    qrow_s = at("qrow_s", [1, 512], F32)
    mb_s = at("mb_s", [128, 512], F32)
    rb_s = at("rb_s", [128, 512], F32)
    attn_end = cur[0]

    pw_s = multi("pw", 2, [128, PCOLS], F32)
    pt_s = at("pt_s", [128, PCOLS], F32)
    pq_s = at("pq_s", [128, PCOLS], BF16)
    pqt_s = multi("pqt", 2, [128, 1024], BF16)
    pabs_s = at("pabs_s", [128, PCOLS], F32)
    assert cur[0] <= SBUF_LIMIT, f"attn+prep over: {cur[0]}"

    fb = [attn_base]
    x2b_s = [multi(f"x2b{s}_", NC, [128, 512], BF16, base=fb) for s in range(NSEQ)]
    h_s = [multi(f"h{s}_", NHH, [128, 512], BF16, base=fb) for s in range(NSEQ)]
    qwT_s = [[at(f"qwT{p}{w}", [128, D], BF16, base=fb) for w in range(2)]
             for p in range(2)]
    q3T_s = [at(f"q3T{p}", [128, NHH * 128], BF16, base=fb) for p in range(2)]
    r2f_s = multi("r2f", 2, [128, 512], BF16, base=fb)
    r2t_s = multi("r2t", NT, [128, D], BF16, base=fb)
    sg_s = multi("sg", 2, [128, 512], BF16, base=fb)
    ost_s = multi("ost", 2, [128, D], F32, base=fb)
    l2v_s = at("l2v", [128, 8], F32, base=fb)
    assert fb[0] <= SBUF_LIMIT, f"ffn over: {fb[0]}"

    psA = nc.alloc_psum_tensor("psA", [128, 1024], F32)
    psB = nc.alloc_psum_tensor("psB", [128, 1024], F32)
    psC = nc.alloc_psum_tensor("psC", [128, 512], F32)
    psE = nc.alloc_psum_tensor("psE", [128, 512], F32)
    psF = nc.alloc_psum_tensor("psF", [128, 512], F32)
    psG = nc.alloc_psum_tensor("psG", [128, 1024], BF16)

    P = Prog(nc)
    c = P.c
    gslot = [[] for _ in range(8)]

    for nm, dst in (("ident", ident_s), ("ocb", ocb_s), ("rot", rot_s),
                    ("cos", cos_s), ("sin", sin_s)):
        P.dma("sp", "smc", dst[:, :], cmap[nm])
    P.dma("sp", "smc", orb_s[0:1, :], cmap["orb"])
    P.dma("sp", "smc", orf_s[0:1, :], cmap["orf"])
    P.dma("sp", "smc", ocf_s[:, :], cmap["ocf"])
    P.dma("sp", "smc", magic_s[:, :], cmap["magic"])
    P.dma("sp", "smc", nmagic_s[:, :], cmap["nmagic"])
    N_CONST = c("smc")
    P.dma("pool", "smc2", bu_s[0:1, :], bu)

    for s in range(NSEQ):
        P.dma("pool", f"smxc{s}", x_bf[s], x_in[s])
    for n in ("wq", "wk", "wv", "wo"):
        P.dma("pool", "smwc", aw_bf[n], win[n])

    P.dve(lambda eng: eng.memset(eT_s[64:128, 512:1024], 0.0))

    prep_steps = []
    ps = {"pw_free": [[("smc2", 16)], [("smc2", 16)]], "pt_free": 0, "pq_free": 0,
          "pabs_free": 0}

    def gamma_chunk(widx, w_ap, row0, col0, ncols, par, first):
        def step():
            nd = P.dma("pool", f"smqf{par}", pw_s[par][:, 0:ncols],
                       w_ap[row0:row0 + 128, col0:col0 + ncols],
                       waits=ps["pw_free"][par])

            def fn2(eng, par=par, ncols=ncols, first=first):
                eng.tensor_reduce(ve_s[:, 3:4], pw_s[par][:, 0:ncols],
                                  axis=AX.X, op=ALU.add, apply_absolute_value=True)
                eng.drain()
                if first:
                    return eng.tensor_copy(pabs_s[:, 0:1], ve_s[:, 3:4])
                return eng.tensor_add(pabs_s[:, 0:1], pabs_s[:, 0:1], ve_s[:, 3:4])
            dv = P.dve(fn2, waits=[(f"smqf{par}", nd)] +
                       ([("smpe", c("smpe"))] if first else []))
            ps["pw_free"][par] = [("smdve", dv)]
        return step

    def gamma_fin(widx, count):
        def step():
            dv0 = c("smdve")

            def mm0(eng):
                return eng.matmul(psC[0:1, 0:1], pabs_s[:, 0:1], ocf_s[:, 0:1],
                                  start=True, stop=True)
            pv0 = P.pe(mm0, waits=[("smdve", dv0)] + P.bank_w("C"))

            def fin(eng, count=count):
                eng.tensor_scalar(ve_s[0:1, 0:1], psC[0:1, 0:1],
                                  1.0 / count, 1e-8, op0=ALU.mult, op1=ALU.add)
                eng.drain()
                return eng.reciprocal(ve_s[0:1, 1:2], ve_s[0:1, 0:1])
            dv = P.dve(fin, waits=[("smpe", pv0)])
            P.bank_r(["C"], [("smdve", dv)])

            def bc(eng):
                return eng.matmul(psC[:, 0:1], orf_s[0:1, :], ve_s[0:1, 1:2],
                                  start=True, stop=True)
            pv2 = P.pe(bc, waits=[("smdve", dv)] + P.bank_w("C"))

            def ev(eng, widx=widx):
                return eng.tensor_copy(rsc_s[widx][:, 0:1], psC[:, 0:1])
            dv2 = P.dve(ev, waits=[("smpe", pv2)])
            P.bank_r(["C"], [("smdve", dv2)])
        return step

    def quant_chunk(widx, w_ap, q_ap, row0, col0, ncols, par):
        def step():
            nd = P.dma("pool", f"smqf{par}", pw_s[par][:, 0:ncols],
                       w_ap[row0:row0 + 128, col0:col0 + ncols],
                       waits=ps["pw_free"][par])

            def qf(eng, par=par, ncols=ncols, widx=widx):
                return eng.tensor_scalar(pt_s[:, 0:ncols], pw_s[par][:, 0:ncols],
                                         rsc_s[widx][:, 0:1], MAGIC,
                                         op0=ALU.mult, op1=ALU.add)
            dv = P.pool(qf, waits=[(f"smqf{par}", nd), ("smdve", c("smdve")),
                                   ("smact", ps["pt_free"])])
            ps["pw_free"][par] = [("smpool", dv)]

            def sg(eng, ncols=ncols):
                return eng.activation(pq_s[:, 0:ncols], pt_s[:, 0:ncols],
                                      AF.Sign, bias=nmagic_s[:, 0:1], scale=1.0)
            av = P.act(sg, waits=[("smpool", dv), ("smqs", ps["pq_free"]),
                                  ("smc", N_CONST)])
            ps["pt_free"] = av
            ps["pq_free"] = P.dma("pool", "smqs",
                                  q_ap[row0:row0 + 128, col0:col0 + ncols],
                                  pq_s[:, 0:ncols], waits=[("smact", av)])
        return step

    def chunks_of(total, width):
        out = []
        o = 0
        while o < total:
            out.append((o, min(width, total - o)))
            o += width
        return out

    wshapes = {0: (win["w1"], FH, D), 1: (win["w2"], FH, D), 2: (win["w3"], D, FH)}
    for widx in (0, 1, 2):
        w_ap, rows, cols = wshapes[widx]
        first = True
        for i in range(rows // 128):
            for (c0, cn) in chunks_of(cols, PCOLS):
                prep_steps.append(gamma_chunk(widx, w_ap, i * 128, c0, cn,
                                              len(prep_steps) % 2, first))
                first = False
        prep_steps.append(gamma_fin(widx, rows * cols))
    for widx in (0, 1, 2):
        w_ap, rows, cols = wshapes[widx]
        for i in range(rows // 128):
            for (c0, cn) in chunks_of(cols, PCOLS):
                prep_steps.append(quant_chunk(widx, w_ap, q_bf[widx], i * 128,
                                              c0, cn, len(prep_steps) % 2))
        def mk_thop(widx=widx, rows=rows, cols=cols):
            def step():
                gate = [("smqs", c("smqs"))]
                if widx < 2:
                    for cc in range(NC):
                        for r0, nr in chunks_of(rows, 1024):
                            dst = q12T_img[widx][r0 // 128:(r0 + nr) // 128, 0:128,
                                                 cc * 128:(cc + 1) * 128].rearrange(
                                                     "h p f -> p h f")
                            thop_pair(q_bf[widx][r0:r0 + nr,
                                                 cc * 128:(cc + 1) * 128],
                                      dst, gate)
                else:
                    for hhc in range(NHH):
                        for r0, nr in chunks_of(D, 1024):
                            dst = q3T_img[r0 // 128:(r0 + nr) // 128, 0:128,
                                          hhc * 128:(hhc + 1) * 128].rearrange(
                                              "d p f -> p d f")
                            thop_pair(q_bf[2][r0:r0 + nr,
                                              hhc * 128:(hhc + 1) * 128],
                                      dst, gate)
            return step
        prep_steps.append(mk_thop())
    prep_iter = iter(prep_steps)
    PREP_PER_SLOT = max(1, (len(prep_steps) + NSEQ * H - 1) // (NSEQ * H))

    th_state = {"par": 0}

    def thop_pair(src_ap, img_dst, gate_waits):
        par = th_state["par"]
        th_state["par"] ^= 1
        nr = src_ap.shape[0]
        P.dma("sp", f"smth{par}", pqt_s[par][:, 0:nr], src_ap,
              waits=gate_waits + [(f"smtw{par}", c(f"smtw{par}"))], transpose=True)
        P.dma("pool", f"smtw{par}", img_dst, pqt_s[par][:, 0:nr],
              waits=[(f"smth{par}", c(f"smth{par}"))])

    def sched_attn_thop():
        gates = []
        for h0 in (0, 8):
            for w in ("wq", "wk", "wv"):
                for cc in range(NC):
                    dst = awT_img[w][h0:h0 + 8, 0:128,
                                     cc * 128:(cc + 1) * 128].rearrange(
                                         "h p f -> p h f")
                    thop_pair(aw_bf[w][h0 * 128:(h0 + 8) * 128,
                                       cc * 128:(cc + 1) * 128],
                              dst, [("smwc", 64)])
            gates.append([("smtw0", c("smtw0")), ("smtw1", c("smtw1"))])
        for g0 in (0, 4):
            for cc in range(NC):
                dst = woT_img[g0:g0 + 4, 0:128,
                              cc * 256:(cc + 1) * 256].rearrange("g p f -> p g f")
                thop_pair(aw_bf["wo"][g0 * 256:g0 * 256 + 1024,
                                      cc * 128:(cc + 1) * 128],
                          dst, [("smwc", 64)])
        gates.append([("smtw0", c("smtw0")), ("smtw1", c("smtw1"))])
        return gates

    st = {"wt_free": [[], []], "wt_have": [0, 0], "x_have": 0,
          "qpre_free": [0, 0], "t1_free": 0, "t2_free": 0,
          "Qh_free": 0, "Kh_free": 0, "Vh_free": 0, "eT_free": 0,
          "qk_ready": 0, "v_ready": 0,
          "shb_free": 0, "vtm_free": 0, "atv_free": [0, 0],
          "sq_free": [0, 0], "woT_free": [[], []], "atb_free": []}

    def load_X(s, extra=()):
        v = 0
        for cc in range(NC):
            v = P.dma("sp", "smxt", X_s[cc][:, :],
                      x_bf[s][0:T, cc * 128:(cc + 1) * 128],
                      waits=[(f"smxc{s}", 16), ("smc", N_CONST)] + list(extra),
                      transpose=True)
        return v

    def load_wT(h, par):
        gate = st["awT_gate"][0 if h < 8 else 1]
        for wi, name in enumerate(("wq", "wk", "wv")):
            P.dma("sp", f"smwt{par}", wT_s[par][wi][:, :], awT_img[name][h],
                  waits=gate + st["wt_free"][par])
        return c(f"smwt{par}")

    def proj_head(s, h, par, only=None):
        nw = st["wt_have"][par]
        nx = st["x_have"]
        for wi, name in enumerate(("q", "k", "v")[:only] if only else ("q", "k", "v")):
            for tc in range(NT):
                bank = ("E", "F")[tc % 2]
                pj = (psE, psF)[tc % 2]
                sl = slice(tc * 512, (tc + 1) * 512)

                def mk_mm(eng, wi=wi, pj=pj, sl=sl, par=par):
                    ins = None
                    for cc in range(NC):
                        ins = eng.matmul(pj[:, 0:512],
                                         wT_s[par][wi][:, cc * 128:(cc + 1) * 128],
                                         X_s[cc][:, sl],
                                         start=(cc == 0), stop=(cc == NC - 1))
                    return ins
                mmv = P.pe(mk_mm, waits=[(f"smwt{par}", nw), ("smxt", nx)]
                           + P.bank_w(bank))
                if name == "v":
                    def ev(eng, pj=pj, sl=sl):
                        return eng.copy(Vh_s[:, sl], pj[:, 0:512])
                    av = P.act(ev, waits=[("smpe", mmv), ("smpe", st["Vh_free"])])
                    P.bank_r([bank], [("smact", av)])
                else:
                    dst, dfree = ((Qh_s, "Qh_free") if name == "q"
                                  else (Kh_s, "Kh_free"))
                    qp = tc % 2

                    def ev(eng, pj=pj, qp=qp):
                        return eng.copy(qpre_s[qp][:, :], pj[:, 0:512])
                    av = P.act(ev, waits=[("smpe", mmv),
                                          ("smpe", st["qpre_free"][qp])])
                    P.bank_r([bank], [("smact", av)])

                    def mt1(eng, qp=qp, sl=sl):
                        return eng.tensor_mul(t1_s[:, :], qpre_s[qp][:, :],
                                              cos_s[:, sl])
                    dv1 = P.dve(mt1, waits=[("smact", av), ("smdve", st["t1_free"])])

                    def mrot(eng, qp=qp):
                        return eng.matmul(psC[:, 0:512], rot_s[:, :],
                                          qpre_s[qp][:, :], start=True, stop=True)
                    mmr = P.pe(mrot, waits=[("smact", av)] + P.bank_w("C"))
                    st["qpre_free"][qp] = mmr

                    def mt2(eng, sl=sl):
                        return eng.tensor_mul(t2_s[:, :], psC[:, 0:512], sin_s[:, sl])
                    dv2 = P.dve(mt2, waits=[("smpe", mmr), ("smdve", st["t2_free"])])
                    P.bank_r(["C"], [("smdve", dv2)])

                    def madd(eng, dst=dst, sl=sl):
                        return eng.tensor_add(dst[:, sl], t1_s[:, :], t2_s[:, :])
                    dv3 = P.dve(madd, waits=[("smpe", st[dfree])])
                    st["t1_free"] = dv3
                    st["t2_free"] = dv3
        st["wt_free"][par] = [("smpe", c("smpe"))]

    def sT_exp(qg):
        qsl = slice(qg * 512, (qg + 1) * 512)
        for p in range(KT // 2):
            pp = (psA, psB)[p % 2]
            bks = ("A0", "A1") if p % 2 == 0 else ("B0", "B1")

            def mm(eng, p=p, pp=pp, qsl=qsl):
                ins = None
                for jj in range(2):
                    j = 2 * p + jj
                    ins = eng.matmul(pp[:, jj * 512:(jj + 1) * 512],
                                     Kh_s[:, _kstart(j):_kstart(j) + 128],
                                     Qh_s[:, qsl], start=True, stop=True)
                return ins
            mmv = P.pe(mm, waits=[("smdve", st["qk_ready"])] + P.bank_w(*bks))
            if p == 0:
                def ex(eng):
                    eng.activation(eT_s[:, 0:512], psA[:, 0:512], AF.Exp,
                                   bias=0.0, scale=INV_SQRT_HD)
                    return eng.activation(eT_s[0:76, 512:1024], psA[0:76, 512:1024],
                                          AF.Exp, bias=0.0, scale=INV_SQRT_HD)
            else:
                def ex(eng, p=p, pp=pp):
                    return eng.activation(eT_s[:, p * 1024:(p + 1) * 1024], pp[:, :],
                                          AF.Exp, bias=0.0, scale=INV_SQRT_HD)
            av = P.act(ex, waits=[("smpe", mmv), ("smpe", st["eT_free"])])
            P.bank_r(bks, [("smact", av)])
        return c("smact")

    def S_sum(exp_done):
        def mm(eng):
            ins = None
            for j in range(KT):
                ins = eng.matmul(psC[0:1, 0:512], ocb_s[:, 0:1],
                                 eT_s[:, j * 512:(j + 1) * 512],
                                 start=(j == 0), stop=(j == KT - 1))
            return ins
        return P.pe(mm, waits=[("smact", exp_done)] + P.bank_w("C"))

    def shb_make(sv):
        def rc(eng):
            with nc.allow_low_precision(reason="1/S in bf16; probs are bf16 anyway"):
                return eng.reciprocal(shb_s[0:1, 0:512], psC[0:1, 0:512])
        dv = P.dve(rc, waits=[("smpe", sv), ("smdve", st["shb_free"])])
        P.bank_r(["C"], [("smdve", dv)])

        def bc(eng):
            return eng.matmul(psC[:, 0:512], orb_s[0:1, :], shb_s[0:1, 0:512],
                              start=True, stop=True)
        pv = P.pe(bc, waits=[("smdve", dv)] + P.bank_w("C"))

        def ev(eng):
            return eng.tensor_copy(shb_s[:, :], psC[:, 0:512])
        dv2 = P.dve(ev, waits=[("smpe", pv)])
        P.bank_r(["C"], [("smdve", dv2)])
        return dv2

    def pv_evac(s, h, qg, exp_done, shb_done, vtm_done):
        bank = ("E", "F")[qg % 2]
        pv_ = (psE, psF)[qg % 2]

        def mm(eng, pv_=pv_):
            ins = None
            for j in range(KT):
                ins = eng.matmul(pv_[:, 0:512], vtm_s[:, j * 128:(j + 1) * 128],
                                 eT_s[:, j * 512:(j + 1) * 512],
                                 start=(j == 0), stop=(j == KT - 1))
            return ins
        mv = P.pe(mm, waits=[("smact", exp_done), ("smdve", shb_done),
                             ("smact", vtm_done)] + P.bank_w(bank))
        ap = qg % 2

        def ev(eng, pv_=pv_, ap=ap):
            return eng.tensor_mul(atv_s[ap][:, :], pv_[:, 0:512], shb_s[:, :])
        dv = P.dve(ev, waits=[("smpe", mv), (f"smat{ap}", st["atv_free"][ap])])
        P.bank_r([bank], [("smdve", dv)])
        st["eT_free"] = mv
        st["shb_free"] = dv
        st["atv_free"][ap] = P.dma("act", f"smat{ap}",
                                   at_sp[h][:, qg * 512:(qg + 1) * 512],
                                   atv_s[ap][:, :], waits=[("smdve", dv)])
        return dv

    def prepass_vupdate(h, exp_done, sv):
        def asum(eng):
            ins = None
            for j in range(2, KT):
                ins = eng.matmul(psE[0:1, 0:1],
                                 eT_s[:, j * 512 + 511:j * 512 + 512],
                                 ocb_s[:, 0:1], start=(j == 2), stop=(j == KT - 1))
            return ins
        pa = P.pe(asum, waits=[("smact", exp_done)] + P.bank_w("E"))

        def pmath(eng, h=h):
            eng.tensor_copy(ve_s[0:1, 0:1], psE[0:1, 0:1])
            eng.tensor_copy(ve_s[0:1, 1:2], eT_s[0:1, 2 * 512 + 511:2 * 512 + 512])
            eng.tensor_scalar_mul(ve_s[0:1, 2:3], psC[0:1, 511:512], 1e-6)
            eng.drain()
            eng.tensor_sub(ve_s[0:1, 0:1], ve_s[0:1, 0:1], ve_s[0:1, 1:2])
            eng.drain()
            eng.tensor_scalar_mul(ve_s[0:1, 0:1], ve_s[0:1, 0:1], 1.0 / 1023.0)
            eng.drain()
            eng.tensor_add(ve_s[0:1, 0:1], ve_s[0:1, 0:1], ve_s[0:1, 2:3])
            eng.drain()
            eng.reciprocal(ve_s[0:1, 0:1], ve_s[0:1, 0:1])
            eng.drain()
            eng.tensor_mul(ve_s[0:1, 0:1], ve_s[0:1, 1:2], ve_s[0:1, 0:1])
            eng.drain()
            eng.tensor_copy(pdbg_s[0:1, h:h + 1], ve_s[0:1, 0:1])
            eng.tensor_tensor(ve_s[0:1, 1:2], bu_s[0:1, h:h + 1], ve_s[0:1, 0:1],
                              op=ALU.is_lt)
            eng.drain()
            eng.tensor_copy(pdbg_s[0:1, 16 + h:17 + h], ve_s[0:1, 1:2])
            return eng.tensor_scalar_mul(ve_s[0:1, 1:2], ve_s[0:1, 1:2], 1.0 / 1024.0)
        dv = P.dve(pmath, waits=[("smpe", pa), ("smpe", sv), ("smc2", 16)])
        P.bank_r(["E"], [("smdve", dv)])
        P.bank_r(["C"], [("smdve", dv)])

        def bcf(eng):
            return eng.matmul(psE[:, 0:1], orf_s[0:1, :], ve_s[0:1, 1:2],
                              start=True, stop=True)
        pb = P.pe(bcf, waits=[("smdve", dv)] + P.bank_w("E"))

        def vupd(eng):
            eng.tensor_copy(ve_s[:, 0:1], psE[:, 0:1])
            eng.tensor_copy(ve_s[:, 1:2], Vh_s[:, 1024:1025])
            eng.drain()
            eng.tensor_mul(ve_s[:, 0:1], ve_s[:, 0:1], ve_s[:, 1:2])
            eng.drain()
            return eng.tensor_scalar_add(Vh_s[:, 1025:2048], Vh_s[:, 1025:2048],
                                         ve_s[:, 0:1])
        dv2 = P.dve(vupd, waits=[("smpe", pb), ("smact", st["v_ready"])])
        P.bank_r(["E"], [("smdve", dv2)])
        return dv2

    def v_transposes(vupd_done):
        for g0, g1 in ((0, 8), (8, KT)):
            tv = 0
            for j in range(g0, g1):
                slot = j - g0

                def tr(eng, slot=slot, j=j):
                    return eng.transpose(psG[:, slot * 128:(slot + 1) * 128],
                                         Vh_s[:, _kstart(j):_kstart(j) + 128],
                                         ident_s[:, :])
                w = [("smdve", vupd_done)] + (P.bank_w("G") if j == g0 else [])
                tv = P.pe(tr, waits=w)

            def ev(eng, g0=g0, g1=g1):
                return eng.copy(vtm_s[:, g0 * 128:g1 * 128],
                                psG[:, 0:(g1 - g0) * 128])
            av = P.act(ev, waits=[("smpe", tv), ("smpe", st["vtm_free"])])
            P.bank_r(["G"], [("smact", av)])
        return c("smact")

    def attn_head(s, h):
        par = h % 2
        proj_head(s, h, par)
        st["qk_ready"] = c("smdve")
        st["v_ready"] = c("smact")
        nh, npar = h + 1, (h + 1) % 2
        if nh < H:
            st["wt_have"][npar] = load_wT(nh, npar)
        elif s + 1 < NSEQ:
            st["wt_have"][0] = load_wT(0, 0)
        ed3 = sT_exp(3)
        sv3 = S_sum(ed3)
        vu = prepass_vupdate(h, ed3, sv3)
        vt = v_transposes(vu)
        st["Vh_free"] = c("smpe")
        sh3 = shb_make(sv3)
        pv_evac(s, h, 3, ed3, sh3, vt)
        for qg in (0, 1, 2):
            ed = sT_exp(qg)
            sv = S_sum(ed)
            sh = shb_make(sv)
            pv_evac(s, h, qg, ed, sh, vt)
        st["Qh_free"] = c("smpe")
        st["Kh_free"] = c("smpe")
        st["vtm_free"] = c("smpe")

    def oproj_ln1(s):
        nat0, nat1 = c("smat0"), c("smat1")
        for tc in range(NT):
            sl = slice(tc * 512, (tc + 1) * 512)
            for cc in range(NC):
                P.dma("sp", "smatb", atb_s[cc][:, :], at_sp[cc][:, sl],
                      waits=[("smat0", nat0), ("smat1", nat1)] + st["atb_free"])
            natb = c("smatb")
            for g in range(8):
                gpar = g % 2
                P.dma("sp", f"smwo{gpar}", woT_s[gpar][:, :], woT_img[g],
                      waits=st["awT_gate"][2] + st["woT_free"][gpar])
                nwo = c(f"smwo{gpar}")
                for mi in range(2):
                    m = g * 2 + mi
                    bank = ("E", "F")[m % 2]
                    pj = (psE, psF)[m % 2]

                    def mm(eng, mi=mi, pj=pj, gpar=gpar):
                        ins = None
                        for cc in range(NC):
                            ins = eng.matmul(
                                pj[:, 0:512],
                                woT_s[gpar][:, cc * 256 + mi * 128:
                                            cc * 256 + (mi + 1) * 128],
                                atb_s[cc][:, :],
                                start=(cc == 0), stop=(cc == NC - 1))
                        return ins
                    mv = P.pe(mm, waits=[(f"smwo{gpar}", nwo), ("smatb", natb)]
                              + P.bank_w(bank))

                    def ev(eng, m=m, pj=pj, sl=sl):
                        return eng.tensor_add(X_s[m][:, sl], pj[:, 0:512],
                                              X_s[m][:, sl])
                    dv = P.dve(ev, waits=[("smpe", mv)], mode="instr")
                    P.bank_r([bank], [("smdve", dv)])
                st["woT_free"][gpar] = [("smpe", c("smpe"))]
            rX = P.dve(lambda e: e.engine_nop())
            def mms(eng, sl=sl):
                ins = None
                for m in range(NC):
                    ins = eng.matmul(psC[0:1, 0:512], ocb_s[:, 0:1], X_s[m][:, sl],
                                     start=(m == 0), stop=(m == NC - 1))
                return ins
            sv = P.pe(mms, waits=[("smdve", rX)] + P.bank_w("C"))

            def evs(eng):
                return eng.tensor_copy(srow_s[0:1, :], psC[0:1, 0:512])
            dvs = P.dve(evs, waits=[("smpe", sv)])
            P.bank_r(["C"], [("smdve", dvs)])
            for m in range(NC):
                def sq(eng, m=m, sl=sl):
                    return eng.activation(sq_s[m % 2][:, :], X_s[m][:, sl], AF.Square)
                av = P.act(sq, waits=[("smdve", rX), ("smpe", st["sq_free"][m % 2])])

                def mmq(eng, m=m):
                    return eng.matmul(psE[0:1, 0:512], ocb_s[:, 0:1],
                                      sq_s[m % 2][:, :], start=(m == 0),
                                      stop=(m == NC - 1))
                st["sq_free"][m % 2] = P.pe(
                    mmq, waits=[("smact", av)] + (P.bank_w("E") if m == 0 else []))
            qv = c("smpe")

            def evq(eng):
                return eng.tensor_copy(qrow_s[0:1, :], psE[0:1, 0:512])
            dvq = P.dve(evq, waits=[("smpe", qv)])
            P.bank_r(["E"], [("smdve", dvq)])

            def rows(eng):
                eng.tensor_scalar_mul(srow_s[0:1, :], srow_s[0:1, :], 1.0 / D)
                eng.tensor_scalar(qrow_s[0:1, :], qrow_s[0:1, :], 1.0 / D, 1e-5,
                                  op0=ALU.mult, op1=ALU.add)
                eng.drain()
                eng.tensor_mul(t1_s[0:1, 0:512], srow_s[0:1, :], srow_s[0:1, :])
                eng.drain()
                return eng.tensor_sub(qrow_s[0:1, :], qrow_s[0:1, :],
                                      t1_s[0:1, 0:512])
            rv = P.dve(rows, waits=[("smdve", st["t1_free"])])
            st["t1_free"] = rv

            def rsq(eng):
                return eng.activation(qrow_s[0:1, :], qrow_s[0:1, :], AF.Sqrt)
            av2 = P.act(rsq, waits=[("smdve", rv)])

            def rrec(eng):
                return eng.reciprocal(qrow_s[0:1, :], qrow_s[0:1, :])
            rrv = P.dve(rrec, waits=[("smact", av2)])

            def bcm(eng):
                return eng.matmul(psC[:, 0:512], orf_s[0:1, :], srow_s[0:1, :],
                                  start=True, stop=True)
            p1 = P.pe(bcm, waits=[("smdve", rrv)] + P.bank_w("C"))

            def bcr(eng):
                return eng.matmul(psE[:, 0:512], orf_s[0:1, :], qrow_s[0:1, :],
                                  start=True, stop=True)
            p2 = P.pe(bcr, waits=P.bank_w("E"))

            def evb(eng):
                eng.tensor_copy(mb_s[:, :], psC[:, 0:512])
                return eng.tensor_copy(rb_s[:, :], psE[:, 0:512])
            dvb = P.dve(evb, waits=[("smpe", p2)])
            P.bank_r(["C"], [("smdve", dvb)])
            P.bank_r(["E"], [("smdve", dvb)])
            def ap_(eng, sl=sl):
                ins = None
                for m in range(NC):
                    eng.tensor_sub(X_s[m][:, sl], X_s[m][:, sl], mb_s[:, :])
                    ins = eng.tensor_mul(X_s[m][:, sl], X_s[m][:, sl], rb_s[:, :])
                return ins
            P.dve(ap_, waits=[("smdve", dvb)])
            st["atb_free"] = [("smpe", c("smpe")), ("smdve", c("smdve"))]
        xv = c("smdve")
        for m in range(NC):
            P.dma("act", "smsp", x2_sp[s][m], X_s[m][:, :], waits=[("smdve", xv)])
        return c("smsp")

    fs = {"qw_free": [[], []], "qw_have": [0, 0], "q3_free": [[], []],
          "q3_have": [0, 0], "ost_free": [0, 0], "x2_free": [],
          "h_free": 0, "r2f_free": [0, 0], "r2t_free": 0, "sg_free": [0, 0]}

    def ffn_tb(tb, nq, nsp, ntw):
        for s in range(NSEQ):
            for m in range(NC):
                P.dma("pool", "smx2", x2b_s[s][m][:, :],
                      x2_sp[s][m][:, tb * 512:(tb + 1) * 512],
                      waits=[("smsp", nsp)] + fs["x2_free"])
        nx2 = c("smx2")

        def load_qw12(hhc, par):
            for wi in range(2):
                P.dma("sp", f"smqt{par}", qwT_s[par][wi][:, :], q12T_img[wi][hhc],
                      waits=ntw + fs["qw_free"][par])
            fs["qw_have"][par] = c(f"smqt{par}")
        load_qw12(0, 0)
        for hhc in range(NHH):
            par = hhc % 2
            have = fs["qw_have"][par]
            if hhc + 1 < NHH:
                load_qw12(hhc + 1, (hhc + 1) % 2)
            for s in range(NSEQ):
                pA2 = (psA, psB)[s]
                bks = ("A0", "A1") if s == 0 else ("B0", "B1")

                def mm1(eng, s=s, par=par, pA2=pA2):
                    ins = None
                    for cc in range(NC):
                        ins = eng.matmul(pA2[:, 0:512],
                                         qwT_s[par][0][:, cc * 128:(cc + 1) * 128],
                                         x2b_s[s][cc][:, :],
                                         start=(cc == 0), stop=(cc == NC - 1))
                    return ins
                P.pe(mm1, waits=[(f"smqt{par}", have), ("smx2", nx2)]
                     + P.bank_w(bks[0]))

                def mm2(eng, s=s, par=par, pA2=pA2):
                    ins = None
                    for cc in range(NC):
                        ins = eng.matmul(pA2[:, 512:1024],
                                         qwT_s[par][1][:, cc * 128:(cc + 1) * 128],
                                         x2b_s[s][cc][:, :],
                                         start=(cc == 0), stop=(cc == NC - 1))
                    return ins
                mv = P.pe(mm2, waits=P.bank_w(bks[1]))

                def sgm(eng, s=s, pA2=pA2):
                    return eng.activation(sg_s[s][:, :], pA2[:, 0:512], AF.Sigmoid)
                av = P.act(sgm, waits=[("smpe", mv), ("smdve", fs["sg_free"][s])])

                def hm(eng, s=s, hhc=hhc, pA2=pA2):
                    return eng.tensor_mul(h_s[s][hhc][:, :], sg_s[s][:, :],
                                          pA2[:, 512:1024])
                dv = P.dve(hm, waits=[("smact", av), ("smpe", fs["h_free"])])
                fs["sg_free"][s] = dv
                P.bank_r([bks[0]], [("smact", av)])
                P.bank_r([bks[1]], [("smdve", dv)])
            fs["qw_free"][par] = [("smpe", c("smpe"))]
        hv = c("smdve")

        def load_qw3(dc, par):
            P.dma("sp", f"smq3{par}", q3T_s[par][:, :], q3T_img[dc],
                  waits=ntw + fs["q3_free"][par])
            fs["q3_have"][par] = c(f"smq3{par}")
        for s in range(NSEQ):
            load_qw3(0, 0)
            for dc in range(NC):
                par = dc % 2
                have = fs["q3_have"][par]
                if dc + 1 < NC:
                    load_qw3(dc + 1, (dc + 1) % 2)
                bank = ("E", "F")[dc % 2]
                pv_ = (psE, psF)[dc % 2]

                def mm(eng, s=s, par=par, pv_=pv_):
                    ins = None
                    for hhc in range(NHH):
                        ins = eng.matmul(pv_[:, 0:512],
                                         q3T_s[par][:, hhc * 128:(hhc + 1) * 128],
                                         h_s[s][hhc][:, :],
                                         start=(hhc == 0), stop=(hhc == NHH - 1))
                    return ins
                mv = P.pe(mm, waits=[(f"smq3{par}", have), ("smdve", hv)]
                          + P.bank_w(bank))

                def ev(eng, s=s, dc=dc, pv_=pv_):
                    return eng.tensor_add(r2f_s[dc % 2][:, :], pv_[:, 0:512],
                                          x2b_s[s][dc][:, :])
                dv = P.dve(ev, waits=[("smpe", mv), ("smpe", fs["r2f_free"][dc % 2])])
                P.bank_r([bank], [("smdve", dv)])
                tv = 0
                for tc in range(4):
                    def tr(eng, dc=dc, tc=tc):
                        return eng.transpose(psG[:, tc * 128:(tc + 1) * 128],
                                             r2f_s[dc % 2][:, tc * 128:(tc + 1) * 128],
                                             ident_s[:, :])
                    w = [("smdve", dv)] + (P.bank_w("G") if tc == 0 else [])
                    tv = P.pe(tr, waits=w)
                av = 0
                for tc in range(4):
                    def tev(eng, dc=dc, tc=tc):
                        return eng.copy(r2t_s[tc][:, dc * 128:(dc + 1) * 128],
                                        psG[:, tc * 128:(tc + 1) * 128])
                    av = P.act(tev, waits=[("smpe", tv), ("smdve", fs["r2t_free"])])
                P.bank_r(["G"], [("smact", av)])
                fs["r2f_free"][dc % 2] = c("smpe")
                fs["q3_free"][par] = [("smpe", c("smpe"))]
            rT = c("smact")
            for tc in range(4):
                def st1(eng, tc=tc):
                    return eng.reduce_sum(l2v_s[:, 0:1], r2t_s[tc][:, :], axis=AX.X)
                P.dve(st1, waits=[("smact", rT)])

                def st2(eng, tc=tc):
                    ins = None
                    for piece in range(4):
                        sl2 = slice(piece * 512, (piece + 1) * 512)
                        ins = eng.activation(sg_s[0][:, :], r2t_s[tc][:, sl2],
                                             AF.Square,
                                             accum_out=l2v_s[:, 2 + piece:3 + piece])
                    return ins
                a2 = P.act(st2, waits=[("smact", rT), ("smdve", fs["sg_free"][0])])

                def vecs(eng):
                    eng.tensor_add(l2v_s[:, 1:2], l2v_s[:, 2:3], l2v_s[:, 3:4])
                    eng.tensor_add(l2v_s[:, 6:7], l2v_s[:, 4:5], l2v_s[:, 5:6])
                    eng.tensor_scalar_mul(l2v_s[:, 0:1], l2v_s[:, 0:1], 1.0 / D)
                    eng.drain()
                    eng.tensor_add(l2v_s[:, 1:2], l2v_s[:, 1:2], l2v_s[:, 6:7])
                    eng.tensor_mul(l2v_s[:, 2:3], l2v_s[:, 0:1], l2v_s[:, 0:1])
                    eng.drain()
                    eng.tensor_scalar_mul(l2v_s[:, 1:2], l2v_s[:, 1:2], 1.0 / D)
                    eng.drain()
                    eng.tensor_sub(l2v_s[:, 1:2], l2v_s[:, 1:2], l2v_s[:, 2:3])
                    eng.drain()
                    return eng.tensor_scalar_add(l2v_s[:, 1:2], l2v_s[:, 1:2], 1e-5)
                vv = P.dve(vecs, waits=[("smact", a2)])

                def sqr(eng):
                    return eng.activation(l2v_s[:, 1:2], l2v_s[:, 1:2], AF.Sqrt)
                a3 = P.act(sqr, waits=[("smdve", vv)])

                def rstd_bias(eng):
                    eng.reciprocal(l2v_s[:, 1:2], l2v_s[:, 1:2])
                    eng.drain()
                    eng.tensor_mul(l2v_s[:, 2:3], l2v_s[:, 0:1], l2v_s[:, 1:2])
                    eng.drain()
                    return eng.tensor_scalar_mul(l2v_s[:, 2:3], l2v_s[:, 2:3], -1.0)
                rb2 = P.dve(rstd_bias, waits=[("smact", a3)])

                def appl(eng, tc=tc):
                    return eng.activation(ost_s[tc % 2][:, :], r2t_s[tc][:, :],
                                          AF.Identity, bias=l2v_s[:, 2:3],
                                          scale=l2v_s[:, 1:2])
                ov = P.act(appl, waits=[("smdve", rb2),
                                        (f"smout{tc % 2}", fs["ost_free"][tc % 2])])
                fs["ost_free"][tc % 2] = P.dma(
                    "act", f"smout{tc % 2}",
                    y_out[s][tb * 512 + tc * 128: tb * 512 + (tc + 1) * 128, :],
                    ost_s[tc % 2][:, :], waits=[("smact", ov)])
            fs["r2t_free"] = c("smdve")
        fs["h_free"] = c("smpe")
        fs["x2_free"] = [("smpe", c("smpe")), ("smdve", c("smdve"))]

    def dump_rows(entries, waits):
        for ap_, r0, wdt in entries:
            P.dma("pool", "smdbg", dbg[r0:r0 + 128, 0:wdt], ap_, waits=waits)
        P.barrier("sp", [("smdbg", c("smdbg"))])

    st["awT_gate"] = sched_attn_thop()
    st["x_have"] = load_X(0)
    st["wt_have"][0] = load_wT(0, 0)

    stopped = False
    if dump in ("pq0", "pq1", "pq2"):
        level = {"pq0": 0, "pq1": 1, "pq2": 2}[dump]
        ntc = 2 if dump == "pq2" else NT
        nw = st["wt_have"][0]
        nx = st["x_have"]
        for tc in range(ntc):
            bank = ("E", "F")[tc % 2]
            pj = (psE, psF)[tc % 2]
            sl = slice(tc * 512, (tc + 1) * 512)
            def mk_mm(eng, pj=pj, sl=sl):
                ins = None
                for cc in range(NC):
                    ins = eng.matmul(pj[:, 0:512], wT_s[0][0][:, cc * 128:(cc + 1) * 128],
                                     X_s[cc][:, sl], start=(cc == 0), stop=(cc == NC - 1))
                return ins
            mmv = P.pe(mk_mm, waits=[("smwt0", nw), ("smxt", nx)] + P.bank_w(bank))
            qp = tc % 2
            def ev(eng, pj=pj, qp=qp):
                return eng.copy(qpre_s[qp][:, :], pj[:, 0:512])
            av = P.act(ev, waits=[("smpe", mmv), ("smpe", st["qpre_free"][qp])])
            P.bank_r([bank], [("smact", av)])
            if level >= 1:
                def mt1(eng, qp=qp, sl=sl):
                    return eng.tensor_mul(t1_s[:, :], qpre_s[qp][:, :], cos_s[:, sl])
                dv1 = P.dve(mt1, waits=[("smact", av), ("smdve", st["t1_free"])])
            if level >= 2:
                def mrot(eng, qp=qp):
                    return eng.matmul(psC[:, 0:512], rot_s[:, :], qpre_s[qp][:, :],
                                      start=True, stop=True)
                mmr = P.pe(mrot, waits=[("smact", av)] + P.bank_w("C"))
                st["qpre_free"][qp] = mmr
                def mt2(eng, sl=sl):
                    return eng.tensor_mul(t2_s[:, :], psC[:, 0:512], sin_s[:, sl])
                dv2 = P.dve(mt2, waits=[("smpe", mmr), ("smdve", st["t2_free"])])
                P.bank_r(["C"], [("smdve", dv2)])
                def madd(eng, sl=sl):
                    return eng.tensor_add(Qh_s[:, sl], t1_s[:, :], t2_s[:, :])
                dv3 = P.dve(madd, waits=[("smpe", st["Qh_free"])])
                st["t1_free"] = dv3
                st["t2_free"] = dv3
        dump_rows([(Qh_s[:, :], 0, T)],
                  [("smdve", c("smdve")), ("smact", c("smact")), ("smpe", c("smpe"))])
        stopped = True
    if dump in ("projq", "projqk"):
        proj_head(0, 0, 0, only=1 if dump == "projq" else 2)
        dump_rows([(Qh_s[:, :], 0, T), (Kh_s[:, :], 128, T)],
                  [("smdve", c("smdve")), ("smact", c("smact")), ("smpe", c("smpe"))])
        stopped = True
    if dump in ("p0", "p1", "p2", "p3"):
        level = int(dump[1])
        nw = st["wt_have"][0]
        nx = st["x_have"]
        def mk_mm(eng):
            ins = None
            for cc in range(NC):
                ins = eng.matmul(psE[:, 0:512], wT_s[0][0][:, cc * 128:(cc + 1) * 128],
                                 X_s[cc][:, 0:512], start=(cc == 0), stop=(cc == NC - 1))
            return ins
        mmv = P.pe(mk_mm, waits=[("smwt0", nw), ("smxt", nx)])
        def ev(eng):
            return eng.copy(qpre_s[0][:, :], psE[:, 0:512])
        P.act(ev, waits=[("smpe", mmv)])
        if level >= 1:
            def mt1(eng):
                return eng.tensor_mul(t1_s[:, :], psE[:, 0:512], cos_s[:, 0:512])
            P.dve(mt1, waits=[("smpe", mmv)])
        if level >= 2:
            def mrot(eng):
                return eng.matmul(psC[:, 0:512], rot_s[:, :], qpre_s[0][:, :],
                                  start=True, stop=True)
            mmr = P.pe(mrot, waits=[("smact", c("smact"))])
            if level >= 3:
                def mt2(eng):
                    return eng.tensor_mul(t2_s[:, :], psC[:, 0:512], sin_s[:, 0:512])
                P.dve(mt2, waits=[("smpe", mmr)])
                def madd(eng):
                    return eng.tensor_add(Qh_s[:, 0:512], t1_s[:, :], t2_s[:, :])
                P.dve(madd)
        dump_rows([(qpre_s[0][:, :], 0, 512), (Qh_s[:, 0:512], 128, 512)],
                  [("smdve", c("smdve")), ("smact", c("smact")), ("smpe", c("smpe"))])
        stopped = True
    if dump == "proj":
        proj_head(0, 0, 0)
        dump_rows([(Qh_s[:, :], 0, T), (Kh_s[:, :], 128, T),
                   (Vh_s[:, :], 256, T)],
                  [("smdve", c("smdve")), ("smact", c("smact")),
                   ("smpe", c("smpe"))])
        stopped = True
    if dump == "X":
        dump_rows([(X_s[m][:, :], m * 128, T) for m in range(NC)],
                  [("smxt", st["x_have"])])
        stopped = True

    done_sp = 0
    if not stopped:
        for s in range(NSEQ):
            for h in range(H):
                attn_head(s, h)
                for _ in range(PREP_PER_SLOT):
                    stp = next(prep_iter, None)
                    if stp is not None:
                        stp()
                if dump == "qkv" and s == 0 and h == 0:
                    dump_rows([(Qh_s[:, :], 0, T), (Kh_s[:, :], 128, T),
                               (Vh_s[:, :], 256, T)],
                              [("smdve", c("smdve")), ("smact", c("smact")),
                               ("smpe", c("smpe"))])
                    stopped = True
                    break
            if stopped:
                break
            if dump == "attn" and s == 0:
                dump_rows([], [])
                for m in range(NC):
                    P.dma("pool", "smdbg", dbg[m * 128:(m + 1) * 128, 0:T],
                          at_sp[m], waits=[("smat0", c("smat0")),
                                           ("smat1", c("smat1"))])
                P.dma("pool", "smdbg", dbg[2048:2049, 0:48], pdbg_s[0:1, :],
                      waits=[("smdve", c("smdve"))])
                P.barrier("sp", [("smdbg", c("smdbg"))])
                stopped = True
                break
            done_sp = oproj_ln1(s)
            if dump == "x2" and s == 0:
                dump_rows([(X_s[m][:, :], m * 128, T) for m in range(NC)],
                          [("smdve", c("smdve")), ("smsp", done_sp)])
                stopped = True
                break
            if s + 1 < NSEQ:
                st["x_have"] = load_X(s + 1, extra=[("smsp", done_sp)])

    if not stopped:
        for stp in prep_iter:
            stp()
        nq = c("smqs")
        ntw = [("smtw0", c("smtw0")), ("smtw1", c("smtw1"))]
        for tb in range(TBLK):
            ffn_tb(tb, nq, done_sp, ntw)
            if dump == "h" and tb == 0:
                dump_rows([(h_s[0][m][:, :], m * 128, 512) for m in range(32)],
                          [("smdve", c("smdve"))])
                stopped = True
                break
        if not stopped:
            P.barrier("sp", [("smout0", c("smout0")), ("smout1", c("smout1"))])

    body = P.emit()
    with nc.Block() as blk:
        blk.tensor(body("pe"))
        blk.scalar(body("act"))
        blk.vector(body("dve"))
        blk.gpsimd(body("pool"))
        blk.sync(body("sp"))
    return nc


def kernel(L0_attn_hidden_states, L0_attn_wq, L0_attn_wk, L0_attn_wv,
           L0_attn_wo, L0_attn_bern_u, L0_ffn_w1, L0_ffn_w2, L0_ffn_w3):
    x = np.ascontiguousarray(L0_attn_hidden_states, dtype=np.float32)
    nc = build_program()
    common = {
        "wq": np.ascontiguousarray(L0_attn_wq, dtype=np.float32),
        "wk": np.ascontiguousarray(L0_attn_wk, dtype=np.float32),
        "wv": np.ascontiguousarray(L0_attn_wv, dtype=np.float32),
        "wo": np.ascontiguousarray(L0_attn_wo, dtype=np.float32),
        "w1": np.ascontiguousarray(L0_ffn_w1, dtype=np.float32),
        "w2": np.ascontiguousarray(L0_ffn_w2, dtype=np.float32),
        "w3": np.ascontiguousarray(L0_ffn_w3, dtype=np.float32),
        "bu": np.ascontiguousarray(L0_attn_bern_u, dtype=np.float32),
    }
    in_maps = [dict(common, x=x[i * 2:(i + 1) * 2]) for i in range(8)]
    res = run_bass_kernel_spmd(nc, in_maps, core_ids=list(range(8)))
    return np.concatenate([res.results[i]["y"] for i in range(8)], axis=0)


# version 07
# speedup vs baseline: 3.4003x; 1.0491x over previous
import math

import numpy as np
import ml_dtypes

import concourse.bass as bass
import concourse.mybir as mybir
from concourse.bass_utils import run_bass_kernel_spmd

F32 = mybir.dt.float32
BF16 = mybir.dt.bfloat16
AF = mybir.ActivationFunctionType
ALU = mybir.AluOpType
AX = mybir.AxisListType

D = 2048
T = 2048
H = 16
HD = 128
NSEQ = 2
NC = D // 128
NT = T // 512
KT = 10
FH = 5632
NHH = FH // 128
TBLK = 4
MAGIC = 12582912.0
INV_SQRT_HD = 1.0 / math.sqrt(HD)
PCOLS = 1024

DETECT_RACES = False


def _kstart(j):
    return j * 128 if j < 2 else 1024 + (j - 2) * 128


def _host_consts():
    ident = np.eye(128, dtype=ml_dtypes.bfloat16)
    ones_col_bf = np.ones((128, 1), dtype=ml_dtypes.bfloat16)
    ones_row_bf = np.ones((1, 128), dtype=ml_dtypes.bfloat16)
    ones_row_f = np.ones((1, 128), dtype=np.float32)
    ones_col_f = np.ones((128, 1), dtype=np.float32)
    R = np.zeros((128, 128), dtype=np.float32)
    R[np.arange(64), np.arange(64) + 64] = -1.0
    R[np.arange(64) + 64, np.arange(64)] = 1.0
    rot_lhsT = np.ascontiguousarray(R.T).astype(ml_dtypes.bfloat16)
    a = np.arange(0, HD, 2, dtype=np.float32)
    inv_freq = (np.float32(1.0)
                / np.power(np.float32(10000.0), a / np.float32(HD))).astype(np.float32)
    pos = np.arange(T, dtype=np.float32)
    freqs = pos[None, :] * inv_freq[:, None]
    emb = np.concatenate([freqs, freqs], axis=0)
    return dict(ident=ident, ocb=ones_col_bf, orb=ones_row_bf, orf=ones_row_f,
                ocf=ones_col_f, rot=rot_lhsT,
                magic=np.full((128, 1), MAGIC, dtype=np.float32),
                nmagic=np.full((128, 1), -MAGIC, dtype=np.float32),
                cos=np.cos(emb).astype(np.float32),
                sin=np.sin(emb).astype(np.float32))


class Prog:
    def __init__(self, nc):
        self.nc = nc
        self.ops = []
        self.cnt = {}
        self.sems = {}
        self.banks = {}

    def sem(self, name):
        if name not in self.sems:
            self.sems[name] = self.nc.alloc_semaphore(name)
            self.cnt[name] = 0
        return self.sems[name]

    def c(self, name):
        self.sem(name)
        return self.cnt[name]

    def _add(self, eng, fn, waits, inc, mode):
        ws = []
        for s, v in waits:
            self.sem(s)
            if v > 0:
                ws.append((s, v))
        if inc is not None:
            self.sem(inc[0])
            self.cnt[inc[0]] += inc[1]
        self.ops.append((eng, ws, fn, inc, mode))

    def pe(self, fn, waits=(), sig=1):
        self._add("pe", fn, waits, ("smpe", sig) if sig else None, "instr")
        return self.cnt["smpe"]

    def act(self, fn, waits=(), sig=1):
        self._add("act", fn, waits, ("smact", sig) if sig else None, "instr")
        return self.cnt["smact"]

    def dve(self, fn, waits=(), sig=1, mode="drain"):
        self._add("dve", fn, waits, ("smdve", sig) if sig else None, mode)
        return self.cnt["smdve"]

    def pool(self, fn, waits=(), sig=1):
        self._add("pool", fn, waits, ("smpool", sig) if sig else None, "drain")
        return self.cnt["smpool"]

    def barrier(self, eng, waits):
        self._add(eng, lambda e: e.nop(), waits, None, "none")

    def dma(self, queue, sem_name, out_ap, in_ap, waits=(), transpose=False):
        def fn(eng, out_ap=out_ap, in_ap=in_ap, transpose=transpose):
            return eng.dma_start(out=out_ap, in_=in_ap, transpose=transpose)
        self._add(queue, fn, waits, (sem_name, 16), "instr")
        return self.cnt[sem_name]

    def bank_w(self, *names):
        ws = []
        for n in names:
            ws.extend(self.banks.get(n, []))
        return ws

    def bank_r(self, names, entries):
        for n in names:
            self.banks[n] = list(entries)

    def emit(self):
        waited = {}

        def body(eng_name):
            def run(eng):
                for op_eng, ws, fn, inc, mode in self.ops:
                    if op_eng != eng_name:
                        continue
                    for s, v in ws:
                        key = (eng_name, s)
                        if waited.get(key, 0) >= v:
                            continue
                        waited[key] = v
                        eng.wait_ge(self.sems[s], v)
                    ins = fn(eng)
                    if inc is not None:
                        s, n = inc
                        if mode == "instr":
                            assert ins is not None
                            ins.then_inc(self.sems[s], n)
                        else:
                            eng.drain()
                            eng.nop().then_inc(self.sems[s], n)
            return run
        return body


def build_program(dump=None):
    nc = bass.Bass("TRN2", target_bir_lowering=False, debug=False,
                   detect_race_conditions=DETECT_RACES)
    consts = _host_consts()

    x_in = nc.dram_tensor("x", [NSEQ, T, D], F32, kind="ExternalInput").ap()
    win = {}
    for nm, shp in (("wq", [D, D]), ("wk", [D, D]), ("wv", [D, D]), ("wo", [D, D]),
                    ("w1", [FH, D]), ("w2", [FH, D]), ("w3", [D, FH])):
        win[nm] = nc.dram_tensor(nm, shp, F32, kind="ExternalInput").ap()
    bu = nc.dram_tensor("bu", [1, H], F32, kind="ExternalInput").ap()
    y_out = nc.dram_tensor("y", [NSEQ, T, D], F32, kind="ExternalOutput").ap()
    dbg = None
    if dump is not None:
        dbg = nc.dram_tensor("dbg", [4096, 4096], F32, kind="ExternalOutput").ap()

    x_bf = nc.dram_tensor("x_bf", [NSEQ, T, D], BF16, kind="Internal").ap()
    aw_bf = {n: nc.dram_tensor(f"{n}_bf", [D, D], BF16, kind="Internal").ap()
             for n in ("wq", "wk", "wv", "wo")}
    q_bf = {0: nc.dram_tensor("q1_bf", [FH, D], BF16, kind="Internal").ap(),
            1: nc.dram_tensor("q2_bf", [FH, D], BF16, kind="Internal").ap(),
            2: nc.dram_tensor("q3_bf", [D, FH], BF16, kind="Internal").ap()}
    at_sp = nc.dram_tensor("at_sp", [NC, 128, T], BF16, kind="Internal").ap()
    awT_img = {w: nc.dram_tensor(f"awT_{w}", [H, 128, D], BF16, kind="Internal").ap()
               for w in ("wq", "wk", "wv")}
    woT_img = nc.dram_tensor("woT_img", [8, 128, 4096], BF16, kind="Internal").ap()
    q12T_img = [nc.dram_tensor(f"q{i}T_img", [NHH, 128, D], BF16,
                               kind="Internal").ap() for i in (1, 2)]
    q3T_img = nc.dram_tensor("q3T_img", [NC, 128, NHH * 128], BF16,
                             kind="Internal").ap()
    x2_sp = nc.dram_tensor("x2_sp", [NSEQ, NC, 128, T], BF16, kind="Internal").ap()

    cmap = {nm: nc.inline_tensor(arr, name=f"c_{nm}").ap()
            for nm, arr in consts.items()}

    SBUF_LIMIT = nc.sbuf_top
    cur = [nc.sbuf_base]

    def at(name, shape, dtype, base=None):
        bpe = 2 if dtype == BF16 else 4
        ptr = cur if base is None else base
        o = (ptr[0] + 31) & ~31
        t = nc.alloc_sbuf_tensor_at(name, list(shape), dtype, offset=o)
        ptr[0] = o + int(np.prod(shape[1:])) * bpe
        return t

    def multi(name, n, shape, dtype, base=None):
        return [at(f"{name}{i}", shape, dtype, base=base) for i in range(n)]

    ident_s = at("ident_s", [128, 128], BF16)
    ocb_s = at("ocb_s", [128, 1], BF16)
    orb_s = at("orb_s", [1, 128], BF16)
    orf_s = at("orf_s", [1, 128], F32)
    ocf_s = at("ocf_s", [128, 1], F32)
    rot_s = at("rot_s", [128, 128], BF16)
    bu_s = at("bu_s", [1, H], F32)
    rsc_s = multi("rsc", 3, [128, 1], F32)
    ve_s = at("ve_s", [128, 4], F32)
    pdbg_s = at("pdbg_s", [1, 48], F32)
    magic_s = at("magic_s", [128, 1], F32)
    nmagic_s = at("nmagic_s", [128, 1], F32)

    attn_base = cur[0]
    cos_s = at("cos_s", [128, T], F32)
    sin_s = at("sin_s", [128, T], F32)
    X_s = multi("X", NC, [128, T], BF16)
    Qh_s = at("Qh_s", [128, T], BF16)
    Kh_s = at("Kh_s", [128, T], BF16)
    Vh_s = at("Vh_s", [128, T], BF16)
    qpre_s = multi("qpre", 2, [128, 512], BF16)
    t1_s = at("t1_s", [128, 512], F32)
    t2_s = at("t2_s", [128, 512], F32)
    eT_s = at("eT_s", [128, KT * 512], BF16)
    vtm_s = at("vtm", [128, KT * 128], BF16)
    shb_s = at("shb_s", [128, 512], BF16)
    wT_s = [multi(f"wT{p}_", 3, [128, D], BF16) for p in range(2)]
    atv_s = multi("atv", 2, [128, 512], BF16)
    atb_s = multi("atb", NC, [128, 512], BF16)
    woT_s = [at(f"woT{p}", [128, 4096], BF16) for p in range(2)]
    sq_s = multi("sq", 2, [128, 512], BF16)
    srow_s = at("srow_s", [1, 512], F32)
    qrow_s = at("qrow_s", [1, 512], F32)
    mb_s = at("mb_s", [128, 512], F32)
    rb_s = at("rb_s", [128, 512], F32)
    attn_end = cur[0]

    pw_s = multi("pw", 3, [128, PCOLS], F32)
    pt_s = multi("pt", 2, [128, PCOLS], F32)
    pq_s = multi("pq", 2, [128, PCOLS], BF16)
    pqt_s = multi("pqt", 1, [128, 1024], BF16)
    pabs_s = at("pabs_s", [128, 1], F32)
    assert cur[0] <= SBUF_LIMIT, f"attn+prep over: {cur[0]}"

    fb = [attn_base]
    x2b_s = [multi(f"x2b{s}_", NC, [128, 512], BF16, base=fb) for s in range(NSEQ)]
    h_s = [multi(f"h{s}_", NHH, [128, 512], BF16, base=fb) for s in range(NSEQ)]
    qwT_s = [[at(f"qwT{p}{w}", [128, D], BF16, base=fb) for w in range(2)]
             for p in range(2)]
    q3T_s = [at(f"q3T{p}", [128, NHH * 128], BF16, base=fb) for p in range(2)]
    r2f_s = multi("r2f", 2, [128, 512], BF16, base=fb)
    r2t_s = multi("r2t", NT, [128, D], BF16, base=fb)
    sg_s = multi("sg", 2, [128, 512], BF16, base=fb)
    ost_s = multi("ost", 2, [128, D], F32, base=fb)
    l2v_s = at("l2v", [128, 8], F32, base=fb)
    assert fb[0] <= SBUF_LIMIT, f"ffn over: {fb[0]}"

    psA = nc.alloc_psum_tensor("psA", [128, 1024], F32)
    psB = nc.alloc_psum_tensor("psB", [128, 1024], F32)
    psC = nc.alloc_psum_tensor("psC", [128, 512], F32)
    psE = nc.alloc_psum_tensor("psE", [128, 512], F32)
    psF = nc.alloc_psum_tensor("psF", [128, 512], F32)
    psG = nc.alloc_psum_tensor("psG", [128, 1024], BF16)

    P = Prog(nc)
    c = P.c
    gslot = [[] for _ in range(8)]

    for nm, dst in (("ident", ident_s), ("ocb", ocb_s), ("rot", rot_s),
                    ("cos", cos_s), ("sin", sin_s)):
        P.dma("sp", "smc", dst[:, :], cmap[nm])
    P.dma("sp", "smc", orb_s[0:1, :], cmap["orb"])
    P.dma("sp", "smc", orf_s[0:1, :], cmap["orf"])
    P.dma("sp", "smc", ocf_s[:, :], cmap["ocf"])
    P.dma("sp", "smc", magic_s[:, :], cmap["magic"])
    P.dma("sp", "smc", nmagic_s[:, :], cmap["nmagic"])
    N_CONST = c("smc")
    P.dma("pool", "smc2", bu_s[0:1, :], bu)

    for s in range(NSEQ):
        P.dma("pool", f"smxc{s}", x_bf[s], x_in[s])
    for n in ("wq", "wk", "wv", "wo"):
        P.dma("pool", "smwc", aw_bf[n], win[n])

    P.dve(lambda eng: eng.memset(eT_s[64:128, 512:1024], 0.0))

    prep_steps = []
    ps = {"pw_free": [[("smc2", 16)] for _ in range(3)],
          "pt_free": [0, 0], "pq_free": [0, 0]}

    def gamma_chunk(widx, w_ap, row0, col0, ncols, par, first):
        def step():
            nd = P.dma("pool", f"smqf{par % 2}", pw_s[par][:, 0:ncols],
                       w_ap[row0:row0 + 128, col0:col0 + ncols],
                       waits=ps["pw_free"][par])

            def fn2(eng, par=par, ncols=ncols, first=first):
                eng.tensor_reduce(ve_s[:, 3:4], pw_s[par][:, 0:ncols],
                                  axis=AX.X, op=ALU.add, apply_absolute_value=True)
                eng.drain()
                if first:
                    return eng.tensor_copy(pabs_s[:, 0:1], ve_s[:, 3:4])
                return eng.tensor_add(pabs_s[:, 0:1], pabs_s[:, 0:1], ve_s[:, 3:4])
            dv = P.dve(fn2, waits=[(f"smqf{par % 2}", nd)] +
                       ([("smpe", c("smpe"))] if first else []))
            ps["pw_free"][par] = [("smdve", dv)]
        return step

    def gamma_fin(widx, count):
        def step():
            dv0 = c("smdve")

            def mm0(eng):
                return eng.matmul(psC[0:1, 0:1], pabs_s[:, 0:1], ocf_s[:, 0:1],
                                  start=True, stop=True)
            pv0 = P.pe(mm0, waits=[("smdve", dv0)] + P.bank_w("C"))

            def fin(eng, count=count):
                eng.tensor_scalar(ve_s[0:1, 0:1], psC[0:1, 0:1],
                                  1.0 / count, 1e-8, op0=ALU.mult, op1=ALU.add)
                eng.drain()
                return eng.reciprocal(ve_s[0:1, 1:2], ve_s[0:1, 0:1])
            dv = P.dve(fin, waits=[("smpe", pv0)])
            P.bank_r(["C"], [("smdve", dv)])

            def bc(eng):
                return eng.matmul(psC[:, 0:1], orf_s[0:1, :], ve_s[0:1, 1:2],
                                  start=True, stop=True)
            pv2 = P.pe(bc, waits=[("smdve", dv)] + P.bank_w("C"))

            def ev(eng, widx=widx):
                return eng.tensor_copy(rsc_s[widx][:, 0:1], psC[:, 0:1])
            dv2 = P.dve(ev, waits=[("smpe", pv2)])
            P.bank_r(["C"], [("smdve", dv2)])
        return step

    def quant_chunk(widx, w_ap, q_ap, row0, col0, ncols, par):
        def step():
            nd = P.dma("pool", f"smqf{par % 2}", pw_s[par][:, 0:ncols],
                       w_ap[row0:row0 + 128, col0:col0 + ncols],
                       waits=ps["pw_free"][par])

            tpar = ps.setdefault("tpar", 0)
            ps["tpar"] = tpar ^ 1

            def qf(eng, par=par, ncols=ncols, widx=widx, tpar=tpar):
                return eng.tensor_scalar(pt_s[tpar][:, 0:ncols], pw_s[par][:, 0:ncols],
                                         rsc_s[widx][:, 0:1], MAGIC,
                                         op0=ALU.mult, op1=ALU.add)
            dv = P.pool(qf, waits=[(f"smqf{par % 2}", nd), ("smdve", c("smdve")),
                                   ("smact", ps["pt_free"][tpar])])
            ps["pw_free"][par] = [("smpool", dv)]

            def sg(eng, ncols=ncols, tpar=tpar):
                return eng.activation(pq_s[tpar][:, 0:ncols], pt_s[tpar][:, 0:ncols],
                                      AF.Sign, bias=nmagic_s[:, 0:1], scale=1.0)
            av = P.act(sg, waits=[("smpool", dv), ("smqs", ps["pq_free"][tpar]),
                                  ("smc", N_CONST)])
            ps["pt_free"][tpar] = av
            ps["pq_free"][tpar] = P.dma("pool", f"smqs",
                                        q_ap[row0:row0 + 128, col0:col0 + ncols],
                                        pq_s[tpar][:, 0:ncols], waits=[("smact", av)])
        return step

    def chunks_of(total, width):
        out = []
        o = 0
        while o < total:
            out.append((o, min(width, total - o)))
            o += width
        return out

    wshapes = {0: (win["w1"], FH, D), 1: (win["w2"], FH, D), 2: (win["w3"], D, FH)}
    for widx in (0, 1, 2):
        w_ap, rows, cols = wshapes[widx]
        first = True
        for i in range(rows // 128):
            for (c0, cn) in chunks_of(cols, PCOLS):
                prep_steps.append(gamma_chunk(widx, w_ap, i * 128, c0, cn,
                                              len(prep_steps) % 3, first))
                first = False
        prep_steps.append(gamma_fin(widx, rows * cols))
    for widx in (0, 1, 2):
        w_ap, rows, cols = wshapes[widx]
        for i in range(rows // 128):
            for (c0, cn) in chunks_of(cols, PCOLS):
                prep_steps.append(quant_chunk(widx, w_ap, q_bf[widx], i * 128,
                                              c0, cn, len(prep_steps) % 3))
        def mk_thop(widx=widx, rows=rows, cols=cols):
            def step():
                gate = [("smqs", c("smqs"))]
                if widx < 2:
                    for cc in range(NC):
                        for r0, nr in chunks_of(rows, 1024):
                            dst = q12T_img[widx][r0 // 128:(r0 + nr) // 128, 0:128,
                                                 cc * 128:(cc + 1) * 128].rearrange(
                                                     "h p f -> p h f")
                            thop_pair(q_bf[widx][r0:r0 + nr,
                                                 cc * 128:(cc + 1) * 128],
                                      dst, gate)
                else:
                    for hhc in range(NHH):
                        for r0, nr in chunks_of(D, 1024):
                            dst = q3T_img[r0 // 128:(r0 + nr) // 128, 0:128,
                                          hhc * 128:(hhc + 1) * 128].rearrange(
                                              "d p f -> p d f")
                            thop_pair(q_bf[2][r0:r0 + nr,
                                              hhc * 128:(hhc + 1) * 128],
                                      dst, gate)
            return step
        prep_steps.append(mk_thop())
    prep_iter = iter(prep_steps)
    PREP_PER_SLOT = max(1, (len(prep_steps) + NSEQ * H - 1) // (NSEQ * H))

    th_state = {"par": 0}

    def thop_pair(src_ap, img_dst, gate_waits):
        par = 0
        nr = src_ap.shape[0]
        P.dma("sp", f"smth{par}", pqt_s[par][:, 0:nr], src_ap,
              waits=gate_waits + [(f"smtw{par}", c(f"smtw{par}"))], transpose=True)
        P.dma("pool", f"smtw{par}", img_dst, pqt_s[par][:, 0:nr],
              waits=[(f"smth{par}", c(f"smth{par}"))])

    def sched_attn_thop():
        gates = []
        for h0 in (0, 8):
            for w in ("wq", "wk", "wv"):
                for cc in range(NC):
                    dst = awT_img[w][h0:h0 + 8, 0:128,
                                     cc * 128:(cc + 1) * 128].rearrange(
                                         "h p f -> p h f")
                    thop_pair(aw_bf[w][h0 * 128:(h0 + 8) * 128,
                                       cc * 128:(cc + 1) * 128],
                              dst, [("smwc", 64)])
            gates.append([("smtw0", c("smtw0")), ("smtw1", c("smtw1"))])
        for g0 in (0, 4):
            for cc in range(NC):
                dst = woT_img[g0:g0 + 4, 0:128,
                              cc * 256:(cc + 1) * 256].rearrange("g p f -> p g f")
                thop_pair(aw_bf["wo"][g0 * 256:g0 * 256 + 1024,
                                      cc * 128:(cc + 1) * 128],
                          dst, [("smwc", 64)])
        gates.append([("smtw0", c("smtw0")), ("smtw1", c("smtw1"))])
        return gates

    st = {"wt_free": [[], []], "wt_have": [0, 0], "x_have": 0,
          "qpre_free": [0, 0], "t1_free": 0, "t2_free": 0,
          "Qh_free": 0, "Kh_free": 0, "Vh_free": 0, "eT_free": 0,
          "qk_ready": 0, "v_ready": 0,
          "shb_free": 0, "vtm_free": 0, "atv_free": [0, 0],
          "sq_free": [0, 0], "woT_free": [[], []], "atb_free": []}

    def load_X(s, extra=()):
        v = 0
        for cc in range(NC):
            v = P.dma("sp", "smxt", X_s[cc][:, :],
                      x_bf[s][0:T, cc * 128:(cc + 1) * 128],
                      waits=[(f"smxc{s}", 16), ("smc", N_CONST)] + list(extra),
                      transpose=True)
        return v

    def load_wT(h, par):
        gate = st["awT_gate"][0 if h < 8 else 1]
        for wi, name in enumerate(("wq", "wk", "wv")):
            P.dma("sp", f"smwt{par}", wT_s[par][wi][:, :], awT_img[name][h],
                  waits=gate + st["wt_free"][par])
        return c(f"smwt{par}")

    def proj_head(s, h, par, only=None):
        nw = st["wt_have"][par]
        nx = st["x_have"]
        for wi, name in enumerate(("q", "k", "v")[:only] if only else ("q", "k", "v")):
            for tc in range(NT):
                bank = ("E", "F")[tc % 2]
                pj = (psE, psF)[tc % 2]
                sl = slice(tc * 512, (tc + 1) * 512)

                def mk_mm(eng, wi=wi, pj=pj, sl=sl, par=par):
                    ins = None
                    for cc in range(NC):
                        ins = eng.matmul(pj[:, 0:512],
                                         wT_s[par][wi][:, cc * 128:(cc + 1) * 128],
                                         X_s[cc][:, sl],
                                         start=(cc == 0), stop=(cc == NC - 1))
                    return ins
                mmv = P.pe(mk_mm, waits=[(f"smwt{par}", nw), ("smxt", nx)]
                           + P.bank_w(bank))
                if name == "v":
                    def ev(eng, pj=pj, sl=sl):
                        return eng.copy(Vh_s[:, sl], pj[:, 0:512])
                    av = P.act(ev, waits=[("smpe", mmv), ("smpe", st["Vh_free"])])
                    P.bank_r([bank], [("smact", av)])
                else:
                    dst, dfree = ((Qh_s, "Qh_free") if name == "q"
                                  else (Kh_s, "Kh_free"))
                    qp = tc % 2

                    def ev(eng, pj=pj, qp=qp):
                        return eng.copy(qpre_s[qp][:, :], pj[:, 0:512])
                    av = P.act(ev, waits=[("smpe", mmv),
                                          ("smpe", st["qpre_free"][qp])])
                    P.bank_r([bank], [("smact", av)])

                    def mt1(eng, qp=qp, sl=sl):
                        return eng.tensor_mul(t1_s[:, :], qpre_s[qp][:, :],
                                              cos_s[:, sl])
                    dv1 = P.dve(mt1, waits=[("smact", av), ("smdve", st["t1_free"])])

                    def mrot(eng, qp=qp):
                        return eng.matmul(psC[:, 0:512], rot_s[:, :],
                                          qpre_s[qp][:, :], start=True, stop=True)
                    mmr = P.pe(mrot, waits=[("smact", av)] + P.bank_w("C"))
                    st["qpre_free"][qp] = mmr

                    def mt2(eng, sl=sl):
                        return eng.tensor_mul(t2_s[:, :], psC[:, 0:512], sin_s[:, sl])
                    dv2 = P.dve(mt2, waits=[("smpe", mmr), ("smdve", st["t2_free"])])
                    P.bank_r(["C"], [("smdve", dv2)])

                    def madd(eng, dst=dst, sl=sl):
                        return eng.tensor_add(dst[:, sl], t1_s[:, :], t2_s[:, :])
                    dv3 = P.dve(madd, waits=[("smpe", st[dfree])])
                    st["t1_free"] = dv3
                    st["t2_free"] = dv3
        st["wt_free"][par] = [("smpe", c("smpe"))]

    def sT_exp(qg):
        qsl = slice(qg * 512, (qg + 1) * 512)
        for p in range(KT // 2):
            pp = (psA, psB)[p % 2]
            bks = ("A0", "A1") if p % 2 == 0 else ("B0", "B1")

            def mm(eng, p=p, pp=pp, qsl=qsl):
                ins = None
                for jj in range(2):
                    j = 2 * p + jj
                    ins = eng.matmul(pp[:, jj * 512:(jj + 1) * 512],
                                     Kh_s[:, _kstart(j):_kstart(j) + 128],
                                     Qh_s[:, qsl], start=True, stop=True)
                return ins
            mmv = P.pe(mm, waits=[("smdve", st["qk_ready"])] + P.bank_w(*bks))
            if p == 0:
                def ex(eng):
                    eng.activation(eT_s[:, 0:512], psA[:, 0:512], AF.Exp,
                                   bias=0.0, scale=INV_SQRT_HD)
                    return eng.activation(eT_s[0:76, 512:1024], psA[0:76, 512:1024],
                                          AF.Exp, bias=0.0, scale=INV_SQRT_HD)
            else:
                def ex(eng, p=p, pp=pp):
                    return eng.activation(eT_s[:, p * 1024:(p + 1) * 1024], pp[:, :],
                                          AF.Exp, bias=0.0, scale=INV_SQRT_HD)
            av = P.act(ex, waits=[("smpe", mmv), ("smpe", st["eT_free"])])
            P.bank_r(bks, [("smact", av)])
        return c("smact")

    def S_sum(exp_done):
        def mm(eng):
            ins = None
            for j in range(KT):
                ins = eng.matmul(psC[0:1, 0:512], ocb_s[:, 0:1],
                                 eT_s[:, j * 512:(j + 1) * 512],
                                 start=(j == 0), stop=(j == KT - 1))
            return ins
        return P.pe(mm, waits=[("smact", exp_done)] + P.bank_w("C"))

    def shb_make(sv):
        def rc(eng):
            with nc.allow_low_precision(reason="1/S in bf16; probs are bf16 anyway"):
                return eng.reciprocal(shb_s[0:1, 0:512], psC[0:1, 0:512])
        dv = P.dve(rc, waits=[("smpe", sv), ("smdve", st["shb_free"])])
        P.bank_r(["C"], [("smdve", dv)])

        def bc(eng):
            return eng.matmul(psC[:, 0:512], orb_s[0:1, :], shb_s[0:1, 0:512],
                              start=True, stop=True)
        pv = P.pe(bc, waits=[("smdve", dv)] + P.bank_w("C"))

        def ev(eng):
            return eng.tensor_copy(shb_s[:, :], psC[:, 0:512])
        dv2 = P.dve(ev, waits=[("smpe", pv)])
        P.bank_r(["C"], [("smdve", dv2)])
        return dv2

    def pv_evac(s, h, qg, exp_done, shb_done, vtm_done):
        bank = ("E", "F")[qg % 2]
        pv_ = (psE, psF)[qg % 2]

        def mm(eng, pv_=pv_):
            ins = None
            for j in range(KT):
                ins = eng.matmul(pv_[:, 0:512], vtm_s[:, j * 128:(j + 1) * 128],
                                 eT_s[:, j * 512:(j + 1) * 512],
                                 start=(j == 0), stop=(j == KT - 1))
            return ins
        mv = P.pe(mm, waits=[("smact", exp_done), ("smdve", shb_done),
                             ("smact", vtm_done)] + P.bank_w(bank))
        ap = qg % 2

        def ev(eng, pv_=pv_, ap=ap):
            return eng.tensor_mul(atv_s[ap][:, :], pv_[:, 0:512], shb_s[:, :])
        dv = P.dve(ev, waits=[("smpe", mv), (f"smat{ap}", st["atv_free"][ap])])
        P.bank_r([bank], [("smdve", dv)])
        st["eT_free"] = mv
        st["shb_free"] = dv
        st["atv_free"][ap] = P.dma("act", f"smat{ap}",
                                   at_sp[h][:, qg * 512:(qg + 1) * 512],
                                   atv_s[ap][:, :], waits=[("smdve", dv)])
        return dv

    def prepass_vupdate(h, exp_done, sv):
        def asum(eng):
            ins = None
            for j in range(2, KT):
                ins = eng.matmul(psE[0:1, 0:1],
                                 eT_s[:, j * 512 + 511:j * 512 + 512],
                                 ocb_s[:, 0:1], start=(j == 2), stop=(j == KT - 1))
            return ins
        pa = P.pe(asum, waits=[("smact", exp_done)] + P.bank_w("E"))

        def pmath(eng, h=h):
            eng.tensor_copy(ve_s[0:1, 0:1], psE[0:1, 0:1])
            eng.tensor_copy(ve_s[0:1, 1:2], eT_s[0:1, 2 * 512 + 511:2 * 512 + 512])
            eng.tensor_scalar_mul(ve_s[0:1, 2:3], psC[0:1, 511:512], 1e-6)
            eng.drain()
            eng.tensor_sub(ve_s[0:1, 0:1], ve_s[0:1, 0:1], ve_s[0:1, 1:2])
            eng.drain()
            eng.tensor_scalar_mul(ve_s[0:1, 0:1], ve_s[0:1, 0:1], 1.0 / 1023.0)
            eng.drain()
            eng.tensor_add(ve_s[0:1, 0:1], ve_s[0:1, 0:1], ve_s[0:1, 2:3])
            eng.drain()
            eng.reciprocal(ve_s[0:1, 0:1], ve_s[0:1, 0:1])
            eng.drain()
            eng.tensor_mul(ve_s[0:1, 0:1], ve_s[0:1, 1:2], ve_s[0:1, 0:1])
            eng.drain()
            eng.tensor_copy(pdbg_s[0:1, h:h + 1], ve_s[0:1, 0:1])
            eng.tensor_tensor(ve_s[0:1, 1:2], bu_s[0:1, h:h + 1], ve_s[0:1, 0:1],
                              op=ALU.is_lt)
            eng.drain()
            eng.tensor_copy(pdbg_s[0:1, 16 + h:17 + h], ve_s[0:1, 1:2])
            return eng.tensor_scalar_mul(ve_s[0:1, 1:2], ve_s[0:1, 1:2], 1.0 / 1024.0)
        dv = P.dve(pmath, waits=[("smpe", pa), ("smpe", sv), ("smc2", 16)])
        P.bank_r(["E"], [("smdve", dv)])
        P.bank_r(["C"], [("smdve", dv)])

        def bcf(eng):
            return eng.matmul(psE[:, 0:1], orf_s[0:1, :], ve_s[0:1, 1:2],
                              start=True, stop=True)
        pb = P.pe(bcf, waits=[("smdve", dv)] + P.bank_w("E"))

        def vupd(eng):
            eng.tensor_copy(ve_s[:, 0:1], psE[:, 0:1])
            eng.tensor_copy(ve_s[:, 1:2], Vh_s[:, 1024:1025])
            eng.drain()
            eng.tensor_mul(ve_s[:, 0:1], ve_s[:, 0:1], ve_s[:, 1:2])
            eng.drain()
            return eng.tensor_scalar_add(Vh_s[:, 1025:2048], Vh_s[:, 1025:2048],
                                         ve_s[:, 0:1])
        dv2 = P.dve(vupd, waits=[("smpe", pb), ("smact", st["v_ready"])])
        P.bank_r(["E"], [("smdve", dv2)])
        return dv2

    def v_transposes(vupd_done):
        for g0, g1 in ((0, 8), (8, KT)):
            tv = 0
            for j in range(g0, g1):
                slot = j - g0

                def tr(eng, slot=slot, j=j):
                    return eng.transpose(psG[:, slot * 128:(slot + 1) * 128],
                                         Vh_s[:, _kstart(j):_kstart(j) + 128],
                                         ident_s[:, :])
                w = [("smdve", vupd_done)] + (P.bank_w("G") if j == g0 else [])
                tv = P.pe(tr, waits=w)

            def ev(eng, g0=g0, g1=g1):
                return eng.copy(vtm_s[:, g0 * 128:g1 * 128],
                                psG[:, 0:(g1 - g0) * 128])
            av = P.act(ev, waits=[("smpe", tv), ("smpe", st["vtm_free"])])
            P.bank_r(["G"], [("smact", av)])
        return c("smact")

    def attn_head(s, h):
        par = h % 2
        proj_head(s, h, par)
        st["qk_ready"] = c("smdve")
        st["v_ready"] = c("smact")
        nh, npar = h + 1, (h + 1) % 2
        if nh < H:
            st["wt_have"][npar] = load_wT(nh, npar)
        elif s + 1 < NSEQ:
            st["wt_have"][0] = load_wT(0, 0)
        ed3 = sT_exp(3)
        sv3 = S_sum(ed3)
        vu = prepass_vupdate(h, ed3, sv3)
        vt = v_transposes(vu)
        st["Vh_free"] = c("smpe")
        sh3 = shb_make(sv3)
        pv_evac(s, h, 3, ed3, sh3, vt)
        for qg in (0, 1, 2):
            ed = sT_exp(qg)
            sv = S_sum(ed)
            sh = shb_make(sv)
            pv_evac(s, h, qg, ed, sh, vt)
        st["Qh_free"] = c("smpe")
        st["Kh_free"] = c("smpe")
        st["vtm_free"] = c("smpe")

    def oproj_ln1(s):
        nat0, nat1 = c("smat0"), c("smat1")
        for tc in range(NT):
            sl = slice(tc * 512, (tc + 1) * 512)
            for cc in range(NC):
                P.dma("sp", "smatb", atb_s[cc][:, :], at_sp[cc][:, sl],
                      waits=[("smat0", nat0), ("smat1", nat1)] + st["atb_free"])
            natb = c("smatb")
            for g in range(8):
                gpar = g % 2
                P.dma("sp", f"smwo{gpar}", woT_s[gpar][:, :], woT_img[g],
                      waits=st["awT_gate"][2] + st["woT_free"][gpar])
                nwo = c(f"smwo{gpar}")
                for mi in range(2):
                    m = g * 2 + mi
                    bank = ("E", "F")[m % 2]
                    pj = (psE, psF)[m % 2]

                    def mm(eng, mi=mi, pj=pj, gpar=gpar):
                        ins = None
                        for cc in range(NC):
                            ins = eng.matmul(
                                pj[:, 0:512],
                                woT_s[gpar][:, cc * 256 + mi * 128:
                                            cc * 256 + (mi + 1) * 128],
                                atb_s[cc][:, :],
                                start=(cc == 0), stop=(cc == NC - 1))
                        return ins
                    mv = P.pe(mm, waits=[(f"smwo{gpar}", nwo), ("smatb", natb)]
                              + P.bank_w(bank))

                    def ev(eng, m=m, pj=pj, sl=sl):
                        return eng.tensor_add(X_s[m][:, sl], pj[:, 0:512],
                                              X_s[m][:, sl])
                    dv = P.dve(ev, waits=[("smpe", mv)], mode="instr")
                    P.bank_r([bank], [("smdve", dv)])
                st["woT_free"][gpar] = [("smpe", c("smpe"))]
            rX = P.dve(lambda e: e.engine_nop())
            def mms(eng, sl=sl):
                ins = None
                for m in range(NC):
                    ins = eng.matmul(psC[0:1, 0:512], ocb_s[:, 0:1], X_s[m][:, sl],
                                     start=(m == 0), stop=(m == NC - 1))
                return ins
            sv = P.pe(mms, waits=[("smdve", rX)] + P.bank_w("C"))

            def evs(eng):
                return eng.tensor_copy(srow_s[0:1, :], psC[0:1, 0:512])
            dvs = P.dve(evs, waits=[("smpe", sv)])
            P.bank_r(["C"], [("smdve", dvs)])
            for m in range(NC):
                def sq(eng, m=m, sl=sl):
                    return eng.activation(sq_s[m % 2][:, :], X_s[m][:, sl], AF.Square)
                av = P.act(sq, waits=[("smdve", rX), ("smpe", st["sq_free"][m % 2])])

                def mmq(eng, m=m):
                    return eng.matmul(psE[0:1, 0:512], ocb_s[:, 0:1],
                                      sq_s[m % 2][:, :], start=(m == 0),
                                      stop=(m == NC - 1))
                st["sq_free"][m % 2] = P.pe(
                    mmq, waits=[("smact", av)] + (P.bank_w("E") if m == 0 else []))
            qv = c("smpe")

            def evq(eng):
                return eng.tensor_copy(qrow_s[0:1, :], psE[0:1, 0:512])
            dvq = P.dve(evq, waits=[("smpe", qv)])
            P.bank_r(["E"], [("smdve", dvq)])

            def rows(eng):
                eng.tensor_scalar_mul(srow_s[0:1, :], srow_s[0:1, :], 1.0 / D)
                eng.tensor_scalar(qrow_s[0:1, :], qrow_s[0:1, :], 1.0 / D, 1e-5,
                                  op0=ALU.mult, op1=ALU.add)
                eng.drain()
                eng.tensor_mul(t1_s[0:1, 0:512], srow_s[0:1, :], srow_s[0:1, :])
                eng.drain()
                return eng.tensor_sub(qrow_s[0:1, :], qrow_s[0:1, :],
                                      t1_s[0:1, 0:512])
            rv = P.dve(rows, waits=[("smdve", st["t1_free"])])
            st["t1_free"] = rv

            def rsq(eng):
                return eng.activation(qrow_s[0:1, :], qrow_s[0:1, :], AF.Sqrt)
            av2 = P.act(rsq, waits=[("smdve", rv)])

            def rrec(eng):
                return eng.reciprocal(qrow_s[0:1, :], qrow_s[0:1, :])
            rrv = P.dve(rrec, waits=[("smact", av2)])

            def bcm(eng):
                return eng.matmul(psC[:, 0:512], orf_s[0:1, :], srow_s[0:1, :],
                                  start=True, stop=True)
            p1 = P.pe(bcm, waits=[("smdve", rrv)] + P.bank_w("C"))

            def bcr(eng):
                return eng.matmul(psE[:, 0:512], orf_s[0:1, :], qrow_s[0:1, :],
                                  start=True, stop=True)
            p2 = P.pe(bcr, waits=P.bank_w("E"))

            def evb(eng):
                eng.tensor_copy(mb_s[:, :], psC[:, 0:512])
                return eng.tensor_copy(rb_s[:, :], psE[:, 0:512])
            dvb = P.dve(evb, waits=[("smpe", p2)])
            P.bank_r(["C"], [("smdve", dvb)])
            P.bank_r(["E"], [("smdve", dvb)])
            def ap_(eng, sl=sl):
                ins = None
                for m in range(NC):
                    eng.tensor_sub(X_s[m][:, sl], X_s[m][:, sl], mb_s[:, :])
                    ins = eng.tensor_mul(X_s[m][:, sl], X_s[m][:, sl], rb_s[:, :])
                return ins
            P.dve(ap_, waits=[("smdve", dvb)])
            st["atb_free"] = [("smpe", c("smpe")), ("smdve", c("smdve"))]
        xv = c("smdve")
        for m in range(NC):
            P.dma("act", "smsp", x2_sp[s][m], X_s[m][:, :], waits=[("smdve", xv)])
        return c("smsp")

    fs = {"qw_free": [[], []], "qw_have": [0, 0], "q3_free": [[], []],
          "q3_have": [0, 0], "ost_free": [0, 0], "x2_free": [],
          "h_free": 0, "r2f_free": [0, 0], "r2t_free": 0, "sg_free": [0, 0]}

    def ffn_tb(tb, nq, nsp, ntw):
        for s in range(NSEQ):
            for m in range(NC):
                P.dma("pool", "smx2", x2b_s[s][m][:, :],
                      x2_sp[s][m][:, tb * 512:(tb + 1) * 512],
                      waits=[("smsp", nsp)] + fs["x2_free"])
        nx2 = c("smx2")

        def load_qw12(hhc, par):
            for wi in range(2):
                P.dma("sp", f"smqt{par}", qwT_s[par][wi][:, :], q12T_img[wi][hhc],
                      waits=ntw + fs["qw_free"][par])
            fs["qw_have"][par] = c(f"smqt{par}")
        load_qw12(0, 0)
        for hhc in range(NHH):
            par = hhc % 2
            have = fs["qw_have"][par]
            if hhc + 1 < NHH:
                load_qw12(hhc + 1, (hhc + 1) % 2)
            for s in range(NSEQ):
                pA2 = (psA, psB)[s]
                bks = ("A0", "A1") if s == 0 else ("B0", "B1")

                def mm1(eng, s=s, par=par, pA2=pA2):
                    ins = None
                    for cc in range(NC):
                        ins = eng.matmul(pA2[:, 0:512],
                                         qwT_s[par][0][:, cc * 128:(cc + 1) * 128],
                                         x2b_s[s][cc][:, :],
                                         start=(cc == 0), stop=(cc == NC - 1))
                    return ins
                P.pe(mm1, waits=[(f"smqt{par}", have), ("smx2", nx2)]
                     + P.bank_w(bks[0]))

                def mm2(eng, s=s, par=par, pA2=pA2):
                    ins = None
                    for cc in range(NC):
                        ins = eng.matmul(pA2[:, 512:1024],
                                         qwT_s[par][1][:, cc * 128:(cc + 1) * 128],
                                         x2b_s[s][cc][:, :],
                                         start=(cc == 0), stop=(cc == NC - 1))
                    return ins
                mv = P.pe(mm2, waits=P.bank_w(bks[1]))

                def sgm(eng, s=s, pA2=pA2):
                    return eng.activation(sg_s[s][:, :], pA2[:, 0:512], AF.Sigmoid)
                av = P.act(sgm, waits=[("smpe", mv), ("smdve", fs["sg_free"][s])])

                def hm(eng, s=s, hhc=hhc, pA2=pA2):
                    return eng.tensor_mul(h_s[s][hhc][:, :], sg_s[s][:, :],
                                          pA2[:, 512:1024])
                dv = P.dve(hm, waits=[("smact", av), ("smpe", fs["h_free"])])
                fs["sg_free"][s] = dv
                P.bank_r([bks[0]], [("smact", av)])
                P.bank_r([bks[1]], [("smdve", dv)])
            fs["qw_free"][par] = [("smpe", c("smpe"))]
        hv = c("smdve")

        def load_qw3(dc, par):
            P.dma("sp", f"smq3{par}", q3T_s[par][:, :], q3T_img[dc],
                  waits=ntw + fs["q3_free"][par])
            fs["q3_have"][par] = c(f"smq3{par}")
        for s in range(NSEQ):
            load_qw3(0, 0)
            for dc in range(NC):
                par = dc % 2
                have = fs["q3_have"][par]
                if dc + 1 < NC:
                    load_qw3(dc + 1, (dc + 1) % 2)
                bank = ("E", "F")[dc % 2]
                pv_ = (psE, psF)[dc % 2]

                def mm(eng, s=s, par=par, pv_=pv_):
                    ins = None
                    for hhc in range(NHH):
                        ins = eng.matmul(pv_[:, 0:512],
                                         q3T_s[par][:, hhc * 128:(hhc + 1) * 128],
                                         h_s[s][hhc][:, :],
                                         start=(hhc == 0), stop=(hhc == NHH - 1))
                    return ins
                mv = P.pe(mm, waits=[(f"smq3{par}", have), ("smdve", hv)]
                          + P.bank_w(bank))

                def ev(eng, s=s, dc=dc, pv_=pv_):
                    return eng.tensor_add(r2f_s[dc % 2][:, :], pv_[:, 0:512],
                                          x2b_s[s][dc][:, :])
                dv = P.dve(ev, waits=[("smpe", mv), ("smpe", fs["r2f_free"][dc % 2])])
                P.bank_r([bank], [("smdve", dv)])
                tv = 0
                for tc in range(4):
                    def tr(eng, dc=dc, tc=tc):
                        return eng.transpose(psG[:, tc * 128:(tc + 1) * 128],
                                             r2f_s[dc % 2][:, tc * 128:(tc + 1) * 128],
                                             ident_s[:, :])
                    w = [("smdve", dv)] + (P.bank_w("G") if tc == 0 else [])
                    tv = P.pe(tr, waits=w)
                av = 0
                for tc in range(4):
                    def tev(eng, dc=dc, tc=tc):
                        return eng.copy(r2t_s[tc][:, dc * 128:(dc + 1) * 128],
                                        psG[:, tc * 128:(tc + 1) * 128])
                    av = P.act(tev, waits=[("smpe", tv), ("smdve", fs["r2t_free"])])
                P.bank_r(["G"], [("smact", av)])
                fs["r2f_free"][dc % 2] = c("smpe")
                fs["q3_free"][par] = [("smpe", c("smpe"))]
            rT = c("smact")
            for tc in range(4):
                def st1(eng, tc=tc):
                    return eng.reduce_sum(l2v_s[:, 0:1], r2t_s[tc][:, :], axis=AX.X)
                P.dve(st1, waits=[("smact", rT)])

                def st2(eng, tc=tc):
                    ins = None
                    for piece in range(4):
                        sl2 = slice(piece * 512, (piece + 1) * 512)
                        ins = eng.activation(sg_s[0][:, :], r2t_s[tc][:, sl2],
                                             AF.Square,
                                             accum_out=l2v_s[:, 2 + piece:3 + piece])
                    return ins
                a2 = P.act(st2, waits=[("smact", rT), ("smdve", fs["sg_free"][0])])

                def vecs(eng):
                    eng.tensor_add(l2v_s[:, 1:2], l2v_s[:, 2:3], l2v_s[:, 3:4])
                    eng.tensor_add(l2v_s[:, 6:7], l2v_s[:, 4:5], l2v_s[:, 5:6])
                    eng.tensor_scalar_mul(l2v_s[:, 0:1], l2v_s[:, 0:1], 1.0 / D)
                    eng.drain()
                    eng.tensor_add(l2v_s[:, 1:2], l2v_s[:, 1:2], l2v_s[:, 6:7])
                    eng.tensor_mul(l2v_s[:, 2:3], l2v_s[:, 0:1], l2v_s[:, 0:1])
                    eng.drain()
                    eng.tensor_scalar_mul(l2v_s[:, 1:2], l2v_s[:, 1:2], 1.0 / D)
                    eng.drain()
                    eng.tensor_sub(l2v_s[:, 1:2], l2v_s[:, 1:2], l2v_s[:, 2:3])
                    eng.drain()
                    return eng.tensor_scalar_add(l2v_s[:, 1:2], l2v_s[:, 1:2], 1e-5)
                vv = P.dve(vecs, waits=[("smact", a2)])

                def sqr(eng):
                    return eng.activation(l2v_s[:, 1:2], l2v_s[:, 1:2], AF.Sqrt)
                a3 = P.act(sqr, waits=[("smdve", vv)])

                def rstd_bias(eng):
                    eng.reciprocal(l2v_s[:, 1:2], l2v_s[:, 1:2])
                    eng.drain()
                    eng.tensor_mul(l2v_s[:, 2:3], l2v_s[:, 0:1], l2v_s[:, 1:2])
                    eng.drain()
                    return eng.tensor_scalar_mul(l2v_s[:, 2:3], l2v_s[:, 2:3], -1.0)
                rb2 = P.dve(rstd_bias, waits=[("smact", a3)])

                def appl(eng, tc=tc):
                    return eng.activation(ost_s[tc % 2][:, :], r2t_s[tc][:, :],
                                          AF.Identity, bias=l2v_s[:, 2:3],
                                          scale=l2v_s[:, 1:2])
                ov = P.act(appl, waits=[("smdve", rb2),
                                        (f"smout{tc % 2}", fs["ost_free"][tc % 2])])
                fs["ost_free"][tc % 2] = P.dma(
                    "act", f"smout{tc % 2}",
                    y_out[s][tb * 512 + tc * 128: tb * 512 + (tc + 1) * 128, :],
                    ost_s[tc % 2][:, :], waits=[("smact", ov)])
            fs["r2t_free"] = c("smdve")
        fs["h_free"] = c("smpe")
        fs["x2_free"] = [("smpe", c("smpe")), ("smdve", c("smdve"))]

    def dump_rows(entries, waits):
        for ap_, r0, wdt in entries:
            P.dma("pool", "smdbg", dbg[r0:r0 + 128, 0:wdt], ap_, waits=waits)
        P.barrier("sp", [("smdbg", c("smdbg"))])

    st["awT_gate"] = sched_attn_thop()
    st["x_have"] = load_X(0)
    st["wt_have"][0] = load_wT(0, 0)

    stopped = False
    if dump in ("pq0", "pq1", "pq2"):
        level = {"pq0": 0, "pq1": 1, "pq2": 2}[dump]
        ntc = 2 if dump == "pq2" else NT
        nw = st["wt_have"][0]
        nx = st["x_have"]
        for tc in range(ntc):
            bank = ("E", "F")[tc % 2]
            pj = (psE, psF)[tc % 2]
            sl = slice(tc * 512, (tc + 1) * 512)
            def mk_mm(eng, pj=pj, sl=sl):
                ins = None
                for cc in range(NC):
                    ins = eng.matmul(pj[:, 0:512], wT_s[0][0][:, cc * 128:(cc + 1) * 128],
                                     X_s[cc][:, sl], start=(cc == 0), stop=(cc == NC - 1))
                return ins
            mmv = P.pe(mk_mm, waits=[("smwt0", nw), ("smxt", nx)] + P.bank_w(bank))
            qp = tc % 2
            def ev(eng, pj=pj, qp=qp):
                return eng.copy(qpre_s[qp][:, :], pj[:, 0:512])
            av = P.act(ev, waits=[("smpe", mmv), ("smpe", st["qpre_free"][qp])])
            P.bank_r([bank], [("smact", av)])
            if level >= 1:
                def mt1(eng, qp=qp, sl=sl):
                    return eng.tensor_mul(t1_s[:, :], qpre_s[qp][:, :], cos_s[:, sl])
                dv1 = P.dve(mt1, waits=[("smact", av), ("smdve", st["t1_free"])])
            if level >= 2:
                def mrot(eng, qp=qp):
                    return eng.matmul(psC[:, 0:512], rot_s[:, :], qpre_s[qp][:, :],
                                      start=True, stop=True)
                mmr = P.pe(mrot, waits=[("smact", av)] + P.bank_w("C"))
                st["qpre_free"][qp] = mmr
                def mt2(eng, sl=sl):
                    return eng.tensor_mul(t2_s[:, :], psC[:, 0:512], sin_s[:, sl])
                dv2 = P.dve(mt2, waits=[("smpe", mmr), ("smdve", st["t2_free"])])
                P.bank_r(["C"], [("smdve", dv2)])
                def madd(eng, sl=sl):
                    return eng.tensor_add(Qh_s[:, sl], t1_s[:, :], t2_s[:, :])
                dv3 = P.dve(madd, waits=[("smpe", st["Qh_free"])])
                st["t1_free"] = dv3
                st["t2_free"] = dv3
        dump_rows([(Qh_s[:, :], 0, T)],
                  [("smdve", c("smdve")), ("smact", c("smact")), ("smpe", c("smpe"))])
        stopped = True
    if dump in ("projq", "projqk"):
        proj_head(0, 0, 0, only=1 if dump == "projq" else 2)
        dump_rows([(Qh_s[:, :], 0, T), (Kh_s[:, :], 128, T)],
                  [("smdve", c("smdve")), ("smact", c("smact")), ("smpe", c("smpe"))])
        stopped = True
    if dump in ("p0", "p1", "p2", "p3"):
        level = int(dump[1])
        nw = st["wt_have"][0]
        nx = st["x_have"]
        def mk_mm(eng):
            ins = None
            for cc in range(NC):
                ins = eng.matmul(psE[:, 0:512], wT_s[0][0][:, cc * 128:(cc + 1) * 128],
                                 X_s[cc][:, 0:512], start=(cc == 0), stop=(cc == NC - 1))
            return ins
        mmv = P.pe(mk_mm, waits=[("smwt0", nw), ("smxt", nx)])
        def ev(eng):
            return eng.copy(qpre_s[0][:, :], psE[:, 0:512])
        P.act(ev, waits=[("smpe", mmv)])
        if level >= 1:
            def mt1(eng):
                return eng.tensor_mul(t1_s[:, :], psE[:, 0:512], cos_s[:, 0:512])
            P.dve(mt1, waits=[("smpe", mmv)])
        if level >= 2:
            def mrot(eng):
                return eng.matmul(psC[:, 0:512], rot_s[:, :], qpre_s[0][:, :],
                                  start=True, stop=True)
            mmr = P.pe(mrot, waits=[("smact", c("smact"))])
            if level >= 3:
                def mt2(eng):
                    return eng.tensor_mul(t2_s[:, :], psC[:, 0:512], sin_s[:, 0:512])
                P.dve(mt2, waits=[("smpe", mmr)])
                def madd(eng):
                    return eng.tensor_add(Qh_s[:, 0:512], t1_s[:, :], t2_s[:, :])
                P.dve(madd)
        dump_rows([(qpre_s[0][:, :], 0, 512), (Qh_s[:, 0:512], 128, 512)],
                  [("smdve", c("smdve")), ("smact", c("smact")), ("smpe", c("smpe"))])
        stopped = True
    if dump == "proj":
        proj_head(0, 0, 0)
        dump_rows([(Qh_s[:, :], 0, T), (Kh_s[:, :], 128, T),
                   (Vh_s[:, :], 256, T)],
                  [("smdve", c("smdve")), ("smact", c("smact")),
                   ("smpe", c("smpe"))])
        stopped = True
    if dump == "X":
        dump_rows([(X_s[m][:, :], m * 128, T) for m in range(NC)],
                  [("smxt", st["x_have"])])
        stopped = True

    done_sp = 0
    if not stopped:
        for s in range(NSEQ):
            for h in range(H):
                attn_head(s, h)
                for _ in range(PREP_PER_SLOT):
                    stp = next(prep_iter, None)
                    if stp is not None:
                        stp()
                if dump == "qkv" and s == 0 and h == 0:
                    dump_rows([(Qh_s[:, :], 0, T), (Kh_s[:, :], 128, T),
                               (Vh_s[:, :], 256, T)],
                              [("smdve", c("smdve")), ("smact", c("smact")),
                               ("smpe", c("smpe"))])
                    stopped = True
                    break
            if stopped:
                break
            if dump == "attn" and s == 0:
                dump_rows([], [])
                for m in range(NC):
                    P.dma("pool", "smdbg", dbg[m * 128:(m + 1) * 128, 0:T],
                          at_sp[m], waits=[("smat0", c("smat0")),
                                           ("smat1", c("smat1"))])
                P.dma("pool", "smdbg", dbg[2048:2049, 0:48], pdbg_s[0:1, :],
                      waits=[("smdve", c("smdve"))])
                P.barrier("sp", [("smdbg", c("smdbg"))])
                stopped = True
                break
            done_sp = oproj_ln1(s)
            if dump == "x2" and s == 0:
                dump_rows([(X_s[m][:, :], m * 128, T) for m in range(NC)],
                          [("smdve", c("smdve")), ("smsp", done_sp)])
                stopped = True
                break
            if s + 1 < NSEQ:
                st["x_have"] = load_X(s + 1, extra=[("smsp", done_sp)])

    if not stopped:
        for stp in prep_iter:
            stp()
        nq = c("smqs")
        ntw = [("smtw0", c("smtw0")), ("smtw1", c("smtw1"))]
        for tb in range(TBLK):
            ffn_tb(tb, nq, done_sp, ntw)
            if dump == "h" and tb == 0:
                dump_rows([(h_s[0][m][:, :], m * 128, 512) for m in range(32)],
                          [("smdve", c("smdve"))])
                stopped = True
                break
        if not stopped:
            P.barrier("sp", [("smout0", c("smout0")), ("smout1", c("smout1"))])

    body = P.emit()
    with nc.Block() as blk:
        blk.tensor(body("pe"))
        blk.scalar(body("act"))
        blk.vector(body("dve"))
        blk.gpsimd(body("pool"))
        blk.sync(body("sp"))
    return nc


def kernel(L0_attn_hidden_states, L0_attn_wq, L0_attn_wk, L0_attn_wv,
           L0_attn_wo, L0_attn_bern_u, L0_ffn_w1, L0_ffn_w2, L0_ffn_w3):
    x = np.ascontiguousarray(L0_attn_hidden_states, dtype=np.float32)
    nc = build_program()
    common = {
        "wq": np.ascontiguousarray(L0_attn_wq, dtype=np.float32),
        "wk": np.ascontiguousarray(L0_attn_wk, dtype=np.float32),
        "wv": np.ascontiguousarray(L0_attn_wv, dtype=np.float32),
        "wo": np.ascontiguousarray(L0_attn_wo, dtype=np.float32),
        "w1": np.ascontiguousarray(L0_ffn_w1, dtype=np.float32),
        "w2": np.ascontiguousarray(L0_ffn_w2, dtype=np.float32),
        "w3": np.ascontiguousarray(L0_ffn_w3, dtype=np.float32),
        "bu": np.ascontiguousarray(L0_attn_bern_u, dtype=np.float32),
    }
    in_maps = [dict(common, x=x[i * 2:(i + 1) * 2]) for i in range(8)]
    res = run_bass_kernel_spmd(nc, in_maps, core_ids=list(range(8)))
    return np.concatenate([res.results[i]["y"] for i in range(8)], axis=0)


# version 09 (final)
# speedup vs baseline: 3.6895x; 1.0850x over previous
import math

import numpy as np
import ml_dtypes

import concourse.bass as bass
import concourse.mybir as mybir
from concourse.bass_utils import run_bass_kernel_spmd

F32 = mybir.dt.float32
BF16 = mybir.dt.bfloat16
AF = mybir.ActivationFunctionType
ALU = mybir.AluOpType
AX = mybir.AxisListType

D = 2048
T = 2048
H = 16
HD = 128
NSEQ = 2
NC = D // 128
NT = T // 512
KT = 10
FH = 5632
NHH = FH // 128
TBLK = 4
MAGIC = 12582912.0
INV_SQRT_HD = 1.0 / math.sqrt(HD)
PCOLS = 1024

DETECT_RACES = False


def _kstart(j):
    return j * 128 if j < 2 else 1024 + (j - 2) * 128


def _host_consts():
    ident = np.eye(128, dtype=ml_dtypes.bfloat16)
    ones_col_bf = np.ones((128, 1), dtype=ml_dtypes.bfloat16)
    ones_row_bf = np.ones((1, 128), dtype=ml_dtypes.bfloat16)
    ones_row_f = np.ones((1, 128), dtype=np.float32)
    ones_col_f = np.ones((128, 1), dtype=np.float32)
    R = np.zeros((128, 128), dtype=np.float32)
    R[np.arange(64), np.arange(64) + 64] = -1.0
    R[np.arange(64) + 64, np.arange(64)] = 1.0
    rot_lhsT = np.ascontiguousarray(R.T).astype(ml_dtypes.bfloat16)
    a = np.arange(0, HD, 2, dtype=np.float32)
    inv_freq = (np.float32(1.0)
                / np.power(np.float32(10000.0), a / np.float32(HD))).astype(np.float32)
    pos = np.arange(T, dtype=np.float32)
    freqs = pos[None, :] * inv_freq[:, None]
    emb = np.concatenate([freqs, freqs], axis=0)
    return dict(ident=ident, ocb=ones_col_bf, orb=ones_row_bf, orf=ones_row_f,
                ocf=ones_col_f, rot=rot_lhsT,
                magic=np.full((128, 1), MAGIC, dtype=np.float32),
                nmagic=np.full((128, 1), -MAGIC, dtype=np.float32),
                cos=np.cos(emb).astype(np.float32),
                sin=np.sin(emb).astype(np.float32))


class Prog:
    def __init__(self, nc):
        self.nc = nc
        self.ops = []
        self.cnt = {}
        self.sems = {}
        self.banks = {}

    def sem(self, name):
        if name not in self.sems:
            self.sems[name] = self.nc.alloc_semaphore(name)
            self.cnt[name] = 0
        return self.sems[name]

    def c(self, name):
        self.sem(name)
        return self.cnt[name]

    def _add(self, eng, fn, waits, inc, mode):
        ws = []
        for s, v in waits:
            self.sem(s)
            if v > 0:
                ws.append((s, v))
        if inc is not None:
            self.sem(inc[0])
            self.cnt[inc[0]] += inc[1]
        self.ops.append((eng, ws, fn, inc, mode))

    def pe(self, fn, waits=(), sig=1):
        self._add("pe", fn, waits, ("smpe", sig) if sig else None, "instr")
        return self.cnt["smpe"]

    def act(self, fn, waits=(), sig=1):
        self._add("act", fn, waits, ("smact", sig) if sig else None, "instr")
        return self.cnt["smact"]

    def dve(self, fn, waits=(), sig=1, mode="drain"):
        self._add("dve", fn, waits, ("smdve", sig) if sig else None, mode)
        return self.cnt["smdve"]

    def pool(self, fn, waits=(), sig=1):
        self._add("pool", fn, waits, ("smpool", sig) if sig else None, "drain")
        return self.cnt["smpool"]

    def barrier(self, eng, waits):
        self._add(eng, lambda e: e.nop(), waits, None, "none")

    def dma(self, queue, sem_name, out_ap, in_ap, waits=(), transpose=False):
        def fn(eng, out_ap=out_ap, in_ap=in_ap, transpose=transpose):
            return eng.dma_start(out=out_ap, in_=in_ap, transpose=transpose)
        self._add(queue, fn, waits, (sem_name, 16), "instr")
        return self.cnt[sem_name]

    def bank_w(self, *names):
        ws = []
        for n in names:
            ws.extend(self.banks.get(n, []))
        return ws

    def bank_r(self, names, entries):
        for n in names:
            self.banks[n] = list(entries)

    def emit(self):
        waited = {}

        def body(eng_name):
            def run(eng):
                for op_eng, ws, fn, inc, mode in self.ops:
                    if op_eng != eng_name:
                        continue
                    for s, v in ws:
                        key = (eng_name, s)
                        if waited.get(key, 0) >= v:
                            continue
                        waited[key] = v
                        eng.wait_ge(self.sems[s], v)
                    ins = fn(eng)
                    if inc is not None:
                        s, n = inc
                        if mode == "instr":
                            assert ins is not None
                            ins.then_inc(self.sems[s], n)
                        else:
                            eng.drain()
                            eng.nop().then_inc(self.sems[s], n)
            return run
        return body


def build_program(dump=None):
    nc = bass.Bass("TRN2", target_bir_lowering=False, debug=False,
                   detect_race_conditions=DETECT_RACES)
    consts = _host_consts()

    x_in = nc.dram_tensor("x", [NSEQ, T, D], F32, kind="ExternalInput").ap()
    win = {}
    for nm, shp in (("wq", [D, D]), ("wk", [D, D]), ("wv", [D, D]), ("wo", [D, D]),
                    ("w1", [FH, D]), ("w2", [FH, D]), ("w3", [D, FH])):
        win[nm] = nc.dram_tensor(nm, shp, F32, kind="ExternalInput").ap()
    bu = nc.dram_tensor("bu", [1, H], F32, kind="ExternalInput").ap()
    y_out = nc.dram_tensor("y", [NSEQ, T, D], F32, kind="ExternalOutput").ap()
    dbg = None
    if dump is not None:
        dbg = nc.dram_tensor("dbg", [4096, 4096], F32, kind="ExternalOutput").ap()

    x_bf = nc.dram_tensor("x_bf", [NSEQ, T, D], BF16, kind="Internal").ap()
    aw_bf = {n: nc.dram_tensor(f"{n}_bf", [D, D], BF16, kind="Internal").ap()
             for n in ("wq", "wk", "wv", "wo")}
    q_bf = {0: nc.dram_tensor("q1_bf", [FH, D], BF16, kind="Internal").ap(),
            1: nc.dram_tensor("q2_bf", [FH, D], BF16, kind="Internal").ap(),
            2: nc.dram_tensor("q3_bf", [D, FH], BF16, kind="Internal").ap()}
    at_sp = nc.dram_tensor("at_sp", [NC, 128, T], BF16, kind="Internal").ap()
    awT_img = {w: nc.dram_tensor(f"awT_{w}", [H, 128, D], BF16, kind="Internal").ap()
               for w in ("wq", "wk", "wv")}
    woT_img = nc.dram_tensor("woT_img", [8, 128, 4096], BF16, kind="Internal").ap()
    q12T_img = [nc.dram_tensor(f"q{i}T_img", [NHH, 128, D], BF16,
                               kind="Internal").ap() for i in (1, 2)]
    q3T_img = nc.dram_tensor("q3T_img", [NC, 128, NHH * 128], BF16,
                             kind="Internal").ap()
    x2_sp = nc.dram_tensor("x2_sp", [NSEQ, NC, 128, T], BF16, kind="Internal").ap()

    cmap = {nm: nc.inline_tensor(arr, name=f"c_{nm}").ap()
            for nm, arr in consts.items()}

    SBUF_LIMIT = nc.sbuf_top
    cur = [nc.sbuf_base]

    def at(name, shape, dtype, base=None):
        bpe = 2 if dtype == BF16 else 4
        ptr = cur if base is None else base
        o = (ptr[0] + 31) & ~31
        t = nc.alloc_sbuf_tensor_at(name, list(shape), dtype, offset=o)
        ptr[0] = o + int(np.prod(shape[1:])) * bpe
        return t

    def multi(name, n, shape, dtype, base=None):
        return [at(f"{name}{i}", shape, dtype, base=base) for i in range(n)]

    ident_s = at("ident_s", [128, 128], BF16)
    ocb_s = at("ocb_s", [128, 1], BF16)
    orb_s = at("orb_s", [1, 128], BF16)
    orf_s = at("orf_s", [1, 128], F32)
    ocf_s = at("ocf_s", [128, 1], F32)
    rot_s = at("rot_s", [128, 128], BF16)
    bu_s = at("bu_s", [1, H], F32)
    rsc_s = multi("rsc", 3, [128, 1], F32)
    gacc_s = at("gacc_s", [1, 1024], F32)
    ve_s = at("ve_s", [128, 4], F32)
    vebf_s = at("vebf_s", [128, 1], BF16)
    pdbg_s = at("pdbg_s", [1, 48], F32)
    magic_s = at("magic_s", [128, 1], F32)
    nmagic_s = at("nmagic_s", [128, 1], F32)

    attn_base = cur[0]
    cos_s = at("cos_s", [128, T], F32)
    sin_s = at("sin_s", [128, T], F32)
    X_s = multi("X", NC, [128, T], BF16)
    Qh_s = at("Qh_s", [128, T], BF16)
    Kh_s = at("Kh_s", [128, T], BF16)
    Vh_s = at("Vh_s", [128, T], BF16)
    qpre_s = multi("qpre", 2, [128, 512], BF16)
    t1_s = at("t1_s", [128, 512], F32)
    t2_s = at("t2_s", [128, 512], F32)
    eT_s = at("eT_s", [128, KT * 512], BF16)
    vtm_s = at("vtm", [128, KT * 128], BF16)
    shb_s = at("shb_s", [128, 512], BF16)
    wT_s = [multi(f"wT{p}_", 3, [128, D], BF16) for p in range(2)]
    atv_s = multi("atv", 2, [128, 512], BF16)
    atb_s = multi("atb", NC, [128, 512], BF16)
    woT_s = [at(f"woT{p}", [128, 4096], BF16) for p in range(2)]
    sq_s = multi("sq", 2, [128, 512], BF16)
    srow_s = at("srow_s", [1, 512], F32)
    qrow_s = at("qrow_s", [1, 512], F32)
    mb_s = at("mb_s", [128, 512], F32)
    rb_s = at("rb_s", [128, 512], F32)
    attn_end = cur[0]

    pw_s = multi("pw", 2, [128, PCOLS], F32)
    pt_s = at("pt_s", [128, PCOLS], F32)
    pq_s = at("pq_s", [128, PCOLS], BF16)
    pqt_s = multi("pqt", 2, [128, 1024], BF16)
    pabs_s = at("pabs_s", [128, PCOLS], F32)
    assert cur[0] <= SBUF_LIMIT, f"attn+prep over: {cur[0]}"

    fb = [attn_base]
    x2b_s = [multi(f"x2b{s}_", NC, [128, 512], BF16, base=fb) for s in range(NSEQ)]
    h_s = [multi(f"h{s}_", NHH, [128, 512], BF16, base=fb) for s in range(NSEQ)]
    qwT_s = [[at(f"qwT{p}{w}", [128, D], BF16, base=fb) for w in range(2)]
             for p in range(2)]
    q3T_s = [at(f"q3T{p}", [128, NHH * 128], BF16, base=fb) for p in range(2)]
    r2f_s = multi("r2f", 2, [128, 512], BF16, base=fb)
    r2t_s = multi("r2t", NT, [128, D], BF16, base=fb)
    sg_s = multi("sg", 2, [128, 512], BF16, base=fb)
    ost_s = multi("ost", 2, [128, D], F32, base=fb)
    l2v_s = at("l2v", [128, 8], F32, base=fb)
    assert fb[0] <= SBUF_LIMIT, f"ffn over: {fb[0]}"

    psA = nc.alloc_psum_tensor("psA", [128, 1024], F32)
    psB = nc.alloc_psum_tensor("psB", [128, 1024], F32)
    psC = nc.alloc_psum_tensor("psC", [128, 512], F32)
    psE = nc.alloc_psum_tensor("psE", [128, 512], F32)
    psF = nc.alloc_psum_tensor("psF", [128, 512], F32)
    psG = nc.alloc_psum_tensor("psG", [128, 1024], BF16)

    P = Prog(nc)
    c = P.c
    gslot = [[] for _ in range(8)]

    for nm, dst in (("ident", ident_s), ("ocb", ocb_s), ("rot", rot_s),
                    ("cos", cos_s), ("sin", sin_s)):
        P.dma("sp", "smc", dst[:, :], cmap[nm])
    P.dma("sp", "smc", orb_s[0:1, :], cmap["orb"])
    P.dma("sp", "smc", orf_s[0:1, :], cmap["orf"])
    P.dma("sp", "smc", ocf_s[:, :], cmap["ocf"])
    P.dma("sp", "smc", magic_s[:, :], cmap["magic"])
    P.dma("sp", "smc", nmagic_s[:, :], cmap["nmagic"])
    N_CONST = c("smc")
    P.dma("pool", "smc2", bu_s[0:1, :], bu)

    for s in range(NSEQ):
        P.dma("pool", f"smxc{s}", x_bf[s], x_in[s])
    for n in ("wq", "wk", "wv", "wo"):
        P.dma("pool", "smwc", aw_bf[n], win[n])

    P.dve(lambda eng: eng.memset(eT_s[64:128, 512:1024], 0.0))

    prep_steps = []
    ps = {"pw_free": [[("smc2", 16)], [("smc2", 16)]], "pt_free": 0, "pq_free": 0,
          "pabs_free": 0}

    def gamma_chunk(widx, w_ap, row0, col0, ncols, par, first):
        def step():
            nd = P.dma("pool", f"smqf{par}", pw_s[par][:, 0:ncols],
                       w_ap[row0:row0 + 128, col0:col0 + ncols],
                       waits=ps["pw_free"][par])

            def fn2(eng, par=par, ncols=ncols, first=first):
                eng.tensor_reduce(ve_s[:, 3:4], pw_s[par][:, 0:ncols],
                                  axis=AX.X, op=ALU.add, apply_absolute_value=True)
                eng.drain()
                if first:
                    return eng.tensor_copy(pabs_s[:, 0:1], ve_s[:, 3:4])
                return eng.tensor_add(pabs_s[:, 0:1], pabs_s[:, 0:1], ve_s[:, 3:4])
            dv = P.dve(fn2, waits=[(f"smqf{par}", nd)] +
                       ([("smpe", c("smpe"))] if first else []))
            ps["pw_free"][par] = [("smdve", dv)]
        return step

    def gamma_fin(widx, count):
        def step():
            dv0 = c("smdve")

            def mm0(eng):
                return eng.matmul(psC[0:1, 0:1], pabs_s[:, 0:1], ocf_s[:, 0:1],
                                  start=True, stop=True)
            pv0 = P.pe(mm0, waits=[("smdve", dv0)] + P.bank_w("C"))

            def fin(eng, count=count):
                eng.tensor_scalar(ve_s[0:1, 0:1], psC[0:1, 0:1],
                                  1.0 / count, 1e-8, op0=ALU.mult, op1=ALU.add)
                eng.drain()
                return eng.reciprocal(ve_s[0:1, 1:2], ve_s[0:1, 0:1])
            dv = P.dve(fin, waits=[("smpe", pv0)])
            P.bank_r(["C"], [("smdve", dv)])

            def bc(eng):
                return eng.matmul(psC[:, 0:1], orf_s[0:1, :], ve_s[0:1, 1:2],
                                  start=True, stop=True)
            pv2 = P.pe(bc, waits=[("smdve", dv)] + P.bank_w("C"))

            def ev(eng, widx=widx):
                return eng.tensor_copy(rsc_s[widx][:, 0:1], psC[:, 0:1])
            dv2 = P.dve(ev, waits=[("smpe", pv2)])
            P.bank_r(["C"], [("smdve", dv2)])
        return step

    def quant_chunk(widx, w_ap, q_ap, row0, col0, ncols, par):
        def step():
            nd = P.dma("pool", f"smqf{par}", pw_s[par][:, 0:ncols],
                       w_ap[row0:row0 + 128, col0:col0 + ncols],
                       waits=ps["pw_free"][par])

            def qf(eng, par=par, ncols=ncols, widx=widx):
                return eng.tensor_scalar(pt_s[:, 0:ncols], pw_s[par][:, 0:ncols],
                                         rsc_s[widx][:, 0:1], MAGIC,
                                         op0=ALU.mult, op1=ALU.add)
            dv = P.pool(qf, waits=[(f"smqf{par}", nd), ("smdve", c("smdve")),
                                   ("smact", ps["pt_free"])])
            ps["pw_free"][par] = [("smpool", dv)]

            def sg(eng, ncols=ncols):
                return eng.activation(pq_s[:, 0:ncols], pt_s[:, 0:ncols],
                                      AF.Sign, bias=nmagic_s[:, 0:1], scale=1.0)
            av = P.act(sg, waits=[("smpool", dv), ("smqs", ps["pq_free"]),
                                  ("smc", N_CONST)])
            ps["pt_free"] = av
            ps["pq_free"] = P.dma("pool", "smqs",
                                  q_ap[row0:row0 + 128, col0:col0 + ncols],
                                  pq_s[:, 0:ncols], waits=[("smact", av)])
        return step

    def chunks_of(total, width):
        out = []
        o = 0
        while o < total:
            out.append((o, min(width, total - o)))
            o += width
        return out

    wshapes = {0: (win["w1"], FH, D), 1: (win["w2"], FH, D), 2: (win["w3"], D, FH)}
    for widx in (0, 1, 2):
        w_ap, rows, cols = wshapes[widx]
        first = True
        for i in range(rows // 128):
            for (c0, cn) in chunks_of(cols, PCOLS):
                prep_steps.append(gamma_chunk(widx, w_ap, i * 128, c0, cn,
                                              len(prep_steps) % 2, first))
                first = False
        prep_steps.append(gamma_fin(widx, rows * cols))
    for widx in (0, 1, 2):
        w_ap, rows, cols = wshapes[widx]
        for i in range(rows // 128):
            for (c0, cn) in chunks_of(cols, PCOLS):
                prep_steps.append(quant_chunk(widx, w_ap, q_bf[widx], i * 128,
                                              c0, cn, len(prep_steps) % 2))
        def mk_thop(widx=widx, rows=rows, cols=cols):
            def step():
                gate = [("smqs", c("smqs"))]
                if widx < 2:
                    for cc in range(NC):
                        for r0, nr in chunks_of(rows, 1024):
                            dst = q12T_img[widx][r0 // 128:(r0 + nr) // 128, 0:128,
                                                 cc * 128:(cc + 1) * 128].rearrange(
                                                     "h p f -> p h f")
                            thop_pair(q_bf[widx][r0:r0 + nr,
                                                 cc * 128:(cc + 1) * 128],
                                      dst, gate)
                else:
                    for hhc in range(NHH):
                        for r0, nr in chunks_of(D, 1024):
                            dst = q3T_img[r0 // 128:(r0 + nr) // 128, 0:128,
                                          hhc * 128:(hhc + 1) * 128].rearrange(
                                              "d p f -> p d f")
                            thop_pair(q_bf[2][r0:r0 + nr,
                                              hhc * 128:(hhc + 1) * 128],
                                      dst, gate)
            return step
        prep_steps.append(mk_thop())
    prep_iter = iter(prep_steps)
    PREP_PER_SLOT = max(1, (len(prep_steps) + NSEQ * H - 1) // (NSEQ * H))

    th_state = {"par": 0}

    def thop_pair(src_ap, img_dst, gate_waits):
        par = th_state["par"]
        th_state["par"] ^= 1
        nr = src_ap.shape[0]
        P.dma("sp", f"smth{par}", pqt_s[par][:, 0:nr], src_ap,
              waits=gate_waits + [(f"smtw{par}", c(f"smtw{par}"))], transpose=True)
        P.dma("pool", f"smtw{par}", img_dst, pqt_s[par][:, 0:nr],
              waits=[(f"smth{par}", c(f"smth{par}"))])

    def sched_attn_thop():
        gates = []
        for h0 in (0, 8):
            for w in ("wq", "wk", "wv"):
                for cc in range(NC):
                    dst = awT_img[w][h0:h0 + 8, 0:128,
                                     cc * 128:(cc + 1) * 128].rearrange(
                                         "h p f -> p h f")
                    thop_pair(aw_bf[w][h0 * 128:(h0 + 8) * 128,
                                       cc * 128:(cc + 1) * 128],
                              dst, [("smwc", 64)])
            gates.append([("smtw0", c("smtw0")), ("smtw1", c("smtw1"))])
        for g0 in (0, 4):
            for cc in range(NC):
                dst = woT_img[g0:g0 + 4, 0:128,
                              cc * 256:(cc + 1) * 256].rearrange("g p f -> p g f")
                thop_pair(aw_bf["wo"][g0 * 256:g0 * 256 + 1024,
                                      cc * 128:(cc + 1) * 128],
                          dst, [("smwc", 64)])
        gates.append([("smtw0", c("smtw0")), ("smtw1", c("smtw1"))])
        return gates

    st = {"wt_free": [[], []], "wt_have": [0, 0], "x_have": 0,
          "qpre_free": [0, 0], "t1_free": 0, "t2_free": 0,
          "Qh_free": 0, "Kh_free": 0, "Vh_free": 0, "eT_free": 0,
          "qk_ready": 0, "v_ready": 0,
          "shb_free": 0, "vtm_free": 0, "atv_free": [0, 0],
          "sq_free": [0, 0], "woT_free": [[], []], "atb_free": []}

    def load_X(s, extra=()):
        v = 0
        for cc in range(NC):
            v = P.dma("sp", "smxt", X_s[cc][:, :],
                      x_bf[s][0:T, cc * 128:(cc + 1) * 128],
                      waits=[(f"smxc{s}", 16), ("smc", N_CONST)] + list(extra),
                      transpose=True)
        return v

    def load_wT(h, par):
        gate = st["awT_gate"][0 if h < 8 else 1]
        for wi, name in enumerate(("wq", "wk", "wv")):
            P.dma("sp", f"smwt{par}", wT_s[par][wi][:, :], awT_img[name][h],
                  waits=gate + st["wt_free"][par])
        return c(f"smwt{par}")

    def proj_head(s, h, par, only=None):
        nw = st["wt_have"][par]
        nx = st["x_have"]
        for wi, name in enumerate(("q", "k", "v")[:only] if only else ("q", "k", "v")):
            for tc in range(NT):
                bank = ("E", "F")[tc % 2]
                pj = (psE, psF)[tc % 2]
                sl = slice(tc * 512, (tc + 1) * 512)

                def mk_mm(eng, wi=wi, pj=pj, sl=sl, par=par):
                    ins = None
                    for cc in range(NC):
                        ins = eng.matmul(pj[:, 0:512],
                                         wT_s[par][wi][:, cc * 128:(cc + 1) * 128],
                                         X_s[cc][:, sl],
                                         start=(cc == 0), stop=(cc == NC - 1))
                    return ins
                mmv = P.pe(mk_mm, waits=[(f"smwt{par}", nw), ("smxt", nx)]
                           + P.bank_w(bank))
                if name == "v":
                    def ev(eng, pj=pj, sl=sl):
                        return eng.copy(Vh_s[:, sl], pj[:, 0:512])
                    av = P.act(ev, waits=[("smpe", mmv), ("smpe", st["Vh_free"])])
                    P.bank_r([bank], [("smact", av)])
                else:
                    dst, dfree = ((Qh_s, "Qh_free") if name == "q"
                                  else (Kh_s, "Kh_free"))
                    qp = tc % 2

                    def ev(eng, pj=pj, qp=qp):
                        return eng.copy(qpre_s[qp][:, :], pj[:, 0:512])
                    av = P.act(ev, waits=[("smpe", mmv),
                                          ("smpe", st["qpre_free"][qp])])
                    P.bank_r([bank], [("smact", av)])

                    def mt1(eng, qp=qp, sl=sl):
                        return eng.tensor_mul(t1_s[:, :], qpre_s[qp][:, :],
                                              cos_s[:, sl])
                    dv1 = P.dve(mt1, waits=[("smact", av), ("smdve", st["t1_free"])])

                    def mrot(eng, qp=qp):
                        return eng.matmul(psC[:, 0:512], rot_s[:, :],
                                          qpre_s[qp][:, :], start=True, stop=True)
                    mmr = P.pe(mrot, waits=[("smact", av)] + P.bank_w("C"))
                    st["qpre_free"][qp] = mmr

                    def mt2(eng, sl=sl):
                        return eng.tensor_mul(t2_s[:, :], psC[:, 0:512], sin_s[:, sl])
                    dv2 = P.dve(mt2, waits=[("smpe", mmr), ("smdve", st["t2_free"])])
                    P.bank_r(["C"], [("smdve", dv2)])

                    def madd(eng, dst=dst, sl=sl):
                        return eng.tensor_add(dst[:, sl], t1_s[:, :], t2_s[:, :])
                    dv3 = P.dve(madd, waits=[("smpe", st[dfree])])
                    st["t1_free"] = dv3
                    st["t2_free"] = dv3
        st["wt_free"][par] = [("smpe", c("smpe"))]

    def sT_exp(qg):
        qsl = slice(qg * 512, (qg + 1) * 512)
        for p in range(KT // 2):
            pp = (psA, psB)[p % 2]
            bks = ("A0", "A1") if p % 2 == 0 else ("B0", "B1")

            def mm(eng, p=p, pp=pp, qsl=qsl):
                ins = None
                for jj in range(2):
                    j = 2 * p + jj
                    ins = eng.matmul(pp[:, jj * 512:(jj + 1) * 512],
                                     Kh_s[:, _kstart(j):_kstart(j) + 128],
                                     Qh_s[:, qsl], start=True, stop=True)
                return ins
            mmv = P.pe(mm, waits=[("smdve", st["qk_ready"])] + P.bank_w(*bks))
            if p == 0:
                def ex(eng):
                    eng.activation(eT_s[:, 0:512], psA[:, 0:512], AF.Exp,
                                   bias=0.0, scale=INV_SQRT_HD)
                    return eng.activation(eT_s[0:76, 512:1024], psA[0:76, 512:1024],
                                          AF.Exp, bias=0.0, scale=INV_SQRT_HD)
            else:
                def ex(eng, p=p, pp=pp):
                    return eng.activation(eT_s[:, p * 1024:(p + 1) * 1024], pp[:, :],
                                          AF.Exp, bias=0.0, scale=INV_SQRT_HD)
            av = P.act(ex, waits=[("smpe", mmv), ("smpe", st["eT_free"])])
            P.bank_r(bks, [("smact", av)])
        return c("smact")

    def S_sum(exp_done):
        def mm(eng):
            ins = None
            for j in range(KT):
                ins = eng.matmul(psC[0:1, 0:512], ocb_s[:, 0:1],
                                 eT_s[:, j * 512:(j + 1) * 512],
                                 start=(j == 0), stop=(j == KT - 1))
            return ins
        return P.pe(mm, waits=[("smact", exp_done)] + P.bank_w("C"))

    def shb_make(sv):
        def rc(eng):
            with nc.allow_low_precision(reason="1/S in bf16; probs are bf16 anyway"):
                return eng.reciprocal(shb_s[0:1, 0:512], psC[0:1, 0:512])
        dv = P.dve(rc, waits=[("smpe", sv), ("smdve", st["shb_free"])])
        P.bank_r(["C"], [("smdve", dv)])

        def bc(eng):
            return eng.matmul(psC[:, 0:512], orb_s[0:1, :], shb_s[0:1, 0:512],
                              start=True, stop=True)
        pv = P.pe(bc, waits=[("smdve", dv)] + P.bank_w("C"))

        def ev(eng):
            return eng.tensor_copy(shb_s[:, :], psC[:, 0:512])
        dv2 = P.dve(ev, waits=[("smpe", pv)])
        P.bank_r(["C"], [("smdve", dv2)])
        return dv2

    def pv_evac(s, h, qg, exp_done, shb_done, vtm_done):
        bank = ("E", "F")[qg % 2]
        pv_ = (psE, psF)[qg % 2]

        def mm(eng, pv_=pv_):
            ins = None
            for j in range(KT):
                ins = eng.matmul(pv_[:, 0:512], vtm_s[:, j * 128:(j + 1) * 128],
                                 eT_s[:, j * 512:(j + 1) * 512],
                                 start=(j == 0), stop=(j == KT - 1))
            return ins
        mv = P.pe(mm, waits=[("smact", exp_done), ("smdve", shb_done),
                             ("smact", vtm_done)] + P.bank_w(bank))
        ap = qg % 2

        def ev(eng, pv_=pv_, ap=ap):
            return eng.tensor_mul(atv_s[ap][:, :], pv_[:, 0:512], shb_s[:, :])
        dv = P.dve(ev, waits=[("smpe", mv), (f"smat{ap}", st["atv_free"][ap])])
        P.bank_r([bank], [("smdve", dv)])
        st["eT_free"] = mv
        st["shb_free"] = dv
        st["atv_free"][ap] = P.dma("act", f"smat{ap}",
                                   at_sp[h][:, qg * 512:(qg + 1) * 512],
                                   atv_s[ap][:, :], waits=[("smdve", dv)])
        return dv

    def prepass_vupdate(h, exp_done, sv):
        def asum(eng):
            ins = None
            for j in range(2, KT):
                ins = eng.matmul(psE[0:1, 0:1],
                                 eT_s[:, j * 512 + 511:j * 512 + 512],
                                 ocb_s[:, 0:1], start=(j == 2), stop=(j == KT - 1))
            return ins
        pa = P.pe(asum, waits=[("smact", exp_done)] + P.bank_w("E"))

        def pmath(eng, h=h):
            eng.tensor_copy(ve_s[0:1, 0:1], psE[0:1, 0:1])
            eng.tensor_copy(ve_s[0:1, 1:2], eT_s[0:1, 2 * 512 + 511:2 * 512 + 512])
            eng.tensor_scalar_mul(ve_s[0:1, 2:3], psC[0:1, 511:512], 1e-6)
            eng.drain()
            eng.tensor_sub(ve_s[0:1, 0:1], ve_s[0:1, 0:1], ve_s[0:1, 1:2])
            eng.drain()
            eng.tensor_scalar_mul(ve_s[0:1, 0:1], ve_s[0:1, 0:1], 1.0 / 1023.0)
            eng.drain()
            eng.tensor_add(ve_s[0:1, 0:1], ve_s[0:1, 0:1], ve_s[0:1, 2:3])
            eng.drain()
            eng.reciprocal(ve_s[0:1, 0:1], ve_s[0:1, 0:1])
            eng.drain()
            eng.tensor_mul(ve_s[0:1, 0:1], ve_s[0:1, 1:2], ve_s[0:1, 0:1])
            eng.drain()
            eng.tensor_copy(pdbg_s[0:1, h:h + 1], ve_s[0:1, 0:1])
            eng.tensor_tensor(ve_s[0:1, 1:2], bu_s[0:1, h:h + 1], ve_s[0:1, 0:1],
                              op=ALU.is_lt)
            eng.drain()
            eng.tensor_copy(pdbg_s[0:1, 16 + h:17 + h], ve_s[0:1, 1:2])
            return eng.tensor_scalar_mul(ve_s[0:1, 1:2], ve_s[0:1, 1:2], 1.0 / 1024.0)
        dv = P.dve(pmath, waits=[("smpe", pa), ("smpe", sv), ("smc2", 16)])
        P.bank_r(["E"], [("smdve", dv)])
        P.bank_r(["C"], [("smdve", dv)])

        def bcf(eng):
            return eng.matmul(psE[:, 0:1], orf_s[0:1, :], ve_s[0:1, 1:2],
                              start=True, stop=True)
        pb = P.pe(bcf, waits=[("smdve", dv)] + P.bank_w("E"))

        def vupd(eng):
            eng.tensor_copy(ve_s[:, 0:1], psE[:, 0:1])
            eng.tensor_copy(ve_s[:, 1:2], Vh_s[:, 1024:1025])
            eng.drain()
            eng.tensor_mul(ve_s[:, 0:1], ve_s[:, 0:1], ve_s[:, 1:2])
            eng.drain()
            return eng.tensor_scalar_add(Vh_s[:, 1025:2048], Vh_s[:, 1025:2048],
                                         ve_s[:, 0:1])
        dv2 = P.dve(vupd, waits=[("smpe", pb), ("smact", st["v_ready"])])
        P.bank_r(["E"], [("smdve", dv2)])
        return dv2

    def v_transposes(vupd_done):
        for g0, g1 in ((0, 8), (8, KT)):
            tv = 0
            for j in range(g0, g1):
                slot = j - g0

                def tr(eng, slot=slot, j=j):
                    return eng.transpose(psG[:, slot * 128:(slot + 1) * 128],
                                         Vh_s[:, _kstart(j):_kstart(j) + 128],
                                         ident_s[:, :])
                w = [("smdve", vupd_done)] + (P.bank_w("G") if j == g0 else [])
                tv = P.pe(tr, waits=w)

            def ev(eng, g0=g0, g1=g1):
                return eng.copy(vtm_s[:, g0 * 128:g1 * 128],
                                psG[:, 0:(g1 - g0) * 128])
            av = P.act(ev, waits=[("smpe", tv), ("smpe", st["vtm_free"])])
            P.bank_r(["G"], [("smact", av)])
        return c("smact")

    def attn_head(s, h):
        par = h % 2
        proj_head(s, h, par)
        st["qk_ready"] = c("smdve")
        st["v_ready"] = c("smact")
        nh, npar = h + 1, (h + 1) % 2
        if nh < H:
            st["wt_have"][npar] = load_wT(nh, npar)
        elif s + 1 < NSEQ:
            st["wt_have"][0] = load_wT(0, 0)
        ed3 = sT_exp(3)
        sv3 = S_sum(ed3)
        vu = prepass_vupdate(h, ed3, sv3)
        vt = v_transposes(vu)
        st["Vh_free"] = c("smpe")
        sh3 = shb_make(sv3)
        pv_evac(s, h, 3, ed3, sh3, vt)
        for qg in (0, 1, 2):
            ed = sT_exp(qg)
            sv = S_sum(ed)
            sh = shb_make(sv)
            pv_evac(s, h, qg, ed, sh, vt)
        st["Qh_free"] = c("smpe")
        st["Kh_free"] = c("smpe")
        st["vtm_free"] = c("smpe")

    def oproj_ln1(s):
        nat0, nat1 = c("smat0"), c("smat1")
        for tc in range(NT):
            sl = slice(tc * 512, (tc + 1) * 512)
            for cc in range(NC):
                P.dma("sp", "smatb", atb_s[cc][:, :], at_sp[cc][:, sl],
                      waits=[("smat0", nat0), ("smat1", nat1)] + st["atb_free"])
            natb = c("smatb")
            for g in range(8):
                gpar = g % 2
                P.dma("sp", f"smwo{gpar}", woT_s[gpar][:, :], woT_img[g],
                      waits=st["awT_gate"][2] + st["woT_free"][gpar])
                nwo = c(f"smwo{gpar}")
                for mi in range(2):
                    m = g * 2 + mi
                    bank = ("E", "F")[m % 2]
                    pj = (psE, psF)[m % 2]

                    def mm(eng, mi=mi, pj=pj, gpar=gpar):
                        ins = None
                        for cc in range(NC):
                            ins = eng.matmul(
                                pj[:, 0:512],
                                woT_s[gpar][:, cc * 256 + mi * 128:
                                            cc * 256 + (mi + 1) * 128],
                                atb_s[cc][:, :],
                                start=(cc == 0), stop=(cc == NC - 1))
                        return ins
                    mv = P.pe(mm, waits=[(f"smwo{gpar}", nwo), ("smatb", natb)]
                              + P.bank_w(bank))

                    def ev(eng, m=m, pj=pj, sl=sl):
                        return eng.tensor_add(X_s[m][:, sl], pj[:, 0:512],
                                              X_s[m][:, sl])
                    dv = P.dve(ev, waits=[("smpe", mv)], mode="instr")
                    P.bank_r([bank], [("smdve", dv)])
                st["woT_free"][gpar] = [("smpe", c("smpe"))]
            st["atb_free"] = [("smpe", c("smpe"))]
            rX = P.dve(lambda e: e.engine_nop())
            def mms(eng, sl=sl):
                ins = None
                for m in range(NC):
                    ins = eng.matmul(psC[0:1, 0:512], ocb_s[:, 0:1], X_s[m][:, sl],
                                     start=(m == 0), stop=(m == NC - 1))
                return ins
            sv = P.pe(mms, waits=[("smdve", rX)] + P.bank_w("C"))

            def evs(eng):
                return eng.tensor_copy(srow_s[0:1, :], psC[0:1, 0:512])
            dvs = P.dve(evs, waits=[("smpe", sv)])
            P.bank_r(["C"], [("smdve", dvs)])
            for m in range(NC):
                def sq(eng, m=m, sl=sl):
                    return eng.activation(sq_s[m % 2][:, :], X_s[m][:, sl], AF.Square)
                av = P.act(sq, waits=[("smdve", rX), ("smpe", st["sq_free"][m % 2])])

                def mmq(eng, m=m):
                    return eng.matmul(psE[0:1, 0:512], ocb_s[:, 0:1],
                                      sq_s[m % 2][:, :], start=(m == 0),
                                      stop=(m == NC - 1))
                st["sq_free"][m % 2] = P.pe(
                    mmq, waits=[("smact", av)] + (P.bank_w("E") if m == 0 else []))
            qv = c("smpe")

            def evq(eng):
                return eng.tensor_copy(qrow_s[0:1, :], psE[0:1, 0:512])
            dvq = P.dve(evq, waits=[("smpe", qv)])
            P.bank_r(["E"], [("smdve", dvq)])

            def rows(eng):
                eng.tensor_scalar_mul(srow_s[0:1, :], srow_s[0:1, :], 1.0 / D)
                eng.tensor_scalar(qrow_s[0:1, :], qrow_s[0:1, :], 1.0 / D, 1e-5,
                                  op0=ALU.mult, op1=ALU.add)
                eng.drain()
                eng.tensor_mul(t1_s[0:1, 0:512], srow_s[0:1, :], srow_s[0:1, :])
                eng.drain()
                return eng.tensor_sub(qrow_s[0:1, :], qrow_s[0:1, :],
                                      t1_s[0:1, 0:512])
            rv = P.dve(rows, waits=[("smdve", st["t1_free"])])
            st["t1_free"] = rv

            def rsq(eng):
                return eng.activation(qrow_s[0:1, :], qrow_s[0:1, :], AF.Sqrt)
            av2 = P.act(rsq, waits=[("smdve", rv)])

            def rrec(eng):
                return eng.reciprocal(qrow_s[0:1, :], qrow_s[0:1, :])
            rrv = P.dve(rrec, waits=[("smact", av2)])

            def bcm(eng):
                return eng.matmul(psC[:, 0:512], orf_s[0:1, :], srow_s[0:1, :],
                                  start=True, stop=True)
            p1 = P.pe(bcm, waits=[("smdve", rrv)] + P.bank_w("C"))

            def bcr(eng):
                return eng.matmul(psE[:, 0:512], orf_s[0:1, :], qrow_s[0:1, :],
                                  start=True, stop=True)
            p2 = P.pe(bcr, waits=P.bank_w("E"))

            def evb(eng):
                eng.tensor_copy(mb_s[:, :], psC[:, 0:512])
                return eng.tensor_copy(rb_s[:, :], psE[:, 0:512])
            dvb = P.dve(evb, waits=[("smpe", p2)])
            P.bank_r(["C"], [("smdve", dvb)])
            P.bank_r(["E"], [("smdve", dvb)])
            def ap_(eng, sl=sl):
                ins = None
                for m in range(NC):
                    eng.tensor_sub(X_s[m][:, sl], X_s[m][:, sl], mb_s[:, :])
                    ins = eng.tensor_mul(X_s[m][:, sl], X_s[m][:, sl], rb_s[:, :])
                return ins
            P.dve(ap_, waits=[("smdve", dvb)])
        xv = c("smdve")
        for m in range(NC):
            P.dma("act", "smsp", x2_sp[s][m], X_s[m][:, :], waits=[("smdve", xv)])
        return c("smsp")

    fs = {"qw_free": [[], []], "qw_have": [0, 0], "q3_free": [[], []],
          "q3_have": [0, 0], "ost_free": [0, 0], "x2_free": [],
          "h_free": 0, "r2f_free": [0, 0], "r2t_free": 0, "sg_free": [0, 0]}

    def ffn_tb(tb, nq, nsp, ntw):
        for s in range(NSEQ):
            for m in range(NC):
                P.dma("pool", "smx2", x2b_s[s][m][:, :],
                      x2_sp[s][m][:, tb * 512:(tb + 1) * 512],
                      waits=[("smsp", nsp)] + fs["x2_free"])
        nx2 = c("smx2")

        def load_qw12(hhc, par):
            for wi in range(2):
                P.dma("sp", f"smqt{par}", qwT_s[par][wi][:, :], q12T_img[wi][hhc],
                      waits=ntw + fs["qw_free"][par])
            fs["qw_have"][par] = c(f"smqt{par}")
        load_qw12(0, 0)
        for hhc in range(NHH):
            par = hhc % 2
            have = fs["qw_have"][par]
            if hhc + 1 < NHH:
                load_qw12(hhc + 1, (hhc + 1) % 2)
            for s in range(NSEQ):
                pA2 = (psA, psB)[s]
                bks = ("A0", "A1") if s == 0 else ("B0", "B1")

                def mm1(eng, s=s, par=par, pA2=pA2):
                    ins = None
                    for cc in range(NC):
                        ins = eng.matmul(pA2[:, 0:512],
                                         qwT_s[par][0][:, cc * 128:(cc + 1) * 128],
                                         x2b_s[s][cc][:, :],
                                         start=(cc == 0), stop=(cc == NC - 1))
                    return ins
                P.pe(mm1, waits=[(f"smqt{par}", have), ("smx2", nx2)]
                     + P.bank_w(bks[0]))

                def mm2(eng, s=s, par=par, pA2=pA2):
                    ins = None
                    for cc in range(NC):
                        ins = eng.matmul(pA2[:, 512:1024],
                                         qwT_s[par][1][:, cc * 128:(cc + 1) * 128],
                                         x2b_s[s][cc][:, :],
                                         start=(cc == 0), stop=(cc == NC - 1))
                    return ins
                mv = P.pe(mm2, waits=P.bank_w(bks[1]))

                def sgm(eng, s=s, pA2=pA2):
                    return eng.activation(sg_s[s][:, :], pA2[:, 0:512], AF.Sigmoid)
                av = P.act(sgm, waits=[("smpe", mv), ("smdve", fs["sg_free"][s])])

                def hm(eng, s=s, hhc=hhc, pA2=pA2):
                    return eng.tensor_mul(h_s[s][hhc][:, :], sg_s[s][:, :],
                                          pA2[:, 512:1024])
                dv = P.dve(hm, waits=[("smact", av), ("smpe", fs["h_free"])])
                fs["sg_free"][s] = dv
                P.bank_r([bks[0]], [("smact", av)])
                P.bank_r([bks[1]], [("smdve", dv)])
            fs["qw_free"][par] = [("smpe", c("smpe"))]
        hv = c("smdve")

        def load_qw3(dc, par):
            P.dma("sp", f"smq3{par}", q3T_s[par][:, :], q3T_img[dc],
                  waits=ntw + fs["q3_free"][par])
            fs["q3_have"][par] = c(f"smq3{par}")
        for s in range(NSEQ):
            load_qw3(0, 0)
            for dc in range(NC):
                par = dc % 2
                have = fs["q3_have"][par]
                if dc + 1 < NC:
                    load_qw3(dc + 1, (dc + 1) % 2)
                bank = ("E", "F")[dc % 2]
                pv_ = (psE, psF)[dc % 2]

                def mm(eng, s=s, par=par, pv_=pv_):
                    ins = None
                    for hhc in range(NHH):
                        ins = eng.matmul(pv_[:, 0:512],
                                         q3T_s[par][:, hhc * 128:(hhc + 1) * 128],
                                         h_s[s][hhc][:, :],
                                         start=(hhc == 0), stop=(hhc == NHH - 1))
                    return ins
                mv = P.pe(mm, waits=[(f"smq3{par}", have), ("smdve", hv)]
                          + P.bank_w(bank))

                def ev(eng, s=s, dc=dc, pv_=pv_):
                    return eng.tensor_add(r2f_s[dc % 2][:, :], pv_[:, 0:512],
                                          x2b_s[s][dc][:, :])
                dv = P.dve(ev, waits=[("smpe", mv), ("smpe", fs["r2f_free"][dc % 2])])
                P.bank_r([bank], [("smdve", dv)])
                tv = 0
                for tc in range(4):
                    def tr(eng, dc=dc, tc=tc):
                        return eng.transpose(psG[:, tc * 128:(tc + 1) * 128],
                                             r2f_s[dc % 2][:, tc * 128:(tc + 1) * 128],
                                             ident_s[:, :])
                    w = [("smdve", dv)] + (P.bank_w("G") if tc == 0 else [])
                    tv = P.pe(tr, waits=w)
                av = 0
                for tc in range(4):
                    def tev(eng, dc=dc, tc=tc):
                        return eng.copy(r2t_s[tc][:, dc * 128:(dc + 1) * 128],
                                        psG[:, tc * 128:(tc + 1) * 128])
                    av = P.act(tev, waits=[("smpe", tv), ("smdve", fs["r2t_free"])])
                P.bank_r(["G"], [("smact", av)])
                fs["r2f_free"][dc % 2] = c("smpe")
                fs["q3_free"][par] = [("smpe", c("smpe"))]
            rT = c("smact")
            for tc in range(4):
                def st1(eng, tc=tc):
                    return eng.reduce_sum(l2v_s[:, 0:1], r2t_s[tc][:, :], axis=AX.X)
                P.dve(st1, waits=[("smact", rT)])

                def st2(eng, tc=tc):
                    ins = None
                    for piece in range(4):
                        sl2 = slice(piece * 512, (piece + 1) * 512)
                        ins = eng.activation(sg_s[0][:, :], r2t_s[tc][:, sl2],
                                             AF.Square,
                                             accum_out=l2v_s[:, 2 + piece:3 + piece])
                    return ins
                a2 = P.act(st2, waits=[("smact", rT), ("smdve", fs["sg_free"][0])])

                def vecs(eng):
                    eng.tensor_add(l2v_s[:, 1:2], l2v_s[:, 2:3], l2v_s[:, 3:4])
                    eng.tensor_add(l2v_s[:, 6:7], l2v_s[:, 4:5], l2v_s[:, 5:6])
                    eng.tensor_scalar_mul(l2v_s[:, 0:1], l2v_s[:, 0:1], 1.0 / D)
                    eng.drain()
                    eng.tensor_add(l2v_s[:, 1:2], l2v_s[:, 1:2], l2v_s[:, 6:7])
                    eng.tensor_mul(l2v_s[:, 2:3], l2v_s[:, 0:1], l2v_s[:, 0:1])
                    eng.drain()
                    eng.tensor_scalar_mul(l2v_s[:, 1:2], l2v_s[:, 1:2], 1.0 / D)
                    eng.drain()
                    eng.tensor_sub(l2v_s[:, 1:2], l2v_s[:, 1:2], l2v_s[:, 2:3])
                    eng.drain()
                    return eng.tensor_scalar_add(l2v_s[:, 1:2], l2v_s[:, 1:2], 1e-5)
                vv = P.dve(vecs, waits=[("smact", a2)])

                def sqr(eng):
                    return eng.activation(l2v_s[:, 1:2], l2v_s[:, 1:2], AF.Sqrt)
                a3 = P.act(sqr, waits=[("smdve", vv)])

                def rstd_bias(eng):
                    eng.reciprocal(l2v_s[:, 1:2], l2v_s[:, 1:2])
                    eng.drain()
                    eng.tensor_mul(l2v_s[:, 2:3], l2v_s[:, 0:1], l2v_s[:, 1:2])
                    eng.drain()
                    return eng.tensor_scalar_mul(l2v_s[:, 2:3], l2v_s[:, 2:3], -1.0)
                rb2 = P.dve(rstd_bias, waits=[("smact", a3)])

                def appl(eng, tc=tc):
                    return eng.activation(ost_s[tc % 2][:, :], r2t_s[tc][:, :],
                                          AF.Identity, bias=l2v_s[:, 2:3],
                                          scale=l2v_s[:, 1:2])
                ov = P.act(appl, waits=[("smdve", rb2),
                                        (f"smout{tc % 2}", fs["ost_free"][tc % 2])])
                fs["ost_free"][tc % 2] = P.dma(
                    "act", f"smout{tc % 2}",
                    y_out[s][tb * 512 + tc * 128: tb * 512 + (tc + 1) * 128, :],
                    ost_s[tc % 2][:, :], waits=[("smact", ov)])
            fs["r2t_free"] = c("smdve")
        fs["h_free"] = c("smpe")
        fs["x2_free"] = [("smpe", c("smpe")), ("smdve", c("smdve"))]

    def dump_rows(entries, waits):
        for ap_, r0, wdt in entries:
            P.dma("pool", "smdbg", dbg[r0:r0 + 128, 0:wdt], ap_, waits=waits)
        P.barrier("sp", [("smdbg", c("smdbg"))])

    st["awT_gate"] = sched_attn_thop()
    st["x_have"] = load_X(0)
    st["wt_have"][0] = load_wT(0, 0)

    stopped = False
    if dump in ("pq0", "pq1", "pq2"):
        level = {"pq0": 0, "pq1": 1, "pq2": 2}[dump]
        ntc = 2 if dump == "pq2" else NT
        nw = st["wt_have"][0]
        nx = st["x_have"]
        for tc in range(ntc):
            bank = ("E", "F")[tc % 2]
            pj = (psE, psF)[tc % 2]
            sl = slice(tc * 512, (tc + 1) * 512)
            def mk_mm(eng, pj=pj, sl=sl):
                ins = None
                for cc in range(NC):
                    ins = eng.matmul(pj[:, 0:512], wT_s[0][0][:, cc * 128:(cc + 1) * 128],
                                     X_s[cc][:, sl], start=(cc == 0), stop=(cc == NC - 1))
                return ins
            mmv = P.pe(mk_mm, waits=[("smwt0", nw), ("smxt", nx)] + P.bank_w(bank))
            qp = tc % 2
            def ev(eng, pj=pj, qp=qp):
                return eng.copy(qpre_s[qp][:, :], pj[:, 0:512])
            av = P.act(ev, waits=[("smpe", mmv), ("smpe", st["qpre_free"][qp])])
            P.bank_r([bank], [("smact", av)])
            if level >= 1:
                def mt1(eng, qp=qp, sl=sl):
                    return eng.tensor_mul(t1_s[:, :], qpre_s[qp][:, :], cos_s[:, sl])
                dv1 = P.dve(mt1, waits=[("smact", av), ("smdve", st["t1_free"])])
            if level >= 2:
                def mrot(eng, qp=qp):
                    return eng.matmul(psC[:, 0:512], rot_s[:, :], qpre_s[qp][:, :],
                                      start=True, stop=True)
                mmr = P.pe(mrot, waits=[("smact", av)] + P.bank_w("C"))
                st["qpre_free"][qp] = mmr
                def mt2(eng, sl=sl):
                    return eng.tensor_mul(t2_s[:, :], psC[:, 0:512], sin_s[:, sl])
                dv2 = P.dve(mt2, waits=[("smpe", mmr), ("smdve", st["t2_free"])])
                P.bank_r(["C"], [("smdve", dv2)])
                def madd(eng, sl=sl):
                    return eng.tensor_add(Qh_s[:, sl], t1_s[:, :], t2_s[:, :])
                dv3 = P.dve(madd, waits=[("smpe", st["Qh_free"])])
                st["t1_free"] = dv3
                st["t2_free"] = dv3
        dump_rows([(Qh_s[:, :], 0, T)],
                  [("smdve", c("smdve")), ("smact", c("smact")), ("smpe", c("smpe"))])
        stopped = True
    if dump in ("projq", "projqk"):
        proj_head(0, 0, 0, only=1 if dump == "projq" else 2)
        dump_rows([(Qh_s[:, :], 0, T), (Kh_s[:, :], 128, T)],
                  [("smdve", c("smdve")), ("smact", c("smact")), ("smpe", c("smpe"))])
        stopped = True
    if dump in ("p0", "p1", "p2", "p3"):
        level = int(dump[1])
        nw = st["wt_have"][0]
        nx = st["x_have"]
        def mk_mm(eng):
            ins = None
            for cc in range(NC):
                ins = eng.matmul(psE[:, 0:512], wT_s[0][0][:, cc * 128:(cc + 1) * 128],
                                 X_s[cc][:, 0:512], start=(cc == 0), stop=(cc == NC - 1))
            return ins
        mmv = P.pe(mk_mm, waits=[("smwt0", nw), ("smxt", nx)])
        def ev(eng):
            return eng.copy(qpre_s[0][:, :], psE[:, 0:512])
        P.act(ev, waits=[("smpe", mmv)])
        if level >= 1:
            def mt1(eng):
                return eng.tensor_mul(t1_s[:, :], psE[:, 0:512], cos_s[:, 0:512])
            P.dve(mt1, waits=[("smpe", mmv)])
        if level >= 2:
            def mrot(eng):
                return eng.matmul(psC[:, 0:512], rot_s[:, :], qpre_s[0][:, :],
                                  start=True, stop=True)
            mmr = P.pe(mrot, waits=[("smact", c("smact"))])
            if level >= 3:
                def mt2(eng):
                    return eng.tensor_mul(t2_s[:, :], psC[:, 0:512], sin_s[:, 0:512])
                P.dve(mt2, waits=[("smpe", mmr)])
                def madd(eng):
                    return eng.tensor_add(Qh_s[:, 0:512], t1_s[:, :], t2_s[:, :])
                P.dve(madd)
        dump_rows([(qpre_s[0][:, :], 0, 512), (Qh_s[:, 0:512], 128, 512)],
                  [("smdve", c("smdve")), ("smact", c("smact")), ("smpe", c("smpe"))])
        stopped = True
    if dump == "proj":
        proj_head(0, 0, 0)
        dump_rows([(Qh_s[:, :], 0, T), (Kh_s[:, :], 128, T),
                   (Vh_s[:, :], 256, T)],
                  [("smdve", c("smdve")), ("smact", c("smact")),
                   ("smpe", c("smpe"))])
        stopped = True
    if dump == "X":
        dump_rows([(X_s[m][:, :], m * 128, T) for m in range(NC)],
                  [("smxt", st["x_have"])])
        stopped = True

    done_sp = 0
    if not stopped:
        for s in range(NSEQ):
            for h in range(H):
                attn_head(s, h)
                for _ in range(PREP_PER_SLOT):
                    stp = next(prep_iter, None)
                    if stp is not None:
                        stp()
                if dump == "qkv" and s == 0 and h == 0:
                    dump_rows([(Qh_s[:, :], 0, T), (Kh_s[:, :], 128, T),
                               (Vh_s[:, :], 256, T)],
                              [("smdve", c("smdve")), ("smact", c("smact")),
                               ("smpe", c("smpe"))])
                    stopped = True
                    break
            if stopped:
                break
            if dump == "attn" and s == 0:
                dump_rows([], [])
                for m in range(NC):
                    P.dma("pool", "smdbg", dbg[m * 128:(m + 1) * 128, 0:T],
                          at_sp[m], waits=[("smat0", c("smat0")),
                                           ("smat1", c("smat1"))])
                P.dma("pool", "smdbg", dbg[2048:2049, 0:48], pdbg_s[0:1, :],
                      waits=[("smdve", c("smdve"))])
                P.barrier("sp", [("smdbg", c("smdbg"))])
                stopped = True
                break
            done_sp = oproj_ln1(s)
            if dump == "x2" and s == 0:
                dump_rows([(X_s[m][:, :], m * 128, T) for m in range(NC)],
                          [("smdve", c("smdve")), ("smsp", done_sp)])
                stopped = True
                break
            if s + 1 < NSEQ:
                st["x_have"] = load_X(s + 1, extra=[("smsp", done_sp)])

    if not stopped:
        for stp in prep_iter:
            stp()
        nq = c("smqs")
        ntw = [("smtw0", c("smtw0")), ("smtw1", c("smtw1"))]
        for tb in range(TBLK):
            ffn_tb(tb, nq, done_sp, ntw)
            if dump == "h" and tb == 0:
                dump_rows([(h_s[0][m][:, :], m * 128, 512) for m in range(32)],
                          [("smdve", c("smdve"))])
                stopped = True
                break
        if not stopped:
            P.barrier("sp", [("smout0", c("smout0")), ("smout1", c("smout1"))])

    body = P.emit()
    with nc.Block() as blk:
        blk.tensor(body("pe"))
        blk.scalar(body("act"))
        blk.vector(body("dve"))
        blk.gpsimd(body("pool"))
        blk.sync(body("sp"))
    return nc


def kernel(L0_attn_hidden_states, L0_attn_wq, L0_attn_wk, L0_attn_wv,
           L0_attn_wo, L0_attn_bern_u, L0_ffn_w1, L0_ffn_w2, L0_ffn_w3):
    x = np.ascontiguousarray(L0_attn_hidden_states, dtype=np.float32)
    nc = build_program()
    common = {
        "wq": np.ascontiguousarray(L0_attn_wq, dtype=np.float32),
        "wk": np.ascontiguousarray(L0_attn_wk, dtype=np.float32),
        "wv": np.ascontiguousarray(L0_attn_wv, dtype=np.float32),
        "wo": np.ascontiguousarray(L0_attn_wo, dtype=np.float32),
        "w1": np.ascontiguousarray(L0_ffn_w1, dtype=np.float32),
        "w2": np.ascontiguousarray(L0_ffn_w2, dtype=np.float32),
        "w3": np.ascontiguousarray(L0_ffn_w3, dtype=np.float32),
        "bu": np.ascontiguousarray(L0_attn_bern_u, dtype=np.float32),
    }
    in_maps = [dict(common, x=x[i * 2:(i + 1) * 2]) for i in range(8)]
    res = run_bass_kernel_spmd(nc, in_maps, core_ids=list(range(8)))
    return np.concatenate([res.results[i]["y"] for i in range(8)], axis=0)
